# version 08
# baseline (speedup 1.0000x reference)
import math
import contextlib
import numpy as np

try:
    import concourse.bass as bass
except ImportError:
    import sys
    sys.path.insert(0, "/opt/trn_rl_repo")
    import concourse.bass as bass
import concourse.mybir as mybir
from concourse.bass_core_rust import IndirectOffsetOnAxis
from concourse.bass_utils import run_bass_kernel_spmd
import ml_dtypes

F32 = mybir.dt.float32
BF16 = mybir.dt.bfloat16
I32 = mybir.dt.int32
ACTF = mybir.ActivationFunctionType
ALU = mybir.AluOpType
AX = mybir.AxisListType

NCORES = 8
S = 8192
D = 2048
TPC = S // NCORES
NCH = D // 128
NTT = TPC // 128
HEADS = 16
HD = 128
LATENT = 512
NLC = LATENT // 128
E_R = 7
CAP = int(S / E_R)
SLOTS = 224
L1H = 5632
L1HC = L1H // 128
BLK = 256
NBLK_C = TPC // BLK
THETA = 10000.0
QSCALE = 1.0 / math.sqrt(HD)
NSEARCH = 36

_DEBUG = None
LAST_DEBUG = None

ENG = ["sy", "sc", "pe", "act", "dve", "gp"]


def _slopes(n):
    def p2(m):
        start = 2 ** (-(2 ** -(math.log2(m) - 3)))
        return [start * start ** i for i in range(m)]
    if math.log2(n).is_integer():
        return p2(n)
    c = 2 ** math.floor(math.log2(n))
    return p2(c) + _slopes(2 * c)[0::2][: n - c]


class Ctr:
    def __init__(self, nc, ctx, name):
        self.h = ctx.enter_context(nc.semaphore(name))
        self.n = 0
        self.name = name


class LazyTok:
    __slots__ = ("sem",)

    def __init__(self, sem):
        self.sem = sem


class Arena:
    def __init__(self, nc, ctx, size_bytes):
        self.nc = nc
        self.size = size_bytes
        self.big = ctx.enter_context(
            nc.sbuf_tensor("arena", [128, size_bytes], mybir.dt.uint8))
        self.base = nc.lookup_mloc(self.big).addr
        self.off = 0
        self.marks = []
        self.peak = 0
        self.cnt = 0

    def alloc(self, name, shape, dtype, at=None):
        per_part = int(np.prod(shape[1:])) * mybir.dt.size(dtype)
        per_part = (per_part + 31) & ~31
        if at is None:
            at = self.off
            self.off += per_part
        self.peak = max(self.peak, at + per_part)
        assert at + per_part <= self.size, (name, at, per_part)
        self.cnt += 1
        h = self.nc.virtual_sbuf_tensor(f"{name}_{self.cnt}", shape, dtype)
        self.nc._tensor(h.name, h.shape, h.dtype, type="SB")
        mloc = self.nc.lookup_mloc(h)
        mloc.allocated = True
        mloc.addr = self.base + at
        return h

    def mark(self):
        self.marks.append(self.off)

    def reset(self):
        self.off = self.marks.pop()


class Builder:
    def __init__(self, debug_point=None):
        self.nc = bass.Bass("TRN2", target_bir_lowering=False)
        self.dbgp = debug_point
        self.dt = {}
        self.q = {e: [] for e in ENG}
        self.bar = ()
        self.pool = {}
        self.rr = {"sy": 0, "sc": 0, "gp": 0}

    def emit(self, eng, build, waits=(), inc=None, k=1):
        eng = "act" if eng == "sc" else eng
        s = self.sem[inc] if isinstance(inc, str) else inc
        flat = []
        for w in waits:
            if w is None:
                continue
            if isinstance(w, tuple) and w and isinstance(w[0], tuple):
                flat.extend(w)
            else:
                flat.append(w)
        wl = tuple(flat) + self.bar

        def run(e, _b=build, _w=wl, _s=s, _k=k, _eng=eng, _seenref=self):
            seen = _seenref._seen[_eng]
            for (ws, wn) in _w:
                if wn > 0 and seen.get(ws.name, -1) < wn:
                    e.wait_ge(ws.h, wn)
                    seen[ws.name] = wn
            ins = _b(e)
            if _s is not None:
                ins.then_inc(_s.h, _k)
        self.q[eng].append(run)
        if s is not None:
            s.n += k
            return (s, s.n)
        return None

    def dma(self, eng, out, in_, waits=()):
        qn = eng
        s = self.pool[qn][self.rr[qn] % 6]
        self.rr[qn] += 1
        return self.emit(eng, lambda e, o=out, i=in_: e.dma_start(out=o, in_=i),
                         waits=tuple(waits) + ((s, s.n),), inc=s, k=16)

    def qtok(self, qn):
        return tuple((s, s.n) for s in self.pool[qn])

    def idma(self, build, waits=()):
        s = self.pool["gp"][self.rr["gp"] % 6]
        self.rr["gp"] += 1
        return self.emit("gp", build, waits=tuple(waits) + ((s, s.n),),
                         inc=s, k=16)

    def MM(self, psum, lhsT, rhs, start, stop, waits=(), inc=True):
        return self.emit(
            "pe", lambda e, p=psum, l=lhsT, r=rhs, s1=start, s2=stop:
            e.matmul(p, l, r, start=s1, stop=s2, skip_group_check=True),
            waits=waits, inc=("pe" if inc else None))

    def TP(self, psum, in_, ident, waits=()):
        return self.emit("pe", lambda e, p=psum, i=in_, d=ident:
                         e.transpose(p, i, d), waits=waits, inc="pe")

    def AC(self, out, in_, waits=(), func=ACTF.Copy, scale=1.0, bias=0.0):
        return self.emit("act", lambda e, o=out, i=in_, f=func, s=scale, bb=bias:
                         e.activation(o, i, f, bias=bb, scale=s),
                         waits=waits, inc="act")

    def DV(self, op, *args, waits=()):
        return self.emit("dve", lambda e, o=op, a=args: getattr(e, o)(*a),
                         waits=waits, inc="dve")

    def AG(self, name, inname, outname, waits=()):
        return self.emit("gp", lambda e, i=inname, o=outname:
                         e.collective_compute(
                             "AllGather", ALU.bypass,
                             replica_groups=[list(range(NCORES))],
                             ins=[self.dt[i].ap()], outs=[self.dt[o].ap()]),
                         waits=waits, inc=name)

    def new_phase(self):
        toks = [(self.sem[s], self.sem[s].n) for s in ("pe", "act", "dve")]
        for qn in ("sy", "sc", "gp"):
            toks += [(s, s.n) for s in self.pool[qn]]
        self.bar = tuple(toks) + getattr(self, "tok_x_all", ())

    def declare(self):
        nc, dt = self.nc, self.dt

        def din(name, shape, dtype):
            dt[name] = nc.dram_tensor(name, shape, dtype, kind="ExternalInput")

        def dint(name, shape, dtype):
            dt[name] = nc.dram_tensor(name, shape, dtype)

        din("x_t", [D, TPC], F32)
        din("wq_w", [D, D], F32)
        din("wkv_w", [D, LATENT], F32)
        din("wku_w", [LATENT, D], F32)
        din("wvu_w", [LATENT, D], F32)
        din("sw1_w", [D, D], F32)
        din("sw2_w", [1024, D], F32)
        din("rw1_w", [E_R * D, D], F32)
        din("rw2_w", [E_R * 1024, D], F32)
        din("wqkv_w", [D, 3 * D], F32)
        din("wg_w", [D, D], F32)
        din("wout_w", [D, D], F32)
        din("wgT_i", [D, L1H], I32)
        din("wuT_i", [D, L1H], I32)
        din("wdT_i", [L1H, D], I32)
        din("gatew", [128, NCH * 7], F32)
        din("ebias", [7, 1], F32)
        din("sgv", [128, L1HC], F32)
        din("suv", [128, L1HC], F32)
        din("sdv", [128, NCH], F32)
        din("nwv", [128, NCH], F32)
        din("ropec", [128, NTT * 48], F32)
        din("ropes", [128, NTT * 48], F32)
        din("qdec_d", [HEADS, BLK], F32)
        din("kdec_d", [128, HEADS * 2], F32)
        din("ddec_d", [HEADS * 2 * 128, BLK], F32)
        din("coef_d", [128, 7 * HEADS], F32)
        din("causal_d", [128, 896], BF16)
        din("identb_d", [128, 128], BF16)
        din("identf_d", [128, 128], F32)
        din("trib_d", [128, 128], BF16)
        din("onesb_d", [128, 128], BF16)
        din("onesf_d", [1, 128], F32)
        din("onescol_d", [128, 1], F32)
        din("iota224_d", [128, SLOTS], F32)
        din("tokid_d", [128, NTT], F32)
        din("sel64_d", [64, 8], F32)
        din("fcon_d", [128, 4], F32)
        din("selmap8_d", [8, 64], F32)
        din("idxqkv_d", [128, 48], I32)
        din("idxs2_d", [128, NCH], I32)

        self.out_fm = nc.dram_tensor("out_fm", [D, TPC], F32,
                                     kind="ExternalOutput")
        self.dbg = None
        if self.dbgp is not None:
            self.dbg = nc.dram_tensor("dbg", [D, TPC], F32,
                                      kind="ExternalOutput")

        for wn, sh in [("wq_b", [D, D]), ("wkv_b", [D, LATENT]),
                       ("wku_b", [LATENT, D]), ("wvu_b", [LATENT, D]),
                       ("sw1_b", [D, D]), ("sw2_b", [1024, D]),
                       ("rw1_b", [E_R * D, D]), ("rw2_b", [E_R * 1024, D]),
                       ("wqkv_b", [D, 3 * D]), ("wg_b", [D, D]),
                       ("wout_b", [D, D])]:
            dint(wn, sh, BF16)
        dint("ag1b_in", [8, TPC], F32)
        dint("ag1b_out", [8 * NCORES, TPC], F32)
        dint("ag1a_in", [48 * 128, TPC], BF16)
        dint("ag1a_out", [48 * 128 * NCORES, TPC], BF16)
        dint("ag2_in", [2 * 128, S], BF16)
        dint("ag2_out", [2 * 128 * NCORES * S, 1], BF16)
        dint("ag3_in", [HEADS * 128, 128], F32)
        dint("ag3_out", [NCORES * HEADS * 128, 128], F32)
        dint("xn_tm", [TPC, D], BF16)
        dint("routed", [TPC, D], BF16)
        dint("cown", [HEADS * NBLK_C * 128, 128], F32)
        dint("s1_d", [D, TPC], BF16)
        dint("x_d", [128, NCH * TPC], F32)

    def build(self):
        self.declare()
        nc, dt = self.nc, self.dt
        ctx = contextlib.ExitStack()
        self._ctx = ctx
        ctx.__enter__()
        ar = Arena(nc, ctx, 212800)
        self.ar = ar
        self.sem = {n: Ctr(nc, ctx, n) for n in
                    ["ld_sy", "ld_sc", "ld_gp", "pe", "act", "dve", "st",
                     "cc1a", "cc1b", "cc2", "cc3"]}
        for qn in ("sy", "sc", "gp"):
            self.pool[qn] = [Ctr(nc, ctx, f"p_{qn}{i}") for i in range(6)]
        sem = self.sem

        P = {}
        for nm, sh, dty in [
                ("x", [128, NCH * TPC], F32), ("identb", [128, 128], BF16),
                ("identf", [128, 128], F32), ("causal", [128, 896], BF16),
                ("trib", [128, 128], BF16), ("onesb", [128, 128], BF16),
                ("onesf", [1, 128], F32), ("onescol", [128, 1], F32),
                ("iota224", [128, SLOTS], F32), ("tokid", [128, NTT], F32),
                ("sel64", [64, 8], F32), ("selmap8", [8, 64], F32),
                ("fcon", [128, 4], F32),
                ("gatew", [128, NCH * 7], F32), ("ebias", [7, 1], F32),
                ("sgv", [128, L1HC], F32), ("suv", [128, L1HC], F32),
                ("sdv", [128, NCH], F32), ("nwv", [128, NCH], F32),
                ("ropec", [128, NTT * 48], F32), ("ropes", [128, NTT * 48], F32),
                ("kdec", [128, HEADS * 2], F32), ("coef", [128, 7 * HEADS], F32),
                ("idxqkv", [128, 48], I32), ("idxs2", [128, NCH], I32),
                ("xn", [128, NCH * TPC], BF16)]:
            P[nm] = ar.alloc(nm, sh, dty)
        self.P = P

        self.pf = [ctx.enter_context(nc.psum_tensor(f"pf{i}", [128, 512], F32))
                   for i in range(6)]
        self.pb = [ctx.enter_context(nc.psum_tensor(f"pb{i}", [128, 1024], BF16))
                   for i in range(2)]

        for nm, src in [("identb", "identb_d"), ("identf", "identf_d"),
                        ("causal", "causal_d"), ("trib", "trib_d"),
                        ("onesb", "onesb_d"), ("onesf", "onesf_d"),
                        ("onescol", "onescol_d"), ("iota224", "iota224_d"),
                        ("tokid", "tokid_d"), ("sel64", "sel64_d"),
                        ("selmap8", "selmap8_d"), ("fcon", "fcon_d"), ("gatew", "gatew"),
                        ("ebias", "ebias"), ("sgv", "sgv"), ("suv", "suv"),
                        ("sdv", "sdv"), ("nwv", "nwv"), ("ropec", "ropec"),
                        ("ropes", "ropes"), ("kdec", "kdec_d"),
                        ("coef", "coef_d"), ("idxqkv", "idxqkv_d"),
                        ("idxs2", "idxs2_d")]:
            self.dma("sy", P[nm][:], dt[src][:, :])
        for c in range(0, NCH, 4):
            self.dma("sy", P["x"][:, c * TPC:(c + 4) * TPC].rearrange(
                         "p (c t) -> p c t", c=4),
                     dt["x_t"][c * 128:(c + 4) * 128, :].rearrange(
                         "(c p) t -> p c t", p=128))
        self.tok_x = None
        self.tok_x_all = self.qtok("sy")
        self.bar = self.bar + self.tok_x_all
        self.wtok = {}
        for wn, rows in [("wkv", 2048), ("wq", 2048), ("wku", 512),
                         ("wvu", 512), ("sw1", 2048), ("rw1", E_R * 2048),
                         ("rw2", E_R * 1024), ("sw2", 1024),
                         ("wqkv", 2048), ("wg", 2048), ("wout", 2048)]:
            for r0 in range(0, rows, 2048):
                r1 = min(rows, r0 + 2048)
                self.dma("gp", dt[wn + "_b"][r0:r1, :], dt[wn + "_w"][r0:r1, :])
            self.wtok[wn] = self.qtok("gp")

        for ph in [self.phase_ln0, self.phase_routing, self.phase_qkv,
                   self.phase_moe, self.phase_attn, self.phase_combine0,
                   self.phase_l1, self.phase_out]:
            self.new_phase()
            ph()

        self._seen = {e: {} for e in ENG}
        block = ctx.enter_context(nc.Block())
        qs = self.q

        @block.sync
        def _(e):
            for f in qs["sy"]:
                f(e)

        @block.scalar
        def _(e):
            assert not qs["sc"]
            for f in qs["act"]:
                f(e)

        @block.tensor
        def _(e):
            for f in qs["pe"]:
                f(e)

        @block.vector
        def _(e):
            for f in qs["dve"]:
                f(e)

        @block.gpsimd
        def _(e):
            for f in qs["gp"]:
                f(e)

        ctx.__exit__(None, None, None)
        assert ar.peak <= 212800
        return nc

    def dbg_dump(self, point, sb, dtype=BF16, waits=()):
        if self.dbgp != point:
            return waits[0] if waits else None
        last = None
        for c in range(NCH):
            last = self.dma("gp", self.dbg[c * 128:(c + 1) * 128, :],
                            sb[:, c * TPC:(c + 1) * TPC], waits=waits)
        return last

    def ln(self, src, xn_out, mu_tm, inv_tm, eps_idx, tok_src, name):
        b, ar, pf = self, self.ar, self.pf
        ar.mark()
        sq = ar.alloc(f"{name}sq", [128, 512], F32)
        xtmp = ar.alloc(f"{name}xtmp", [128, 512], F32)
        rows = ar.alloc(f"{name}rows", [1, 5 * TPC], F32)
        SUM, SSQ, MU, INV, NEGM = 0, TPC, 2 * TPC, 3 * TPC, 4 * TPC
        tok_xn = None
        tok_tp = None
        prev_bc = None
        for th in range(2):
            o = th * 512
            tsq = None
            tmm = None
            for c in range(NCH):
                xa = src[:, c * TPC + o: c * TPC + o + 512]
                tsq = b.AC(sq[:], xa, waits=[tok_src, tsq, tmm], func=ACTF.Square)
                b.MM(pf[0][0:1, :], self.P["onescol"][:], xa,
                     c == 0, c == NCH - 1, waits=[tok_src, prev_bc])
                tmm = b.MM(pf[1][0:1, :], self.P["onescol"][:], sq[:],
                           c == 0, c == NCH - 1, waits=[tsq, prev_bc])
            t = b.DV("tensor_scalar_mul", rows[:, MU + o:MU + o + 512],
                     pf[0][0:1, :], 1.0 / D, waits=[tmm])
            t = b.DV("tensor_scalar_mul", rows[:, SSQ + o:SSQ + o + 512],
                     pf[1][0:1, :], 1.0 / D, waits=[t])
            t = b.DV("tensor_mul", rows[:, SUM + o:SUM + o + 512],
                     rows[:, MU + o:MU + o + 512],
                     rows[:, MU + o:MU + o + 512], waits=[t])
            t = b.DV("tensor_sub", rows[:, SSQ + o:SSQ + o + 512],
                     rows[:, SSQ + o:SSQ + o + 512],
                     rows[:, SUM + o:SUM + o + 512], waits=[t])
            t = b.AC(rows[:, SSQ + o:SSQ + o + 512],
                     rows[:, SSQ + o:SSQ + o + 512],
                     waits=[t], func=ACTF.Sqrt,
                     bias=self.P["fcon"][0:1, eps_idx:eps_idx + 1])
            t = b.DV("reciprocal", rows[:, INV + o:INV + o + 512],
                     rows[:, SSQ + o:SSQ + o + 512], waits=[t])
            t = b.DV("scalar_tensor_tensor", rows[:, NEGM + o:NEGM + o + 512],
                     rows[:, MU + o:MU + o + 512], -1.0,
                     rows[:, INV + o:INV + o + 512], ALU.mult, ALU.mult,
                     waits=[t])
            tb1 = b.MM(pf[2][:, :], self.P["onesf"][:],
                       rows[:, INV + o:INV + o + 512], True, True, waits=[t])
            tb2 = b.MM(pf[3][:, :], self.P["onesf"][:],
                       rows[:, NEGM + o:NEGM + o + 512], True, True, waits=[t])
            last_dve = prev_bc
            for c in range(NCH):
                xa = src[:, c * TPC + o: c * TPC + o + 512]
                t1 = b.DV("tensor_mul", xtmp[:], xa, pf[2][:, :],
                          waits=[tb1, tb2, last_dve])
                last_dve = b.DV("tensor_add",
                                xn_out[:, c * TPC + o: c * TPC + o + 512],
                                xtmp[:], pf[3][:, :], waits=[t1])
            prev_bc = last_dve
            tok_xn = last_dve
            for g in range(4):
                tt = th * 4 + g
                tp = b.TP(pf[4][:, 0:1],
                          rows[:, MU + o + g * 128: MU + o + g * 128 + 128],
                          self.P["identf"][0:1, 0:1], waits=[t, tok_tp])
                tok_tp = b.AC(mu_tm[:, tt:tt + 1], pf[4][:, 0:1], waits=[tp])
                tp = b.TP(pf[4][:, 1:2],
                          rows[:, INV + o + g * 128: INV + o + g * 128 + 128],
                          self.P["identf"][0:1, 0:1], waits=[t, tok_tp])
                tok_tp = b.AC(inv_tm[:, tt:tt + 1], pf[4][:, 1:2], waits=[tp])
        ar.reset()
        return tok_xn, tok_tp

    def phase_ln0(self):
        ar = self.ar
        self.mu0 = ar.alloc("mu0", [128, NTT], F32)
        self.inv0 = ar.alloc("inv0", [128, NTT], F32)
        self.tok_xn0, self.tok_mi0 = self.ln(
            self.P["x"], self.P["xn"], self.mu0, self.inv0, 0,
            self.tok_x, "ln0")
        self.dbg_dump("xn0", self.P["xn"], waits=[self.tok_xn0])

    def phase_routing(self):
        b, ar, pf, P = self, self.ar, self.pf, self.P
        self.aff = ar.alloc("aff", [128, NTT * 7], F32)
        self.mask2 = ar.alloc("mask2", [128, NTT * 7], F32)
        srow = ar.alloc("srow", [1, 16], F32)
        csum_bc = ar.alloc("csum_bc", [128, 7], F32)
        bias_bc = ar.alloc("bias_bc", [128, 7], F32)
        sc7 = ar.alloc("sc7", [8, TPC], F32)
        g1 = ar.alloc("g1t", [128, 7], F32)
        g2 = ar.alloc("g2t", [128, 7], F32)

        t = None
        for c in range(NCH):
            t = b.MM(pf[4][0:1, 0:7], P["onescol"][:],
                     P["gatew"][:, c * 7:(c + 1) * 7], c == 0, c == NCH - 1,
                     waits=[self.tok_x])
        t = b.DV("tensor_scalar_mul", srow[:, 0:7], pf[4][0:1, 0:7], -1.0,
                 waits=[t])
        t1 = b.MM(pf[4][:, 0:7], P["onesf"][:], srow[:, 0:7], True, True,
                  waits=[t])
        tcs = b.AC(csum_bc[:], pf[4][:, 0:7], waits=[t1])
        t = b.TP(pf[4][0:1, 8:15], P["ebias"][:], P["identf"][0:7, 0:7],
                 waits=[self.tok_x, tcs])
        t = b.AC(srow[:, 8:15], pf[4][0:1, 8:15], waits=[t])
        t = b.MM(pf[4][:, 8:15], P["onesf"][:], srow[:, 8:15], True, True,
                 waits=[t])
        tbs = b.AC(bias_bc[:], pf[4][:, 8:15], waits=[t])

        tmem = b.DV("memset", sc7[:, :], -1.0)
        tok_sc = None
        for g in range(NTT):
            tmm = None
            for c in range(NCH):
                tmm = b.MM(pf[5][:, 0:7],
                           P["x"][:, c * TPC + g * 128: c * TPC + g * 128 + 128],
                           P["gatew"][:, c * 7:(c + 1) * 7],
                           c == 0, c == NCH - 1,
                           waits=[self.tok_x, tcs, tbs, tok_sc])
            t = b.DV("scalar_tensor_tensor", g1[:], csum_bc[:],
                     self.mu0[:, g:g + 1], pf[5][:, 0:7], ALU.mult, ALU.add,
                     waits=[tmm, self.tok_mi0])
            t = b.DV("scalar_tensor_tensor", g2[:], g1[:],
                     self.inv0[:, g:g + 1], bias_bc[:], ALU.mult, ALU.add,
                     waits=[t])
            t = b.AC(self.aff[:, g * 7:(g + 1) * 7], g2[:], waits=[t],
                     func=ACTF.Sigmoid)
            t = b.DV("reduce_max", g1[:, 0:1], self.aff[:, g * 7:(g + 1) * 7],
                     AX.X, waits=[t])
            t = b.DV("tensor_single_scalar", g2[:],
                     self.aff[:, g * 7:(g + 1) * 7], g1[:, 0:1], ALU.is_lt,
                     waits=[t])
            t = b.DV("tensor_scalar_add", g1[:],
                     self.aff[:, g * 7:(g + 1) * 7], 1.0, waits=[t])
            t = b.DV("tensor_mul", g1[:], g1[:], g2[:], waits=[t])
            t = b.DV("tensor_scalar_sub", g1[:], g1[:], 1.0, waits=[t])
            t = b.DV("reduce_max", g2[:, 0:1], g1[:], AX.X, waits=[t])
            t = b.DV("tensor_single_scalar", self.mask2[:, g * 7:(g + 1) * 7],
                     self.aff[:, g * 7:(g + 1) * 7], g2[:, 0:1], ALU.is_ge,
                     waits=[t])
            t = b.DV("tensor_scalar_add", g1[:],
                     self.aff[:, g * 7:(g + 1) * 7], 1.0, waits=[t])
            t = b.DV("tensor_mul", g1[:], g1[:],
                     self.mask2[:, g * 7:(g + 1) * 7], waits=[t])
            t = b.DV("tensor_scalar_sub", g1[:], g1[:], 1.0, waits=[t])
            t = b.TP(pf[5][0:7, 128:256], g1[:], P["identf"][:],
                     waits=[t, tok_sc])
            tok_sc = b.AC(sc7[0:7, g * 128:(g + 1) * 128], pf[5][0:7, 128:256],
                          waits=[t, tmem])
        tw = b.dma("sy", self.dt["ag1b_in"][:, :], sc7[:],
                   waits=[tok_sc, tmem])
        self.tok_cc1b = b.AG("cc1b", "ag1b_in", "ag1b_out", waits=[tw])

    def phase_qkv(self):
        b, ar, pf, pb, P, dt = self, self.ar, self.pf, self.pb, self.P, self.dt
        ar.mark()
        wres = ar.alloc("wres", [128, NCH * 1024], BF16)
        wldf = [ar.alloc(f"wldf{i}", [128, 1024], F32) for i in range(2)]
        ckvT = ar.alloc("ckvT", [128, NLC * TPC], BF16)
        stage_f = ar.alloc("stage_f", [128, 1024], F32)
        stage_tm = ar.alloc("stage_tm", [128, 1024], BF16)
        hstage = ar.alloc("hstage", [128, NCH * TPC], BF16)
        rt1 = ar.alloc("rt1", [128, 384], F32)
        rt2 = ar.alloc("rt2", [128, 384], F32)

        def load_res(dram, rows, c0, cw, prev_users_done):
            wt = self.wtok[dram[:-2]]
            for i in range(rows // 128):
                b.dma("sy", wres[:, i * 1024:i * 1024 + cw],
                      dt[dram][i * 128:(i + 1) * 128, c0:c0 + cw],
                      waits=[prev_users_done, wt])
            return self.qtok("sy")

        twkv = load_res("wkv_b", 2048, 0, 512, None)
        tok_ck = None
        ck_evs = []
        for lc in range(NLC):
            for th in range(2):
                pi = lc * 2 + th
                tmm = None
                for c in range(NCH):
                    tmm = b.MM(pf[pi % 6][:, :],
                               wres[:, c * 1024 + lc * 128: c * 1024 + lc * 128 + 128],
                               P["xn"][:, c * TPC + th * 512: c * TPC + th * 512 + 512],
                               c == 0, c == NCH - 1,
                               waits=[twkv, self.tok_xn0,
                                      ck_evs[pi - 6] if pi >= 6 and c == 0 else None])
                tok_ck = b.AC(ckvT[:, lc * TPC + th * 512: lc * TPC + th * 512 + 512],
                              pf[pi % 6][:, :], waits=[tmm])
                ck_evs.append(tok_ck)
        tok_ckv = tok_ck

        def qk_tm(wtok, nk, src, sec, scale, half, stage_free):
            if stage_free is None or isinstance(stage_free, tuple) and \
                    stage_free and isinstance(stage_free[0], tuple):
                extra = list(stage_free) if stage_free else []
                tok_done = None
            else:
                extra = [stage_free]
                tok_done = None
            for g in range(NTT):
                tmm = None
                for kk in range(nk):
                    for cg in range(2):
                        tmm = b.MM(pf[cg][:, :],
                                   src[:, kk * TPC + g * 128: kk * TPC + g * 128 + 128],
                                   wres[:, kk * 1024 + cg * 512:
                                        kk * 1024 + (cg + 1) * 512],
                                   kk == 0, kk == nk - 1,
                                   waits=[wtok, self.tok_xn0, tok_done] + extra)
                tcp = None
                for cg in range(2):
                    tcp = b.AC(stage_f[:, cg * 512:(cg + 1) * 512], pf[cg][:, :],
                               waits=[tmm, tcp, tok_done] + extra,
                               scale=(self.P["fcon"][:, 2:3] if scale != 1.0
                                      else 1.0))
                cb = P["ropec"][:, g * 48:(g + 1) * 48][:, None, :]\
                    .broadcast_to([128, 8, 48])
                sb_ = P["ropes"][:, g * 48:(g + 1) * 48][:, None, :]\
                    .broadcast_to([128, 8, 48])
                sfv = stage_f[:].rearrange("p (h c) -> p h c", h=8)
                stv = stage_tm[:].rearrange("p (h c) -> p h c", h=8)
                r1v = rt1[:].rearrange("p (h c) -> p h c", h=8)
                r2v = rt2[:].rearrange("p (h c) -> p h c", h=8)
                t = b.DV("tensor_mul", r1v, sfv[:, :, 32:128:2], cb,
                         waits=[tcp, tok_done] + extra)
                t = b.DV("tensor_mul", r2v, sfv[:, :, 33:128:2], sb_, waits=[t])
                t = b.DV("tensor_sub", stv[:, :, 32:128:2], r1v, r2v, waits=[t])
                t = b.DV("tensor_mul", r1v, sfv[:, :, 33:128:2], cb, waits=[t])
                t = b.DV("tensor_mul", r2v, sfv[:, :, 32:128:2], sb_, waits=[t])
                t = b.DV("tensor_add", stv[:, :, 33:128:2], r1v, r2v, waits=[t])
                t = b.DV("tensor_copy", stv[:, :, 0:32], sfv[:, :, 0:32],
                         waits=[t])
                tlast = None
                for c in range(8):
                    hc = half * 8 + c
                    tp = b.TP(pb[0][:, c * 128:(c + 1) * 128],
                              stage_tm[:, c * 128:(c + 1) * 128], P["identb"][:],
                              waits=[t, tlast])
                    tlast = b.AC(hstage[:, hc * TPC + g * 128:
                                        hc * TPC + g * 128 + 128],
                                 pb[0][:, c * 128:(c + 1) * 128], waits=[tp])
                tok_done = tlast
            return tok_done

        tw = load_res("wq_b", 2048, 0, 1024, tok_ckv)
        th0 = qk_tm(tw, NCH, P["xn"], 0, QSCALE, 0, None)
        tw = load_res("wq_b", 2048, 1024, 1024, th0)
        th1 = qk_tm(tw, NCH, P["xn"], 0, QSCALE, 1, th0)
        for c in range(NCH):
            b.dma("sy", dt["ag1a_in"][c * 128:(c + 1) * 128, :],
                  hstage[:, c * TPC:(c + 1) * TPC], waits=[th1])
        tq_all = self.qtok("sy")
        tw = load_res("wku_b", 512, 0, 1024, th1)
        th0 = qk_tm(tw, NLC, ckvT, 1, 1.0, 0, tq_all)
        tw = load_res("wku_b", 512, 1024, 1024, th0)
        th1 = qk_tm(tw, NLC, ckvT, 1, 1.0, 1, th0)
        for c in range(NCH):
            b.dma("sy", dt["ag1a_in"][(16 + c) * 128:(17 + c) * 128, :],
                  hstage[:, c * TPC:(c + 1) * TPC], waits=[th1])
        tk_all = self.qtok("sy")
        tvs = None
        v_evs = []
        for half in range(2):
            tw = load_res("wvu_b", 512, half * 1024, 1024,
                          tvs)
            for cc in range(8):
                c = half * 8 + cc
                for th in range(2):
                    gi = cc * 2 + th
                    pi = gi % 6
                    tmm = None
                    for lc in range(NLC):
                        tmm = b.MM(pf[pi][:, :],
                                   wres[:, lc * 1024 + cc * 128:
                                        lc * 1024 + cc * 128 + 128],
                                   ckvT[:, lc * TPC + th * 512:
                                        lc * TPC + th * 512 + 512],
                                   lc == 0, lc == NLC - 1,
                                   waits=[tw, tok_ckv,
                                          v_evs[-6] if len(v_evs) >= 6 and lc == 0 else None]
                                   + list(tk_all))
                    tvs = b.AC(hstage[:, c * TPC + th * 512:
                                      c * TPC + th * 512 + 512],
                               pf[pi][:, :], waits=[tmm] + list(tk_all))
                    v_evs.append(tvs)
        for c in range(NCH):
            b.dma("sy", dt["ag1a_in"][(32 + c) * 128:(33 + c) * 128, :],
                  hstage[:, c * TPC:(c + 1) * TPC], waits=[tvs])
        self.tok_cc1a = b.AG("cc1a", "ag1a_in", "ag1a_out",
                             waits=list(self.qtok("sy")))

        txm = None
        for g in range(NTT):
            tps = None
            for c in range(NCH):
                tp = b.TP(pb[1][:, (c % 8) * 128:(c % 8) * 128 + 128],
                          P["xn"][:, c * TPC + g * 128: c * TPC + g * 128 + 128],
                          P["identb"][:], waits=[self.tok_xn0, txm, tps])
                tps = b.AC(stage_tm[:, (c % 8) * 128:(c % 8) * 128 + 128],
                           pb[1][:, (c % 8) * 128:(c % 8) * 128 + 128],
                           waits=[tp, tvs, txm]) if c % 8 == 7 or True else tps
                if c % 8 == 7:
                    txm = b.dma("sy",
                                dt["xn_tm"][g * 128:(g + 1) * 128,
                                            (c - 7) * 128:(c + 1) * 128],
                                stage_tm[:, 0:1024], waits=[tps])
        self.tok_xntm = self.qtok("sy")
        ar.reset()

    def phase_moe(self):
        b, ar, pf, pb, P, dt = self, self.ar, self.pf, self.pb, self.P, self.dt
        self.gprime = ar.alloc("gprime", [128, NCH * TPC], BF16)
        ar.mark()
        score64 = ar.alloc("score64", [64, TPC], F32)
        ge64 = ar.alloc("ge64", [64, TPC], BF16)
        thc = ar.alloc("thc", [64, 1], F32)
        lo8 = ar.alloc("lo8", [8, 1], F32)
        hi8 = ar.alloc("hi8", [8, 1], F32)
        mid8 = ar.alloc("mid8", [8, 1], F32)
        m8 = ar.alloc("m8", [8, 1], F32)
        d8 = ar.alloc("d8", [8, 1], F32)
        cnt64 = ar.alloc("cnt64", [64, 1], F32)
        trow = ar.alloc("trow", [1, 8], F32)
        thbc = ar.alloc("thbc", [128, 8], F32)
        kept = ar.alloc("kept", [128, NTT * 7], F32)
        keptb = ar.alloc("keptb", [128, NTT * 7], BF16)
        affw = ar.alloc("affw", [128, NTT * 7], F32)
        rank = ar.alloc("rank", [128, NTT * 7], F32)
        st_e = ar.alloc("st_e", [128, SLOTS], F32)
        idxrow = ar.alloc("idxrow", [1, SLOTS], F32)
        wrow = ar.alloc("wrow", [1, SLOTS], F32)
        idxcol = ar.alloc("idxcol", [128, 2], I32)
        wcol = ar.alloc("wcol", [128, 2], F32)
        g7 = ar.alloc("g7m", [128, 7], F32)
        wl1 = [ar.alloc(f"mwl{i}", [128, 1024], F32) for i in range(2)]
        wlb = [ar.alloc(f"mwb{i}", [128, 1024], BF16) for i in range(2)]
        xg_off = ar.off
        xg = ar.alloc("xg", [128, 2048], BF16)
        xgT = ar.alloc("xgT", [128, NCH * SLOTS], BF16)
        h1 = ar.alloc("h1m", [128, 2048], BF16)
        gu = ar.alloc("gum", [128, 1024], BF16)
        guT = ar.alloc("guT", [128, 8 * SLOTS], BF16)
        oe = ar.alloc("oem", [128, 2048], BF16)
        sguT = ar.alloc("sguT", [128, 8 * TPC], BF16)
        silt = ar.alloc("silt", [128, 512], BF16)
        cstg = ar.alloc("cstg", [128, 512], F32)
        rstg = [ar.alloc("rstg0", [128, NTT * 128], BF16, at=xg_off)] * 2

        tok_sgu = None
        wfree = [None, None]
        for c in range(8):
            tmm = None
            for d in range(NCH):
                buf = d % 2
                tl1 = b.dma("sy", wlb[buf][:, 0:128],
                            dt["sw1_b"][d * 128:(d + 1) * 128,
                                        c * 128:(c + 1) * 128],
                            waits=[wfree[buf], self.wtok["sw1"]])
                tl2 = b.dma("sy", wlb[buf][:, 128:256],
                            dt["sw1_b"][d * 128:(d + 1) * 128,
                                        (c + 8) * 128:(c + 9) * 128],
                            waits=[wfree[buf]])
                for pi in range(4):
                    cc, th = divmod(pi, 2)
                    tmm = b.MM(pf[pi][:, :],
                               wlb[buf][:, cc * 128:(cc + 1) * 128],
                               P["xn"][:, d * TPC + th * 512:
                                       d * TPC + th * 512 + 512],
                               d == 0, d == NCH - 1,
                               waits=[tl1, tl2, self.tok_xn0,
                                      tok_sgu if d == 0 else None])
                wfree[buf] = tmm
            for th in range(2):
                t1 = b.AC(silt[:], pf[2 + th][:, :], waits=[tmm, tok_sgu],
                          func=ACTF.Silu)
                tok_sgu = b.DV("tensor_mul",
                               sguT[:, c * TPC + th * 512: c * TPC + th * 512 + 512],
                               pf[th][:, :], silt[:], waits=[t1])

        self.new_phase()
        tso = b.dma("sy", score64[:], dt["ag1b_out"][:, :],
                    waits=[self.tok_cc1b])
        t = b.DV("memset", lo8[:], 0.0)
        t = b.DV("memset", hi8[:], 1.0, waits=[t])
        t = b.DV("tensor_add", mid8[:], lo8[:], hi8[:], waits=[t])
        t = b.DV("tensor_scalar_mul", mid8[:], mid8[:], 0.5, waits=[t])
        tpe_prev = None
        for it in range(NSEARCH):
            tp1 = b.MM(pf[4][0:64, 0:1], P["selmap8"][:], mid8[:], True, True,
                       waits=[t, self.tok_x, tpe_prev])
            t = b.AC(thc[:], pf[4][0:64, 0:1], waits=[tp1])
            t = b.DV("tensor_single_scalar", ge64[:], score64[:], thc[:, 0:1],
                     ALU.is_ge, waits=[t, tso])
            t = b.DV("reduce_sum", cnt64[:], ge64[:], AX.X, waits=[t])
            tp2 = b.MM(pf[4][0:8, 2:3], P["sel64"][:], cnt64[:], True, True,
                       waits=[t])
            tpe_prev = tp2
            t = b.AC(d8[:], pf[4][0:8, 2:3], waits=[tp2])
            t = b.DV("tensor_single_scalar", m8[:], d8[:], float(CAP),
                     ALU.is_ge, waits=[t])
            t = b.DV("tensor_sub", d8[:], mid8[:], lo8[:], waits=[t])
            t = b.DV("tensor_mul", d8[:], d8[:], m8[:], waits=[t])
            t = b.DV("tensor_add", lo8[:], lo8[:], d8[:], waits=[t])
            t = b.DV("tensor_sub", d8[:], hi8[:], mid8[:], waits=[t])
            t = b.DV("tensor_mul", d8[:], d8[:], m8[:], waits=[t])
            t = b.DV("tensor_add", hi8[:], mid8[:], d8[:], waits=[t])
            t = b.DV("tensor_add", mid8[:], lo8[:], hi8[:], waits=[t])
            t = b.DV("tensor_scalar_mul", mid8[:], mid8[:], 0.5, waits=[t])
        tp = b.TP(pf[4][0:1, 4:12], lo8[:], P["identf"][0:8, 0:8],
                  waits=[t, tpe_prev])
        t = b.AC(trow[:], pf[4][0:1, 4:12], waits=[tp])
        tp = b.MM(pf[4][:, 4:12], P["onesf"][:], trow[:], True, True, waits=[t])
        tthbc = b.AC(thbc[:], pf[4][:, 4:12], waits=[tp])

        tk = None
        for g in range(NTT):
            t = b.DV("tensor_tensor", g7[:], self.aff[:, g * 7:(g + 1) * 7],
                     thbc[:, 0:7], ALU.is_ge, waits=[tthbc, tk])
            t = b.DV("tensor_mul", kept[:, g * 7:(g + 1) * 7], g7[:],
                     self.mask2[:, g * 7:(g + 1) * 7], waits=[t])
            t = b.DV("tensor_mul", affw[:, g * 7:(g + 1) * 7],
                     kept[:, g * 7:(g + 1) * 7],
                     self.aff[:, g * 7:(g + 1) * 7], waits=[t])
            tk = b.DV("tensor_copy", keptb[:, g * 7:(g + 1) * 7],
                      kept[:, g * 7:(g + 1) * 7], waits=[t])
        tok_kept = tk
        trk = None
        for g in range(NTT):
            tmm = None
            for gp in range(g + 1):
                tmm = b.MM(pf[4][:, 16:23],
                           (P["trib"] if gp == g else P["onesb"])[:],
                           keptb[:, gp * 7:(gp + 1) * 7],
                           gp == 0, gp == g, waits=[tok_kept, trk])
            trk = b.AC(rank[:, g * 7:(g + 1) * 7], pf[4][:, 16:23],
                       waits=[tmm])
        tok_rank = trk

        tz = b.DV("memset", oe[:], 0.0)
        for g in range(NTT):
            b.dma("sy", dt["routed"][g * 128:(g + 1) * 128, :], oe[:],
                  waits=[tz])
        tzd_all = self.qtok("sy")

        tok_scat = None
        oe_free = tz
        for ex in range(E_R):
            self.new_phase()
            tdi = None
            tdw = None
            for g in range(NTT):
                t = b.DV("tensor_scalar", st_e[:], P["iota224"][:],
                         rank[:, g * 7 + ex: g * 7 + ex + 1], None,
                         ALU.subtract, waits=[tok_rank, tdi, tdw])
                t = b.DV("tensor_single_scalar", st_e[:], st_e[:], 0.0,
                         ALU.is_equal, waits=[t])
                t = b.DV("tensor_scalar", st_e[:], st_e[:],
                         kept[:, g * 7 + ex: g * 7 + ex + 1], None,
                         ALU.mult, waits=[t])
                tdi = b.MM(pf[4][0:1, 32:32 + SLOTS], P["tokid"][:, g:g + 1],
                           st_e[:], g == 0, g == NTT - 1, waits=[t])
                tdw = b.MM(pf[5][0:1, 0:SLOTS],
                           affw[:, g * 7 + ex: g * 7 + ex + 1],
                           st_e[:], g == 0, g == NTT - 1, waits=[t])
            t1 = b.AC(idxrow[:], pf[4][0:1, 32:32 + SLOTS], waits=[tdi])
            t2 = b.AC(wrow[:], pf[5][0:1, 0:SLOTS], waits=[tdw])
            tix = None
            for s2 in range(2):
                wdt = 128 if s2 == 0 else SLOTS - 128
                tp = b.TP(pf[4][0:wdt, 128:129],
                          idxrow[:, s2 * 128: s2 * 128 + wdt],
                          P["identf"][0:1, 0:1], waits=[t1, t2, tix])
                tix = b.DV("tensor_copy", idxcol[0:wdt, s2:s2 + 1],
                           pf[4][0:wdt, 128:129], waits=[tp])
                tp = b.TP(pf[4][0:wdt, 129:130],
                          wrow[:, s2 * 128: s2 * 128 + wdt],
                          P["identf"][0:1, 0:1], waits=[tix])
                tix = b.DV("tensor_copy", wcol[0:wdt, s2:s2 + 1],
                           pf[4][0:wdt, 129:130], waits=[tp])
            tok_xgT = None
            for s2 in range(2):
                wdt = 128 if s2 == 0 else SLOTS - 128
                tg = self.idma(
                    lambda e, s2=s2, wdt=wdt: e.indirect_dma_start(
                        out=xg[0:wdt, :], out_offset=None,
                        in_=dt["xn_tm"][:, :],
                        in_offset=IndirectOffsetOnAxis(idxcol[0:wdt, s2:s2 + 1], 0)),
                    waits=[tix, tok_scat, tok_xgT] + list(self.tok_xntm))
                for c in range(NCH):
                    tp = b.TP(pb[0][:, (c % 8) * 128:(c % 8) * 128 + wdt],
                              xg[0:wdt, c * 128:(c + 1) * 128],
                              P["identb"][0:wdt, 0:wdt], waits=[tg, tok_xgT])
                    tok_xgT = b.AC(
                        xgT[:, c * SLOTS + s2 * 128: c * SLOTS + s2 * 128 + wdt],
                        pb[0][:, (c % 8) * 128:(c % 8) * 128 + wdt], waits=[tp])
            tok_glu = None
            for s2 in range(2):
                wdt = 128 if s2 == 0 else SLOTS - 128
                tmm = None
                for d in range(NCH):
                    buf = d % 2
                    tl1 = b.dma("sy", wlb[buf][:],
                                dt["rw1_b"][ex * 2048 + d * 128:
                                            ex * 2048 + (d + 1) * 128, 0:1024],
                                waits=[wfree[buf], self.wtok["rw1"]])
                    tc = tl1
                    for cg in range(2):
                        tmm = b.MM(pf[cg][0:wdt, :],
                                   xgT[:, d * SLOTS + s2 * 128:
                                       d * SLOTS + s2 * 128 + wdt],
                                   wlb[buf][:, cg * 512:(cg + 1) * 512],
                                   d == 0, d == NCH - 1,
                                   waits=[tc, tok_xgT,
                                          tok_glu if d == 0 else None])
                    wfree[buf] = tmm
                tev = None
                for cg in range(2):
                    tev = b.AC(h1[0:wdt, cg * 512:(cg + 1) * 512],
                               pf[cg][0:wdt, :], waits=[tmm, tok_glu])
                tmm = None
                for d in range(NCH):
                    buf = d % 2
                    tc = b.dma("sy", wlb[buf][:],
                               dt["rw1_b"][ex * 2048 + d * 128:
                                           ex * 2048 + (d + 1) * 128,
                                           1024:2048],
                               waits=[wfree[buf]])
                    for cg in range(2):
                        tmm = b.MM(pf[2 + cg][0:wdt, :],
                                   xgT[:, d * SLOTS + s2 * 128:
                                       d * SLOTS + s2 * 128 + wdt],
                                   wlb[buf][:, cg * 512:(cg + 1) * 512],
                                   d == 0, d == NCH - 1,
                                   waits=[tc, tev if d == 0 else None])
                    wfree[buf] = tmm
                for cg in range(2):
                    tev = b.AC(h1[0:wdt, 1024 + cg * 512:1024 + (cg + 1) * 512],
                               pf[2 + cg][0:wdt, :], waits=[tmm])
                t = b.AC(gu[0:wdt, :], h1[0:wdt, 1024:2048],
                         waits=[tev], func=ACTF.Silu)
                t = b.DV("tensor_mul", gu[0:wdt, :], gu[0:wdt, :],
                         h1[0:wdt, 0:1024], waits=[t])
                tok_glu = b.DV("tensor_scalar", gu[0:wdt, :], gu[0:wdt, :],
                               wcol[0:wdt, s2:s2 + 1], None, ALU.mult,
                               waits=[t])
                tgt = tok_glu
                for c in range(8):
                    tp = b.TP(pb[0][:, (c % 8) * 128:(c % 8) * 128 + wdt],
                              gu[0:wdt, c * 128:(c + 1) * 128],
                              P["identb"][0:wdt, 0:wdt], waits=[tgt])
                    tgt = b.AC(
                        guT[:, c * SLOTS + s2 * 128: c * SLOTS + s2 * 128 + wdt],
                        pb[0][:, (c % 8) * 128:(c % 8) * 128 + wdt], waits=[tp])
                tok_glu = tgt
            tok_guT = tok_glu
            for s2 in range(2):
                wdt = 128 if s2 == 0 else SLOTS - 128
                for ps in range(2):
                    c0 = ps * 2
                    tmm = None
                    for f in range(8):
                        buf = f % 2
                        tc = b.dma("sy", wlb[buf][:],
                                   dt["rw2_b"][ex * 1024 + f * 128:
                                               ex * 1024 + (f + 1) * 128,
                                               c0 * 512:(c0 + 2) * 512],
                                   waits=[wfree[buf], self.wtok["rw2"]])
                        for cg in range(2):
                            tmm = b.MM(pf[cg][0:wdt, :],
                                       guT[:, f * SLOTS + s2 * 128:
                                           f * SLOTS + s2 * 128 + wdt],
                                       wlb[buf][:, cg * 512:(cg + 1) * 512],
                                       f == 0, f == 7,
                                       waits=[tc, tok_guT,
                                              tok_scat if f == 0 else None])
                        wfree[buf] = tmm
                    tev = None
                    for cg in range(2):
                        cc = c0 + cg
                        tev = b.AC(oe[0:wdt, cc * 512:(cc + 1) * 512],
                                   pf[cg][0:wdt, :], waits=[tmm, tok_scat])
                    tok_scat = self.idma(
                        lambda e, s2=s2, wdt=wdt, c0=c0: e.indirect_dma_start(
                            out=dt["routed"][:, :],
                            out_offset=IndirectOffsetOnAxis(
                                idxcol[0:wdt, s2:s2 + 1], 0),
                            in_=oe[0:wdt, c0 * 512:(c0 + 2) * 512],
                            in_offset=None, compute_op=ALU.add,
                            element_offset=c0 * 512),
                        waits=[tev, tok_scat] + list(tzd_all))
        tok_scatter = None
        scat_all = self.qtok("gp")
        self.new_phase()

        tok_g = None
        trb_prev = None
        for c in range(NCH):
            for g in range(NTT):
                b.dma("sc", rstg[c % 2][:, g * 128:(g + 1) * 128],
                      dt["routed"][g * 128:(g + 1) * 128,
                                   c * 128:(c + 1) * 128],
                      waits=[trb_prev] + list(scat_all))
            trb_all = self.qtok("sc")
            trb_prev = None
            ttp = None
            for g in range(NTT):
                tp = b.TP(pb[1][:, g * 128:(g + 1) * 128],
                          rstg[c % 2][:, g * 128:(g + 1) * 128],
                          P["identb"][:], waits=[ttp, tok_g] + list(trb_all))
                ttp = tp
            trb_prev = ttp
            tmm = None
            for f in range(8):
                buf = f % 2
                tc = b.dma("sy", wlb[buf][:, 0:128],
                           dt["sw2_b"][f * 128:(f + 1) * 128,
                                       c * 128:(c + 1) * 128],
                           waits=[wfree[buf], self.wtok["sw2"]])
                for th in range(2):
                    tmm = b.MM(pf[4 + th][:, :],
                               wlb[buf][:, 0:128],
                               sguT[:, f * TPC + th * 512: f * TPC + th * 512 + 512],
                               f == 0, f == 7,
                               waits=[tc, tok_sgu, tok_g if f == 0 else None])
                wfree[buf] = tmm
            for th in range(2):
                trt = b.AC(silt[:], pb[1][:, th * 512:(th + 1) * 512],
                           waits=[ttp, tok_g])
                t = b.DV("tensor_add", cstg[:], pf[4 + th][:, :],
                         silt[:], waits=[tmm, trt])
                tok_g = b.AC(
                    self.gprime[:, c * TPC + th * 512: c * TPC + th * 512 + 512],
                    cstg[:], waits=[t], func=ACTF.Silu)
        self.tok_gprime = tok_g
        self.dbg_dump("gprime", self.gprime, waits=[tok_g])
        ar.reset()

    def phase_attn(self):
        b, ar, pf, pb, P, dt = self, self.ar, self.pf, self.pb, self.P, self.dt
        ar.mark()
        qfT = ar.alloc("qfT", [128, S], BF16)
        kfT = ar.alloc("kfT", [128, S], BF16)
        vfT_off = ar.off
        vfT = ar.alloc("vfT", [128, S], BF16)
        v_tm = ar.alloc("v_tm", [128, S], BF16)
        PT = [ar.alloc(f"PT{i}", [128, 512], BF16,
                       at=vfT_off + i * 1024) for i in range(2)]
        o_stage = ar.alloc("o_stage", [128, 512], BF16, at=vfT_off + 2048)
        lrow = ar.alloc("lrow", [1, 512], F32, at=vfT_off + 3072)
        linv = ar.alloc("linv", [1, 512], F32, at=vfT_off + 5120)
        binv = ar.alloc("binv", [128, 512], F32, at=vfT_off + 7168)

        ag1av = dt["ag1a_out"]
        tok_ag2in = None
        tok_prev_head = None
        for lh in range(2):
            self.new_phase()
            tg = {0: None, 1: None, 2: None}
            for sec, buf in [(0, qfT), (1, kfT), (2, vfT)]:
                for r in range(NCORES):
                    tg[sec] = self.idma(
                        lambda e, sec=sec, buf=buf, r=r, lh=lh:
                        e.indirect_dma_start(
                            out=buf[:, r * TPC:(r + 1) * TPC], out_offset=None,
                            in_=ag1av[:, :],
                            in_offset=IndirectOffsetOnAxis(
                                P["idxqkv"][:, (sec * 2 + lh) * 8 + r:(sec * 2 + lh) * 8 + r + 1], 0)),
                        waits=[self.tok_cc1a, self.tok_x, tok_prev_head])
            gath_all = self.qtok("gp")
            tvt = None
            for kt in range(64):
                tp = b.TP(pb[0][:, (kt % 8) * 128:(kt % 8) * 128 + 128],
                          vfT[:, kt * 128:(kt + 1) * 128], P["identb"][:],
                          waits=[tvt] + list(gath_all))
                tvt = b.AC(v_tm[:, kt * 128:(kt + 1) * 128],
                           pb[0][:, (kt % 8) * 128:(kt % 8) * 128 + 128],
                           waits=[tp])
            tok_norm = None
            exp_t = {}
            pv_t = {}
            for qt in range(16):
                nkc = 4 * (qt + 1)
                rhsq = qfT[:, qt * 512:(qt + 1) * 512]
                for kc in range(nkc):
                    sbank = kc % 2
                    tsc = b.MM(pf[sbank][:, :], kfT[:, kc * 128:(kc + 1) * 128],
                               rhsq, True, True,
                               waits=[exp_t.get((qt, kc - 2)), tok_norm]
                               + list(gath_all))
                    te = b.AC(PT[sbank][:], pf[sbank][:, :],
                              waits=[tsc, pv_t.get((qt, kc - 2))], func=ACTF.Exp)
                    if kc >= 4 * qt:
                        o = kc - 4 * qt
                        te = b.DV("tensor_mul", PT[sbank][:], PT[sbank][:],
                                  P["causal"][:, 384 - o * 128: 896 - o * 128],
                                  waits=[te])
                    exp_t[(qt, kc)] = te
                    b.MM(pf[2][:, :], v_tm[:, kc * 128:(kc + 1) * 128],
                         PT[sbank][:], kc == 0, kc == nkc - 1,
                         waits=[te, tvt, tok_norm])
                    pv_t[(qt, kc)] = b.MM(pf[3][0:1, :], P["onesb"][:, 0:1],
                                          PT[sbank][:], kc == 0, kc == nkc - 1,
                                          waits=[te, tok_norm if kc == 0 else None])
                t = b.AC(lrow[:], pf[3][0:1, :], waits=[pv_t[(qt, nkc - 1)]])
                t = b.DV("reciprocal", linv[:], lrow[:], waits=[t])
                tb = b.MM(pf[4][:, :], P["onesf"][:], linv[:], True, True,
                          waits=[t, tok_norm])
                tb = b.AC(binv[:], pf[4][:, :], waits=[tb])
                tok_norm = b.DV("tensor_mul", o_stage[:], pf[2][:, :],
                                binv[:], waits=[tb, tok_ag2in])
                tok_ag2in = b.dma("sy", dt["ag2_in"][lh * 128:(lh + 1) * 128,
                                                     qt * 512:(qt + 1) * 512],
                                  o_stage[:], waits=[tok_norm])
            tok_prev_head = tok_ag2in
        self.tok_cc2 = b.AG("cc2", "ag2_in", "ag2_out",
                            waits=list(self.qtok("sy")))
        ar.reset()

    def phase_combine0(self):
        b, ar, pf, P, dt = self, self.ar, self.pf, self.P, self.dt
        ar.mark()
        s_t = [ar.alloc(f"s_t{i}", [128, TPC], BF16) for i in range(2)]
        fstg = ar.alloc("fstg", [128, TPC], F32)
        tok_x1 = None
        tprev = [None, None]
        for c in range(NCH):
            buf = c % 2
            tgt = self.idma(
                lambda e, c=c, buf=buf: e.indirect_dma_start(
                    out=s_t[buf][:], out_offset=None,
                    in_=dt["ag2_out"][:, :],
                    in_offset=IndirectOffsetOnAxis(P["idxs2"][:, c:c + 1], 0)),
                waits=[self.tok_cc2, tprev[buf]])
            t = b.DV("tensor_mul", fstg[:], s_t[buf][:],
                     self.gprime[:, c * TPC:(c + 1) * TPC],
                     waits=[tgt, self.tok_gprime, tok_x1])
            tprev[buf] = t
            tok_x1 = b.DV("tensor_add", P["x"][:, c * TPC:(c + 1) * TPC],
                          P["x"][:, c * TPC:(c + 1) * TPC], fstg[:], waits=[t])
        self.tok_x1 = tok_x1
        self.dbg_dump("x1", P["x"], waits=[tok_x1])
        ar.reset()

    def phase_l1(self):
        b, ar, pf, pb, P, dt = self, self.ar, self.pf, self.pb, self.P, self.dt
        mu1 = ar.alloc("mu1", [128, NTT], F32)
        inv1 = ar.alloc("inv1", [128, NTT], F32)
        tok_xn1, _ = self.ln(P["x"], P["xn"], mu1, inv1, 0,
                             self.tok_x1, "ln1")
        self.new_phase()
        self.dbg_dump("xn1", P["xn"], waits=[tok_xn1])
        tok_xd = None
        for c in range(NCH):
            tok_xd = b.dma("sy", dt["x_d"][:, c * TPC:(c + 1) * TPC],
                           P["x"][:, c * TPC:(c + 1) * TPC],
                           waits=[self.tok_x1])
        ar.mark()
        regA = ar.alloc("regA", [128, 32 * TPC], BF16, at=0)
        gp_off = None
        def slot(j):
            if j < 32:
                return regA[:, j * TPC:(j + 1) * TPC]
            return self.gprime[:, (j - 32) * TPC:(j - 31) * TPC]

        wl = [ar.alloc(f"l1wl{i}", [128, 768], F32) for i in range(2)]
        wli = [ar.alloc(f"l1wi{i}", [128, 768], I32, at=None) for i in range(0)]
        wlb = [ar.alloc(f"l1wb{i}", [128, 768], BF16) for i in range(2)]

        wfree = [None, None]
        tprev = None
        for cp in range(16):
            tmm = None
            for d in range(NCH):
                buf = d % 2
                tc = b.dma("sy", wlb[buf][:, 0:384],
                           dt["wqkv_b"][d * 128:(d + 1) * 128,
                                        cp * 384:(cp + 1) * 384],
                           waits=[wfree[buf], self.wtok["wqkv"]])
                for cg in range(3):
                    for th in range(2):
                        tmm = b.MM(pf[cg * 2 + th][:, :],
                                   wlb[buf][:, cg * 128:(cg + 1) * 128],
                                   P["xn"][:, d * TPC + th * 512:
                                           d * TPC + th * 512 + 512],
                                   d == 0, d == NCH - 1,
                                   waits=[tc, tok_xn1, tok_xd,
                                          tprev if d == 0 else None])
                wfree[buf] = tmm
            for cg in range(3):
                j = cp * 3 + cg
                for th in range(2):
                    tprev = b.AC(slot(j)[:, th * 512:(th + 1) * 512],
                                 pf[cg * 2 + th][:, :], waits=[tmm],
                                 func=ACTF.Silu)
        tok_qkv = tprev

        self.new_phase()
        ktm = ar.alloc("ktm", [128, 256], BF16)
        vtm = ar.alloc("vtm", [128, 256], BF16)
        kmt = ar.alloc("kmt", [128, 256], BF16)
        cstage = ar.alloc("cstage", [128, 128], F32)
        tstage = ar.alloc("tstage", [128, 128], F32)
        slopes = _slopes(HEADS)
        tok_ag3in = None
        tcd = None
        for h in range(HEADS):
            bd = math.exp(-slopes[h] * BLK)
            tT = None
            for blk in range(NBLK_C):
                ttp = None
                for ts in range(2):
                    tp = b.TP(pb[0][:, ts * 128: ts * 128 + 128],
                              slot(3 * h + 1)[:, blk * 256 + ts * 128:
                                              blk * 256 + (ts + 1) * 128],
                              P["identb"][:], waits=[tok_qkv, ttp])
                    ttp = b.AC(ktm[:, ts * 128:(ts + 1) * 128],
                               pb[0][:, ts * 128: ts * 128 + 128], waits=[tp])
                    tp = b.TP(pb[0][:, 256 + ts * 128: 256 + ts * 128 + 128],
                              slot(3 * h + 2)[:, blk * 256 + ts * 128:
                                              blk * 256 + (ts + 1) * 128],
                              P["identb"][:], waits=[tok_qkv, ttp])
                    ttp = b.AC(vtm[:, ts * 128:(ts + 1) * 128],
                               pb[0][:, 256 + ts * 128: 256 + ts * 128 + 128],
                               waits=[tp])
                tkm = None
                for ts in range(2):
                    tkm = b.DV("tensor_scalar",
                               kmt[:, ts * 128:(ts + 1) * 128],
                               ktm[:, ts * 128:(ts + 1) * 128],
                               P["kdec"][:, h * 2 + ts: h * 2 + ts + 1], None,
                               ALU.mult, waits=[ttp, tkm])
                tmm = None
                for ts in range(2):
                    tmm = b.MM(pf[5][:, 0:128],
                               kmt[:, ts * 128:(ts + 1) * 128],
                               vtm[:, ts * 128:(ts + 1) * 128],
                               ts == 0, ts == 1, waits=[tkm, tT])
                tcs = b.AC(cstage[:], pf[5][:, 0:128], waits=[tmm, tcd])
                tcd = b.dma("sc", dt["cown"][(h * NBLK_C + blk) * 128:
                                             (h * NBLK_C + blk + 1) * 128, :],
                            cstage[:], waits=[tcs])
                if blk == 0:
                    tT = b.DV("tensor_copy", tstage[:], cstage[:],
                              waits=[tcs, tok_ag3in])
                else:
                    tT = b.DV("scalar_tensor_tensor", tstage[:], tstage[:],
                              bd, cstage[:], ALU.mult, ALU.add, waits=[tcs])
            tok_ag3in = b.dma("sc", dt["ag3_in"][h * 128:(h + 1) * 128, :],
                              tstage[:], waits=[tT, tok_ag3in])
        tok_cc3 = b.AG("cc3", "ag3_in", "ag3_out",
                       waits=list(self.qtok("sc")))

        kv = ar.alloc("kv", [128, 128], F32)
        kvb = ar.alloc("kvb", [128, 128], BF16)
        Tld = [ar.alloc(f"Tld{i}", [128, 128], F32) for i in range(2)]
        Cld = [ar.alloc(f"Cld{i}", [128, 128], F32) for i in range(2)]
        qdr = ar.alloc("qdr", [1, 256], F32)
        qm = ar.alloc("qm", [128, 256], BF16)
        dd = [ar.alloc(f"dd{i}", [128, 512], F32) for i in range(2)]
        PT = [ar.alloc(f"l1PT{i}", [128, 256], BF16) for i in range(2)]
        tqm_last = None
        for h in range(HEADS):
            self.new_phase()
            bd = math.exp(-slopes[h] * BLK)
            tkv = None
            tldp = [None, None]
            for r in range(NCORES - 1):
                tl = b.dma("sy", Tld[r % 2][:],
                           dt["ag3_out"][(r * HEADS + h) * 128:
                                         (r * HEADS + h + 1) * 128, :],
                           waits=[tok_cc3, tldp[r % 2]])
                if r == 0:
                    tkv = b.DV("tensor_scalar", kv[:], Tld[0][:],
                               P["coef"][:, 0 * HEADS + h: 0 * HEADS + h + 1],
                               None, ALU.mult, waits=[tl])
                else:
                    tkv = b.DV("scalar_tensor_tensor", kv[:], Tld[r % 2][:],
                               P["coef"][:, r * HEADS + h: r * HEADS + h + 1],
                               kv[:], ALU.mult, ALU.add, waits=[tl, tkv])
                tldp[r % 2] = tkv
            tql = b.dma("sy", qdr[:], dt["qdec_d"][h:h + 1, :],
                        waits=[tqm_last])
            tqb = b.MM(pf[4][:, 0:256], P["onesf"][:], qdr[:], True, True,
                       waits=[tql, tqm_last])
            tclp = [None, None]
            tddp = [None, None]
            tevac = None
            for blk in range(NBLK_C):
                ttp = None
                for ts in range(2):
                    tp = b.TP(pb[0][:, 512 + ts * 128: 512 + ts * 128 + 128],
                              slot(3 * h + 2)[:, blk * 256 + ts * 128:
                                              blk * 256 + (ts + 1) * 128],
                              P["identb"][:], waits=[tok_qkv, ttp])
                    ttp = b.AC(vtm[:, ts * 128:(ts + 1) * 128],
                               pb[0][:, 512 + ts * 128: 512 + ts * 128 + 128],
                               waits=[tp])
                tkvb = b.DV("tensor_copy", kvb[:], kv[:], waits=[tkv, tevac])
                tqm = b.DV("tensor_mul", qm[:],
                           slot(3 * h)[:, blk * 256:(blk + 1) * 256],
                           pf[4][:, 0:256], waits=[tqb, tok_qkv, tevac])
                tqm_last = tqm
                tnd = b.MM(pf[3][:, 0:256], kvb[:], qm[:], True, False,
                           waits=[tkvb, tqm])
                tpt = None
                for ks in range(2):
                    tdd = b.dma("sy", dd[ks][:, 0:256],
                                dt["ddec_d"][(h * 2 + ks) * 128:
                                             (h * 2 + ks + 1) * 128, :],
                                waits=[tddp[ks]])
                    tqk = b.MM(pf[ks][:, 0:256],
                               slot(3 * h + 1)[:, blk * 256 + ks * 128:
                                               blk * 256 + (ks + 1) * 128],
                               slot(3 * h)[:, blk * 256:(blk + 1) * 256],
                               True, True, waits=[tok_qkv, tpt, tevac])
                    tpt = b.DV("tensor_mul", PT[ks][:], pf[ks][:, 0:256],
                               dd[ks][:, 0:256], waits=[tqk, tdd])
                    tddp[ks] = tpt
                    b.MM(pf[3][:, 0:256], vtm[:, ks * 128:(ks + 1) * 128],
                         PT[ks][:], False, ks == 1, waits=[tpt, ttp])
                    if ks == 1:
                        tdone = (self.sem["pe"], self.sem["pe"].n)
                tevac = b.AC(slot(3 * h)[:, blk * 256:(blk + 1) * 256],
                             pf[3][:, 0:256], waits=[tdone])
                tcl = b.dma("sy", Cld[blk % 2][:],
                            dt["cown"][(h * NBLK_C + blk) * 128:
                                       (h * NBLK_C + blk + 1) * 128, :],
                            waits=[tclp[blk % 2], tok_ag3in])
                tkv = b.DV("scalar_tensor_tensor", kv[:], kv[:], bd,
                           Cld[blk % 2][:], ALU.mult, ALU.add,
                           waits=[tcl, tkvb, tnd])
                tclp[blk % 2] = tkv
            tok_scan = tevac

        self.new_phase()
        sqs = ar.alloc("sqs", [128, TPC], BF16)
        nrows = ar.alloc("nrows", [1, 2 * TPC], F32)
        rs_bc = ar.alloc("rs_bc", [128, TPC], F32)
        tvar = None
        tsq = None
        for hc in range(NCH):
            tsq = b.AC(sqs[:], slot(3 * (hc // 1) + 0)[:, 0:0]
                       if False else sqs[:], waits=[]) if False else tsq
        tvarp = [None, None]
        for hc in range(NCH):
            tsq = b.AC(sqs[:], slot(3 * hc)[:, 0:TPC],
                       waits=[tok_scan, tsq, tvarp[0], tvarp[1]],
                       func=ACTF.Square)
            for th in range(2):
                tvp = b.MM(pf[th][0:1, :], P["onesb"][:, 0:1],
                           sqs[:, th * 512:(th + 1) * 512],
                           hc == 0, hc == NCH - 1, waits=[tsq])
                tvarp[th] = tvp
        tbc = None
        for th in range(2):
            o = th * 512
            t = b.DV("tensor_scalar_mul", nrows[:, o:o + 512],
                     pf[th][0:1, :], 1.0 / D, waits=[tvarp[th]])
            t = b.AC(nrows[:, o:o + 512], nrows[:, o:o + 512],
                     waits=[t], func=ACTF.Sqrt,
                     bias=P["fcon"][0:1, 1:2])
            t = b.DV("reciprocal", nrows[:, TPC + o:TPC + o + 512],
                     nrows[:, o:o + 512], waits=[t])
            tb = b.MM(pf[2 + th][:, :], P["onesf"][:],
                      nrows[:, TPC + o:TPC + o + 512], True, True, waits=[t])
            tbc = b.AC(rs_bc[:, o:o + 512], pf[2 + th][:, :], waits=[tb])

        self.new_phase()
        sigst = ar.alloc("sigst", [128, 512], BF16)
        gtst = ar.alloc("gtst", [128, 512], F32)
        wfree = [None, None]
        tprev = None
        for cp in range(6):
            ch0 = cp * 3
            ncg = 3 if cp < 5 else 1
            tmm = None
            for d in range(NCH):
                buf = d % 2
                tc = b.dma("sy", wlb[buf][:, 0:ncg * 128],
                           dt["wg_b"][d * 128:(d + 1) * 128,
                                      ch0 * 128:(ch0 + ncg) * 128],
                           waits=[wfree[buf], self.wtok["wg"]])
                for cg in range(ncg):
                    for th in range(2):
                        tmm = b.MM(pf[cg * 2 + th][:, :],
                                   wlb[buf][:, cg * 128:(cg + 1) * 128],
                                   P["xn"][:, d * TPC + th * 512:
                                           d * TPC + th * 512 + 512],
                                   d == 0, d == NCH - 1,
                                   waits=[tc, tprev if d == 0 else None])
                wfree[buf] = tmm
            for cg in range(ncg):
                hc = ch0 + cg
                for th in range(2):
                    t = b.AC(sigst[:], pf[cg * 2 + th][:, :],
                             waits=[tmm, tprev], func=ACTF.Sigmoid)
                    t = b.DV("scalar_tensor_tensor", gtst[:],
                             slot(3 * hc)[:, th * 512:(th + 1) * 512],
                             P["nwv"][:, hc:hc + 1],
                             rs_bc[:, th * 512:(th + 1) * 512],
                             ALU.mult, ALU.mult, waits=[t, tbc, tok_scan])
                    tprev = b.DV("tensor_mul",
                                 slot(3 * hc)[:, th * 512:(th + 1) * 512],
                                 gtst[:], sigst[:], waits=[tprev, t])
        tok_gated = tprev

        self.new_phase()
        wfree = [None, None]
        tprev = None
        for cp in range(6):
            ch0 = cp * 3
            ncg = 3 if cp < 5 else 1
            tmm = None
            for d in range(NCH):
                buf = d % 2
                tc = b.dma("sy", wlb[buf][:, 0:ncg * 128],
                           dt["wout_b"][d * 128:(d + 1) * 128,
                                        ch0 * 128:(ch0 + ncg) * 128],
                           waits=[wfree[buf], self.wtok["wout"]])
                for cg in range(ncg):
                    for th in range(2):
                        tmm = b.MM(pf[cg * 2 + th][:, :],
                                   wlb[buf][:, cg * 128:(cg + 1) * 128],
                                   slot(3 * d)[:, th * 512:(th + 1) * 512],
                                   d == 0, d == NCH - 1,
                                   waits=[tc, tok_gated,
                                          tprev if d == 0 else None])
                wfree[buf] = tmm
            for cg in range(ncg):
                hc = ch0 + cg
                for th in range(2):
                    t = b.AC(sigst[:], pf[cg * 2 + th][:, :],
                             waits=[tmm, tprev])
                    tprev = b.dma("sc", dt["s1_d"][hc * 128:(hc + 1) * 128,
                                                   th * 512:(th + 1) * 512],
                                  sigst[:], waits=[t])
        tok_s1 = tprev
        self.tok_s1 = tok_s1
        self.tok_l1ffn_deps = (tok_s1, tok_xn1)
        ar.reset()

    def phase_out(self):
        b, ar, pf, pb, P, dt = self, self.ar, self.pf, self.pb, self.P, self.dt
        tok_s1, tok_xn1 = self.tok_l1ffn_deps
        ar.mark()
        gust = ar.alloc("gust", [128, L1HC * 512], BF16)
        gstg = ar.alloc("gstg", [128, 6 * 512], BF16)
        wl = [ar.alloc(f"fwl{i}", [128, 768], I32) for i in range(2)]
        wlb = [ar.alloc(f"fwb{i}", [128, 768], BF16) for i in range(2)]
        s1t = [ar.alloc("s1t0", [128, 512], BF16)] * 2
        xt = [ar.alloc("xt0", [128, 512], F32)] * 2
        g1s = ar.alloc("g1s", [128, 512], BF16)
        fst = ar.alloc("fst", [128, 512], F32)

        tok_out = None
        for th in range(2):
            self.new_phase()
            wfree = [None, None]
            tprev = None
            for cp in range(8):
                hc0 = cp * 6
                ncg = 6 if cp < 7 else 2
                for which, wsrc, scl, fn in [
                        (0, "wgT_i", P["sgv"], ACTF.Silu),
                        (1, "wuT_i", P["suv"], ACTF.Copy)]:
                    tmm = None
                    for d in range(NCH):
                        buf = d % 2
                        tl = b.dma("sy", wl[buf][:, 0:ncg * 128],
                                   dt[wsrc][d * 128:(d + 1) * 128,
                                            hc0 * 128:(hc0 + ncg) * 128],
                                   waits=[wfree[buf]])
                        tc = b.DV("tensor_scalar_add", wlb[buf][:, 0:ncg * 128],
                                  wl[buf][:, 0:ncg * 128], -128.0, waits=[tl])
                        for cg in range(ncg):
                            tmm = b.MM(pf[cg % 6][:, :],
                                       wlb[buf][:, cg * 128:(cg + 1) * 128],
                                       P["xn"][:, d * TPC + th * 512:
                                               d * TPC + th * 512 + 512],
                                       d == 0, d == NCH - 1,
                                       waits=[tc, tok_xn1,
                                              tprev if d == 0 else None])
                        wfree[buf] = tmm
                    for cg in range(ncg):
                        hc = hc0 + cg
                        if which == 0:
                            tprev = b.AC(gstg[:, cg * 512:(cg + 1) * 512],
                                         pf[cg % 6][:, :], waits=[tmm],
                                         func=fn, scale=scl[:, hc:hc + 1])
                        else:
                            t = b.AC(g1s[:], pf[cg % 6][:, :],
                                     waits=[tmm, tprev],
                                     func=fn, scale=scl[:, hc:hc + 1])
                            tprev = b.DV("tensor_mul",
                                         gust[:, hc * 512:(hc + 1) * 512],
                                         gstg[:, cg * 512:(cg + 1) * 512],
                                         g1s[:], waits=[t])
            tok_gu = tprev
            self.new_phase()
            wfree = [None, None]
            tprev = None
            for cp in range(6):
                ch0 = cp * 3
                ncg = 3 if cp < 5 else 1
                tmm = None
                for f in range(L1HC):
                    buf = f % 2
                    tl = b.dma("sy", wl[buf][:, 0:ncg * 128],
                               dt["wdT_i"][f * 128:(f + 1) * 128,
                                           ch0 * 128:(ch0 + ncg) * 128],
                               waits=[wfree[buf]])
                    tc = b.DV("tensor_scalar_add", wlb[buf][:, 0:ncg * 128],
                              wl[buf][:, 0:ncg * 128], -128.0, waits=[tl])
                    for cg in range(ncg):
                        tmm = b.MM(pf[cg * 2][:, :],
                                   wlb[buf][:, cg * 128:(cg + 1) * 128],
                                   gust[:, f * 512:(f + 1) * 512],
                                   f == 0, f == L1HC - 1,
                                   waits=[tc, tok_gu,
                                          tprev if f == 0 else None])
                    wfree[buf] = tmm
                for cg in range(ncg):
                    c = ch0 + cg
                    tsl = b.dma("sc", s1t[cg % 2][:],
                                dt["s1_d"][c * 128:(c + 1) * 128,
                                           th * 512:(th + 1) * 512],
                                waits=[tok_s1, tprev])
                    txl = b.dma("sc", xt[cg % 2][:],
                                dt["x_d"][:, c * TPC + th * 512:
                                          c * TPC + th * 512 + 512],
                                waits=[tprev])
                    t = b.AC(g1s[:], pf[cg * 2][:, :], waits=[tmm, tprev],
                             func=ACTF.Silu, scale=P["sdv"][:, c:c + 1])
                    t = b.DV("tensor_mul", fst[:], g1s[:], s1t[cg % 2][:],
                             waits=[t, tsl, tok_out])
                    t = b.DV("tensor_add", fst[:], fst[:], xt[cg % 2][:],
                             waits=[t, txl])
                    tprev = t
                    tok_out = self.dma(
                        "gp", self.out_fm[c * 128:(c + 1) * 128,
                                          th * 512:(th + 1) * 512],
                        fst[:], waits=[t])
        fins = [(s, s.n) for s in self.pool["gp"]]

        def fin(e, _f=tuple(fins)):
            for ws, wn in _f:
                if wn > 0:
                    e.wait_ge(ws.h, wn)
        self.q["gp"].append(fin)
        ar.reset()



_BUILD_CACHE = {}


def _host_tables(core):
    t0 = core * TPC
    pos = np.arange(t0, t0 + TPC, dtype=np.float64)
    freqs = 1.0 / THETA ** (np.arange(0, 96, 2, dtype=np.float64) / 96.0)
    emb = pos[:, None] * freqs[None, :]
    cos = np.cos(emb).astype(np.float32)
    sin = np.sin(emb).astype(np.float32)
    ropec = np.zeros((128, NTT * 48), np.float32)
    ropes = np.zeros((128, NTT * 48), np.float32)
    for g in range(NTT):
        ropec[:, g * 48:(g + 1) * 48] = cos[g * 128:(g + 1) * 128]
        ropes[:, g * 48:(g + 1) * 48] = sin[g * 128:(g + 1) * 128]
    slopes = np.array(_slopes(HEADS), np.float64)
    arr = np.arange(BLK, dtype=np.float64) + 1.0
    qdec = np.exp(-slopes[:, None] * arr[None, :]).astype(np.float32)
    kdec = np.zeros((128, HEADS * 2), np.float32)
    for h in range(HEADS):
        kd = np.exp(-slopes[h] * (BLK - arr)).astype(np.float32)
        kdec[:, h * 2] = kd[0:128]
        kdec[:, h * 2 + 1] = kd[128:256]
    ddec = np.zeros((HEADS * 2 * 128, BLK), np.float32)
    idx = arr[:, None] - arr[None, :]
    for h in range(HEADS):
        m_ = np.where(idx >= 0, np.exp(-slopes[h] * idx), 0.0)
        mT = m_.T.astype(np.float32)
        ddec[(h * 2) * 128:(h * 2 + 1) * 128] = mT[0:128]
        ddec[(h * 2 + 1) * 128:(h * 2 + 2) * 128] = mT[128:256]
    coef = np.zeros((128, 7 * HEADS), np.float32)
    for r in range(NCORES - 1):
        for h in range(HEADS):
            if r < core:
                bd = math.exp(-slopes[h] * BLK)
                coef[:, r * HEADS + h] = bd ** (NBLK_C * (core - 1 - r))
    causal = np.zeros((128, 896), np.float32)
    for p in range(128):
        causal[p, p + 384:] = 1.0
    trib = np.triu(np.ones((128, 128), np.float32), 1)
    iota224 = np.broadcast_to(np.arange(SLOTS, dtype=np.float32), (128, SLOTS)).copy()
    tokid = (np.arange(NTT, dtype=np.float32)[None, :] * 128
             + np.arange(128, dtype=np.float32)[:, None]).astype(np.float32)
    sel64 = np.zeros((64, 8), np.float32)
    sel64[np.arange(64), np.arange(64) % 8] = 1.0
    selmap8 = np.zeros((8, 64), np.float32)
    selmap8[np.arange(64) % 8, np.arange(64)] = 1.0
    idxqkv = np.zeros((128, 48), np.int32)
    for sec in range(3):
        for lh in range(2):
            hme = 2 * core + lh
            for r in range(NCORES):
                idxqkv[:, (sec * 2 + lh) * 8 + r] = \
                    (r * 48 + sec * 16 + hme) * 128 + np.arange(128)
    idxs2 = np.zeros((128, NCH), np.int32)
    for hc in range(NCH):
        idxs2[:, hc] = (hc * 128 + np.arange(128)) * S + TPC * core
    fcon = np.zeros((128, 4), np.float32)
    fcon[:, 0] = 1e-5
    fcon[:, 1] = 1e-6
    fcon[:, 2] = QSCALE
    bf = ml_dtypes.bfloat16
    return dict(
        fcon_d=fcon,
        ropec=ropec, ropes=ropes, qdec_d=qdec, kdec_d=kdec, ddec_d=ddec,
        coef_d=coef, causal_d=causal.astype(bf), identb_d=np.eye(128, dtype=bf),
        identf_d=np.eye(128, dtype=np.float32), trib_d=trib.astype(bf),
        onesb_d=np.ones((128, 128), bf), onesf_d=np.ones((1, 128), np.float32),
        onescol_d=np.ones((128, 1), np.float32), iota224_d=iota224,
        tokid_d=tokid, sel64_d=sel64, selmap8_d=selmap8,
        idxqkv_d=idxqkv, idxs2_d=idxs2)


def kernel(**inputs):
    global LAST_DEBUG
    key = _DEBUG
    if key not in _BUILD_CACHE:
        bld = Builder(debug_point=key)
        _BUILD_CACHE[key] = (bld.build(), bld)
    nc, bld = _BUILD_CACHE[key]

    f32 = np.float32
    x = np.asarray(inputs["L0_attn_x"], f32).reshape(S, D)
    gate_w = np.asarray(inputs["L0_ffn_gate_w"], f32)
    gatew = np.zeros((128, NCH * 7), f32)
    for c in range(NCH):
        gatew[:, c * 7:(c + 1) * 7] = gate_w[c * 128:(c + 1) * 128]
    ebias = np.asarray(inputs["L0_ffn_expert_bias"], f32).reshape(7, 1)
    sgv = np.ascontiguousarray(
        np.asarray(inputs["L1_ffn_s_gate"], f32).reshape(L1HC, 128).T)
    suv = np.ascontiguousarray(
        np.asarray(inputs["L1_ffn_s_up"], f32).reshape(L1HC, 128).T)
    sdv = np.ascontiguousarray(
        np.asarray(inputs["L1_ffn_s_down"], f32).reshape(NCH, 128).T)
    nwv = np.ascontiguousarray(
        np.asarray(inputs["L1_attn_norm_w"], f32).reshape(NCH, 128).T)
    shared = dict(
        wq_w=np.asarray(inputs["L0_attn_Wq"], f32),
        wkv_w=np.asarray(inputs["L0_attn_Wkv_down"], f32),
        wku_w=np.asarray(inputs["L0_attn_Wk_up"], f32),
        wvu_w=np.asarray(inputs["L0_attn_Wv_up"], f32),
        sw1_w=np.asarray(inputs["L0_ffn_shared_w1"], f32).reshape(D, 2048),
        sw2_w=np.asarray(inputs["L0_ffn_shared_w2"], f32).reshape(1024, D),
        rw1_w=np.asarray(inputs["L0_ffn_routed_w1"], f32).reshape(E_R * D, 2048),
        rw2_w=np.asarray(inputs["L0_ffn_routed_w2"], f32).reshape(E_R * 1024, D),
        wqkv_w=np.asarray(inputs["L1_attn_Wqkv"], f32),
        wg_w=np.asarray(inputs["L1_attn_Wg"], f32),
        wout_w=np.asarray(inputs["L1_attn_Wout"], f32),
        wgT_i=np.ascontiguousarray(np.asarray(inputs["L1_ffn_wq_gate"], np.int32).T),
        wuT_i=np.ascontiguousarray(np.asarray(inputs["L1_ffn_wq_up"], np.int32).T),
        wdT_i=np.ascontiguousarray(np.asarray(inputs["L1_ffn_wq_down"], np.int32).T),
        gatew=gatew, ebias=ebias, sgv=sgv, suv=suv, sdv=sdv, nwv=nwv)

    in_maps = []
    for r in range(NCORES):
        m = dict(shared)
        m["x_t"] = np.ascontiguousarray(x[r * TPC:(r + 1) * TPC].T)
        m.update(_host_tables(r))
        in_maps.append(m)
    res = run_bass_kernel_spmd(nc, in_maps, core_ids=list(range(NCORES)))
    out = np.zeros((S, D), f32)
    dbg = {}
    for r in range(NCORES):
        out[r * TPC:(r + 1) * TPC] = res.results[r]["out_fm"].T
        if _DEBUG is not None:
            dbg[r] = res.results[r]["dbg"].T
    LAST_DEBUG = dbg
    return out.reshape(1, S, D)


# version 09
# speedup vs baseline: 1.1633x; 1.1633x over previous
import math
import contextlib
import numpy as np

try:
    import concourse.bass as bass
except ImportError:
    import sys
    sys.path.insert(0, "/opt/trn_rl_repo")
    import concourse.bass as bass
import concourse.mybir as mybir
from concourse.bass_core_rust import IndirectOffsetOnAxis
from concourse.bass_utils import run_bass_kernel_spmd
import ml_dtypes

F32 = mybir.dt.float32
BF16 = mybir.dt.bfloat16
I32 = mybir.dt.int32
ACTF = mybir.ActivationFunctionType
ALU = mybir.AluOpType
AX = mybir.AxisListType

NCORES = 8
S = 8192
D = 2048
TPC = S // NCORES
NCH = D // 128
NTT = TPC // 128
HEADS = 16
HD = 128
LATENT = 512
NLC = LATENT // 128
E_R = 7
CAP = int(S / E_R)
SLOTS = 224
L1H = 5632
L1HC = L1H // 128
BLK = 256
NBLK_C = TPC // BLK
THETA = 10000.0
QSCALE = 1.0 / math.sqrt(HD)
NSEARCH = 36

_DEBUG = None
LAST_DEBUG = None

ENG = ["sy", "sc", "pe", "act", "dve", "gp"]


def _slopes(n):
    def p2(m):
        start = 2 ** (-(2 ** -(math.log2(m) - 3)))
        return [start * start ** i for i in range(m)]
    if math.log2(n).is_integer():
        return p2(n)
    c = 2 ** math.floor(math.log2(n))
    return p2(c) + _slopes(2 * c)[0::2][: n - c]


class Ctr:
    def __init__(self, nc, ctx, name):
        self.h = ctx.enter_context(nc.semaphore(name))
        self.n = 0
        self.name = name


class LazyTok:
    __slots__ = ("sem",)

    def __init__(self, sem):
        self.sem = sem


class Arena:
    def __init__(self, nc, ctx, size_bytes):
        self.nc = nc
        self.size = size_bytes
        self.big = ctx.enter_context(
            nc.sbuf_tensor("arena", [128, size_bytes], mybir.dt.uint8))
        self.base = nc.lookup_mloc(self.big).addr
        self.off = 0
        self.marks = []
        self.peak = 0
        self.cnt = 0

    def alloc(self, name, shape, dtype, at=None):
        per_part = int(np.prod(shape[1:])) * mybir.dt.size(dtype)
        per_part = (per_part + 31) & ~31
        if at is None:
            at = self.off
            self.off += per_part
        self.peak = max(self.peak, at + per_part)
        assert at + per_part <= self.size, (name, at, per_part)
        self.cnt += 1
        h = self.nc.virtual_sbuf_tensor(f"{name}_{self.cnt}", shape, dtype)
        self.nc._tensor(h.name, h.shape, h.dtype, type="SB")
        mloc = self.nc.lookup_mloc(h)
        mloc.allocated = True
        mloc.addr = self.base + at
        return h

    def mark(self):
        self.marks.append(self.off)

    def reset(self):
        self.off = self.marks.pop()


class Builder:
    def __init__(self, debug_point=None):
        self.nc = bass.Bass("TRN2", target_bir_lowering=False)
        self.dbgp = debug_point
        self.dt = {}
        self.q = {e: [] for e in ENG}
        self.bar = ()
        self.pool = {}
        self.rr = {"sy": 0, "sc": 0, "gp": 0}

    def emit(self, eng, build, waits=(), inc=None, k=1):
        eng = "act" if eng == "sc" else eng
        s = self.sem[inc] if isinstance(inc, str) else inc
        flat = []
        for w in waits:
            if w is None:
                continue
            if isinstance(w, tuple) and w and isinstance(w[0], tuple):
                flat.extend(w)
            else:
                flat.append(w)
        wl = tuple(flat) + self.bar

        def run(e, _b=build, _w=wl, _s=s, _k=k, _eng=eng, _seenref=self):
            seen = _seenref._seen[_eng]
            for (ws, wn) in _w:
                if wn > 0 and seen.get(ws.name, -1) < wn:
                    e.wait_ge(ws.h, wn)
                    seen[ws.name] = wn
            ins = _b(e)
            if _s is not None:
                ins.then_inc(_s.h, _k)
        self.q[eng].append(run)
        if s is not None:
            s.n += k
            return (s, s.n)
        return None

    def dma(self, eng, out, in_, waits=()):
        qn = eng
        s = self.pool[qn][self.rr[qn] % 6]
        self.rr[qn] += 1
        return self.emit(eng, lambda e, o=out, i=in_: e.dma_start(out=o, in_=i),
                         waits=tuple(waits) + ((s, s.n),), inc=s, k=16)

    def qtok(self, qn):
        return tuple((s, s.n) for s in self.pool[qn])

    def idma(self, build, waits=()):
        s = self.pool["gp"][self.rr["gp"] % 6]
        self.rr["gp"] += 1
        return self.emit("gp", build, waits=tuple(waits) + ((s, s.n),),
                         inc=s, k=16)

    def MM(self, psum, lhsT, rhs, start, stop, waits=(), inc=True):
        return self.emit(
            "pe", lambda e, p=psum, l=lhsT, r=rhs, s1=start, s2=stop:
            e.matmul(p, l, r, start=s1, stop=s2, skip_group_check=True),
            waits=waits, inc=("pe" if inc else None))

    def TP(self, psum, in_, ident, waits=()):
        return self.emit("pe", lambda e, p=psum, i=in_, d=ident:
                         e.transpose(p, i, d), waits=waits, inc="pe")

    def AC(self, out, in_, waits=(), func=ACTF.Copy, scale=1.0, bias=0.0):
        return self.emit("act", lambda e, o=out, i=in_, f=func, s=scale, bb=bias:
                         e.activation(o, i, f, bias=bb, scale=s),
                         waits=waits, inc="act")

    def DV(self, op, *args, waits=()):
        return self.emit("dve", lambda e, o=op, a=args: getattr(e, o)(*a),
                         waits=waits, inc="dve")

    def AG(self, name, inname, outname, waits=()):
        return self.emit("gp", lambda e, i=inname, o=outname:
                         e.collective_compute(
                             "AllGather", ALU.bypass,
                             replica_groups=[list(range(NCORES))],
                             ins=[self.dt[i].ap()], outs=[self.dt[o].ap()]),
                         waits=waits, inc=name)

    def new_phase(self):
        toks = [(self.sem[s], self.sem[s].n) for s in ("pe", "act", "dve")]
        for qn in ("sy", "sc", "gp"):
            toks += [(s, s.n) for s in self.pool[qn]]
        self.bar = tuple(toks) + getattr(self, "tok_x_all", ())

    def declare(self):
        nc, dt = self.nc, self.dt

        def din(name, shape, dtype):
            dt[name] = nc.dram_tensor(name, shape, dtype, kind="ExternalInput")

        def dint(name, shape, dtype):
            dt[name] = nc.dram_tensor(name, shape, dtype)

        din("x_t", [D, TPC], F32)
        din("wq_w", [D, D], F32)
        din("wkv_w", [D, LATENT], F32)
        din("wku_w", [LATENT, D], F32)
        din("wvu_w", [LATENT, D], F32)
        din("sw1_w", [D, D], F32)
        din("sw2_w", [1024, D], F32)
        din("rw1_w", [E_R * D, D], F32)
        din("rw2_w", [E_R * 1024, D], F32)
        din("wqkv_w", [D, 3 * D], F32)
        din("wg_w", [D, D], F32)
        din("wout_w", [D, D], F32)
        din("wgT_i", [D, L1H], I32)
        din("wuT_i", [D, L1H], I32)
        din("wdT_i", [L1H, D], I32)
        din("gatew", [128, NCH * 7], F32)
        din("ebias", [7, 1], F32)
        din("sgv", [128, L1HC], F32)
        din("suv", [128, L1HC], F32)
        din("sdv", [128, NCH], F32)
        din("nwv", [128, NCH], F32)
        din("ropec", [128, NTT * 48], F32)
        din("ropes", [128, NTT * 48], F32)
        din("qdec_d", [HEADS, BLK], F32)
        din("kdec_d", [128, HEADS * 2], F32)
        din("ddec_d", [HEADS * 2 * 128, BLK], F32)
        din("coef_d", [128, 7 * HEADS], F32)
        din("causal_d", [128, 896], BF16)
        din("identb_d", [128, 128], BF16)
        din("identf_d", [128, 128], F32)
        din("trib_d", [128, 128], BF16)
        din("onesb_d", [128, 128], BF16)
        din("onesf_d", [1, 128], F32)
        din("onescol_d", [128, 1], F32)
        din("iota224_d", [128, SLOTS], F32)
        din("tokid_d", [128, NTT], F32)
        din("sel64_d", [64, 8], F32)
        din("fcon_d", [128, 4], F32)
        din("selmap8_d", [8, 64], F32)
        din("idxqkv_d", [128, 48], I32)
        din("idxs2_d", [128, NCH], I32)

        self.out_fm = nc.dram_tensor("out_fm", [D, TPC], F32,
                                     kind="ExternalOutput")
        self.dbg = None
        if self.dbgp is not None:
            self.dbg = nc.dram_tensor("dbg", [D, TPC], F32,
                                      kind="ExternalOutput")

        for wn, sh in [("wq_b", [D, D]), ("wkv_b", [D, LATENT]),
                       ("wku_b", [LATENT, D]), ("wvu_b", [LATENT, D]),
                       ("sw1_b", [D, D]), ("sw2_b", [1024, D]),
                       ("rw1_b", [E_R * D, D]), ("rw2_b", [E_R * 1024, D]),
                       ("wqkv_b", [D, 3 * D]), ("wg_b", [D, D]),
                       ("wout_b", [D, D])]:
            dint(wn, sh, BF16)
        dint("ag1b_in", [8, TPC], F32)
        dint("ag1b_out", [8 * NCORES, TPC], F32)
        dint("ag1a_in", [48 * 128, TPC], BF16)
        dint("ag1a_out", [48 * 128 * NCORES, TPC], BF16)
        dint("ag2_in", [2 * 128, S], BF16)
        dint("ag2_out", [2 * 128 * NCORES * S, 1], BF16)
        dint("ag3_in", [HEADS * 128, 128], F32)
        dint("ag3_out", [NCORES * HEADS * 128, 128], F32)
        dint("xn_tm", [TPC, D], BF16)
        dint("routed", [TPC, D], BF16)
        dint("cown", [HEADS * NBLK_C * 128, 128], F32)
        dint("s1_d", [D, TPC], BF16)
        dint("x_d", [128, NCH * TPC], F32)

    def build(self):
        self.declare()
        nc, dt = self.nc, self.dt
        ctx = contextlib.ExitStack()
        self._ctx = ctx
        ctx.__enter__()
        ar = Arena(nc, ctx, 212800)
        self.ar = ar
        self.sem = {n: Ctr(nc, ctx, n) for n in
                    ["ld_sy", "ld_sc", "ld_gp", "pe", "act", "dve", "st",
                     "cc1a", "cc1b", "cc2", "cc3"]}
        for qn in ("sy", "sc", "gp"):
            self.pool[qn] = [Ctr(nc, ctx, f"p_{qn}{i}") for i in range(6)]
        sem = self.sem

        P = {}
        for nm, sh, dty in [
                ("x", [128, NCH * TPC], F32), ("identb", [128, 128], BF16),
                ("identf", [128, 128], F32), ("causal", [128, 896], BF16),
                ("trib", [128, 128], BF16), ("onesb", [128, 128], BF16),
                ("onesf", [1, 128], F32), ("onescol", [128, 1], F32),
                ("iota224", [128, SLOTS], F32), ("tokid", [128, NTT], F32),
                ("sel64", [64, 8], F32), ("selmap8", [8, 64], F32),
                ("fcon", [128, 4], F32),
                ("gatew", [128, NCH * 7], F32), ("ebias", [7, 1], F32),
                ("sgv", [128, L1HC], F32), ("suv", [128, L1HC], F32),
                ("sdv", [128, NCH], F32), ("nwv", [128, NCH], F32),
                ("ropec", [128, NTT * 48], F32), ("ropes", [128, NTT * 48], F32),
                ("kdec", [128, HEADS * 2], F32), ("coef", [128, 7 * HEADS], F32),
                ("idxqkv", [128, 48], I32), ("idxs2", [128, NCH], I32),
                ("xn", [128, NCH * TPC], BF16)]:
            P[nm] = ar.alloc(nm, sh, dty)
        self.P = P

        self.pf = [ctx.enter_context(nc.psum_tensor(f"pf{i}", [128, 512], F32))
                   for i in range(6)]
        self.pb = [ctx.enter_context(nc.psum_tensor(f"pb{i}", [128, 1024], BF16))
                   for i in range(2)]

        for nm, src in [("identb", "identb_d"), ("identf", "identf_d"),
                        ("causal", "causal_d"), ("trib", "trib_d"),
                        ("onesb", "onesb_d"), ("onesf", "onesf_d"),
                        ("onescol", "onescol_d"), ("iota224", "iota224_d"),
                        ("tokid", "tokid_d"), ("sel64", "sel64_d"),
                        ("selmap8", "selmap8_d"), ("fcon", "fcon_d"), ("gatew", "gatew"),
                        ("ebias", "ebias"), ("sgv", "sgv"), ("suv", "suv"),
                        ("sdv", "sdv"), ("nwv", "nwv"), ("ropec", "ropec"),
                        ("ropes", "ropes"), ("kdec", "kdec_d"),
                        ("coef", "coef_d"), ("idxqkv", "idxqkv_d"),
                        ("idxs2", "idxs2_d")]:
            self.dma("sy", P[nm][:], dt[src][:, :])
        for c in range(0, NCH, 4):
            self.dma("sy", P["x"][:, c * TPC:(c + 4) * TPC].rearrange(
                         "p (c t) -> p c t", c=4),
                     dt["x_t"][c * 128:(c + 4) * 128, :].rearrange(
                         "(c p) t -> p c t", p=128))
        self.tok_x = None
        self.tok_x_all = self.qtok("sy")
        self.bar = self.bar + self.tok_x_all
        self.wtok = {}
        for wn, rows in [("wkv", 2048), ("wq", 2048), ("wku", 512),
                         ("wvu", 512), ("sw1", 2048), ("rw1", E_R * 2048),
                         ("rw2", E_R * 1024), ("sw2", 1024),
                         ("wqkv", 2048), ("wg", 2048), ("wout", 2048)]:
            for r0 in range(0, rows, 2048):
                r1 = min(rows, r0 + 2048)
                self.dma("gp", dt[wn + "_b"][r0:r1, :], dt[wn + "_w"][r0:r1, :])
            self.wtok[wn] = self.qtok("gp")

        for ph in [self.phase_ln0, self.phase_routing, self.phase_qkv,
                   self.phase_moe, self.phase_attn, self.phase_combine0,
                   self.phase_l1, self.phase_out]:
            self.new_phase()
            ph()

        self._seen = {e: {} for e in ENG}
        block = ctx.enter_context(nc.Block())
        qs = self.q

        @block.sync
        def _(e):
            for f in qs["sy"]:
                f(e)

        @block.scalar
        def _(e):
            assert not qs["sc"]
            for f in qs["act"]:
                f(e)

        @block.tensor
        def _(e):
            for f in qs["pe"]:
                f(e)

        @block.vector
        def _(e):
            for f in qs["dve"]:
                f(e)

        @block.gpsimd
        def _(e):
            for f in qs["gp"]:
                f(e)

        ctx.__exit__(None, None, None)
        assert ar.peak <= 212800
        return nc

    def dbg_dump(self, point, sb, dtype=BF16, waits=()):
        if self.dbgp != point:
            return waits[0] if waits else None
        last = None
        for c in range(NCH):
            last = self.dma("gp", self.dbg[c * 128:(c + 1) * 128, :],
                            sb[:, c * TPC:(c + 1) * TPC], waits=waits)
        return last

    def ln(self, src, xn_out, mu_tm, inv_tm, eps_idx, tok_src, name):
        b, ar, pf = self, self.ar, self.pf
        ar.mark()
        sq = ar.alloc(f"{name}sq", [128, 512], F32)
        xtmp = ar.alloc(f"{name}xtmp", [128, 512], F32)
        rows = ar.alloc(f"{name}rows", [1, 5 * TPC], F32)
        SUM, SSQ, MU, INV, NEGM = 0, TPC, 2 * TPC, 3 * TPC, 4 * TPC
        tok_xn = None
        tok_tp = None
        prev_bc = None
        for th in range(2):
            o = th * 512
            tsq = None
            tmm = None
            for c in range(NCH):
                xa = src[:, c * TPC + o: c * TPC + o + 512]
                tsq = b.AC(sq[:], xa, waits=[tok_src, tsq, tmm], func=ACTF.Square)
                b.MM(pf[0][0:1, :], self.P["onescol"][:], xa,
                     c == 0, c == NCH - 1, waits=[tok_src, prev_bc])
                tmm = b.MM(pf[1][0:1, :], self.P["onescol"][:], sq[:],
                           c == 0, c == NCH - 1, waits=[tsq, prev_bc])
            t = b.DV("tensor_scalar_mul", rows[:, MU + o:MU + o + 512],
                     pf[0][0:1, :], 1.0 / D, waits=[tmm])
            t = b.DV("tensor_scalar_mul", rows[:, SSQ + o:SSQ + o + 512],
                     pf[1][0:1, :], 1.0 / D, waits=[t])
            t = b.DV("tensor_mul", rows[:, SUM + o:SUM + o + 512],
                     rows[:, MU + o:MU + o + 512],
                     rows[:, MU + o:MU + o + 512], waits=[t])
            t = b.DV("tensor_sub", rows[:, SSQ + o:SSQ + o + 512],
                     rows[:, SSQ + o:SSQ + o + 512],
                     rows[:, SUM + o:SUM + o + 512], waits=[t])
            t = b.AC(rows[:, SSQ + o:SSQ + o + 512],
                     rows[:, SSQ + o:SSQ + o + 512],
                     waits=[t], func=ACTF.Sqrt,
                     bias=self.P["fcon"][0:1, eps_idx:eps_idx + 1])
            t = b.DV("reciprocal", rows[:, INV + o:INV + o + 512],
                     rows[:, SSQ + o:SSQ + o + 512], waits=[t])
            t = b.DV("scalar_tensor_tensor", rows[:, NEGM + o:NEGM + o + 512],
                     rows[:, MU + o:MU + o + 512], -1.0,
                     rows[:, INV + o:INV + o + 512], ALU.mult, ALU.mult,
                     waits=[t])
            tb1 = b.MM(pf[2][:, :], self.P["onesf"][:],
                       rows[:, INV + o:INV + o + 512], True, True, waits=[t])
            tb2 = b.MM(pf[3][:, :], self.P["onesf"][:],
                       rows[:, NEGM + o:NEGM + o + 512], True, True, waits=[t])
            last_dve = prev_bc
            for c in range(NCH):
                xa = src[:, c * TPC + o: c * TPC + o + 512]
                t1 = b.DV("tensor_mul", xtmp[:], xa, pf[2][:, :],
                          waits=[tb1, tb2, last_dve])
                last_dve = b.DV("tensor_add",
                                xn_out[:, c * TPC + o: c * TPC + o + 512],
                                xtmp[:], pf[3][:, :], waits=[t1])
            prev_bc = last_dve
            tok_xn = last_dve
            for g in range(4):
                tt = th * 4 + g
                tp = b.TP(pf[4][:, 0:1],
                          rows[:, MU + o + g * 128: MU + o + g * 128 + 128],
                          self.P["identf"][0:1, 0:1], waits=[t, tok_tp])
                tok_tp = b.AC(mu_tm[:, tt:tt + 1], pf[4][:, 0:1], waits=[tp])
                tp = b.TP(pf[4][:, 1:2],
                          rows[:, INV + o + g * 128: INV + o + g * 128 + 128],
                          self.P["identf"][0:1, 0:1], waits=[t, tok_tp])
                tok_tp = b.AC(inv_tm[:, tt:tt + 1], pf[4][:, 1:2], waits=[tp])
        ar.reset()
        return tok_xn, tok_tp

    def phase_ln0(self):
        ar = self.ar
        self.mu0 = ar.alloc("mu0", [128, NTT], F32)
        self.inv0 = ar.alloc("inv0", [128, NTT], F32)
        self.tok_xn0, self.tok_mi0 = self.ln(
            self.P["x"], self.P["xn"], self.mu0, self.inv0, 0,
            self.tok_x, "ln0")
        self.dbg_dump("xn0", self.P["xn"], waits=[self.tok_xn0])

    def phase_routing(self):
        b, ar, pf, P = self, self.ar, self.pf, self.P
        self.aff = ar.alloc("aff", [128, NTT * 7], F32)
        self.mask2 = ar.alloc("mask2", [128, NTT * 7], F32)
        srow = ar.alloc("srow", [1, 16], F32)
        csum_bc = ar.alloc("csum_bc", [128, 7], F32)
        bias_bc = ar.alloc("bias_bc", [128, 7], F32)
        sc7 = ar.alloc("sc7", [8, TPC], F32)
        g1 = ar.alloc("g1t", [128, 7], F32)
        g2 = ar.alloc("g2t", [128, 7], F32)

        t = None
        for c in range(NCH):
            t = b.MM(pf[4][0:1, 0:7], P["onescol"][:],
                     P["gatew"][:, c * 7:(c + 1) * 7], c == 0, c == NCH - 1,
                     waits=[self.tok_x])
        t = b.DV("tensor_scalar_mul", srow[:, 0:7], pf[4][0:1, 0:7], -1.0,
                 waits=[t])
        t1 = b.MM(pf[4][:, 0:7], P["onesf"][:], srow[:, 0:7], True, True,
                  waits=[t])
        tcs = b.AC(csum_bc[:], pf[4][:, 0:7], waits=[t1])
        t = b.TP(pf[4][0:1, 8:15], P["ebias"][:], P["identf"][0:7, 0:7],
                 waits=[self.tok_x, tcs])
        t = b.AC(srow[:, 8:15], pf[4][0:1, 8:15], waits=[t])
        t = b.MM(pf[4][:, 8:15], P["onesf"][:], srow[:, 8:15], True, True,
                 waits=[t])
        tbs = b.AC(bias_bc[:], pf[4][:, 8:15], waits=[t])

        tmem = b.DV("memset", sc7[:, :], -1.0)
        tok_sc = None
        for g in range(NTT):
            tmm = None
            for c in range(NCH):
                tmm = b.MM(pf[5][:, 0:7],
                           P["x"][:, c * TPC + g * 128: c * TPC + g * 128 + 128],
                           P["gatew"][:, c * 7:(c + 1) * 7],
                           c == 0, c == NCH - 1,
                           waits=[self.tok_x, tcs, tbs, tok_sc])
            t = b.DV("scalar_tensor_tensor", g1[:], csum_bc[:],
                     self.mu0[:, g:g + 1], pf[5][:, 0:7], ALU.mult, ALU.add,
                     waits=[tmm, self.tok_mi0])
            t = b.DV("scalar_tensor_tensor", g2[:], g1[:],
                     self.inv0[:, g:g + 1], bias_bc[:], ALU.mult, ALU.add,
                     waits=[t])
            t = b.AC(self.aff[:, g * 7:(g + 1) * 7], g2[:], waits=[t],
                     func=ACTF.Sigmoid)
            t = b.DV("reduce_max", g1[:, 0:1], self.aff[:, g * 7:(g + 1) * 7],
                     AX.X, waits=[t])
            t = b.DV("tensor_single_scalar", g2[:],
                     self.aff[:, g * 7:(g + 1) * 7], g1[:, 0:1], ALU.is_lt,
                     waits=[t])
            t = b.DV("tensor_scalar_add", g1[:],
                     self.aff[:, g * 7:(g + 1) * 7], 1.0, waits=[t])
            t = b.DV("tensor_mul", g1[:], g1[:], g2[:], waits=[t])
            t = b.DV("tensor_scalar_sub", g1[:], g1[:], 1.0, waits=[t])
            t = b.DV("reduce_max", g2[:, 0:1], g1[:], AX.X, waits=[t])
            t = b.DV("tensor_single_scalar", self.mask2[:, g * 7:(g + 1) * 7],
                     self.aff[:, g * 7:(g + 1) * 7], g2[:, 0:1], ALU.is_ge,
                     waits=[t])
            t = b.DV("tensor_scalar_add", g1[:],
                     self.aff[:, g * 7:(g + 1) * 7], 1.0, waits=[t])
            t = b.DV("tensor_mul", g1[:], g1[:],
                     self.mask2[:, g * 7:(g + 1) * 7], waits=[t])
            t = b.DV("tensor_scalar_sub", g1[:], g1[:], 1.0, waits=[t])
            t = b.TP(pf[5][0:7, 128:256], g1[:], P["identf"][:],
                     waits=[t, tok_sc])
            tok_sc = b.AC(sc7[0:7, g * 128:(g + 1) * 128], pf[5][0:7, 128:256],
                          waits=[t, tmem])
        tw = b.dma("sy", self.dt["ag1b_in"][:, :], sc7[:],
                   waits=[tok_sc, tmem])
        self.tok_cc1b = b.AG("cc1b", "ag1b_in", "ag1b_out", waits=[tw])

    def phase_qkv(self):
        b, ar, pf, pb, P, dt = self, self.ar, self.pf, self.pb, self.P, self.dt
        ar.mark()
        wres = ar.alloc("wres", [128, NCH * 1024], BF16)
        wldf = [ar.alloc(f"wldf{i}", [128, 1024], F32) for i in range(2)]
        ckvT = ar.alloc("ckvT", [128, NLC * TPC], BF16)
        stage_f = ar.alloc("stage_f", [128, 1024], F32)
        stage_tm = ar.alloc("stage_tm", [128, 1024], BF16)
        hstage = ar.alloc("hstage", [128, NCH * TPC], BF16)
        rt1 = ar.alloc("rt1", [128, 384], F32)
        rt2 = ar.alloc("rt2", [128, 384], F32)

        def load_res(dram, rows, c0, cw, prev_users_done):
            wt = self.wtok[dram[:-2]]
            for i in range(rows // 128):
                b.dma("sy", wres[:, i * 1024:i * 1024 + cw],
                      dt[dram][i * 128:(i + 1) * 128, c0:c0 + cw],
                      waits=[prev_users_done, wt])
            return self.qtok("sy")

        twkv = load_res("wkv_b", 2048, 0, 512, None)
        tok_ck = None
        ck_evs = []
        for lc in range(NLC):
            for th in range(2):
                pi = lc * 2 + th
                tmm = None
                for c in range(NCH):
                    tmm = b.MM(pf[pi % 6][:, :],
                               wres[:, c * 1024 + lc * 128: c * 1024 + lc * 128 + 128],
                               P["xn"][:, c * TPC + th * 512: c * TPC + th * 512 + 512],
                               c == 0, c == NCH - 1,
                               waits=[twkv, self.tok_xn0,
                                      ck_evs[pi - 6] if pi >= 6 and c == 0 else None])
                tok_ck = b.AC(ckvT[:, lc * TPC + th * 512: lc * TPC + th * 512 + 512],
                              pf[pi % 6][:, :], waits=[tmm])
                ck_evs.append(tok_ck)
        tok_ckv = tok_ck

        def qk_tm(wtok, nk, src, sec, scale, half, stage_free):
            if stage_free is None or isinstance(stage_free, tuple) and \
                    stage_free and isinstance(stage_free[0], tuple):
                extra = list(stage_free) if stage_free else []
                tok_done = None
            else:
                extra = [stage_free]
                tok_done = None
            for g in range(NTT):
                tmm = None
                for kk in range(nk):
                    for cg in range(2):
                        tmm = b.MM(pf[cg][:, :],
                                   src[:, kk * TPC + g * 128: kk * TPC + g * 128 + 128],
                                   wres[:, kk * 1024 + cg * 512:
                                        kk * 1024 + (cg + 1) * 512],
                                   kk == 0, kk == nk - 1,
                                   waits=[wtok, self.tok_xn0, tok_done] + extra)
                tcp = None
                for cg in range(2):
                    tcp = b.AC(stage_f[:, cg * 512:(cg + 1) * 512], pf[cg][:, :],
                               waits=[tmm, tcp, tok_done] + extra,
                               scale=(self.P["fcon"][:, 2:3] if scale != 1.0
                                      else 1.0))
                cb = P["ropec"][:, g * 48:(g + 1) * 48][:, None, :]\
                    .broadcast_to([128, 8, 48])
                sb_ = P["ropes"][:, g * 48:(g + 1) * 48][:, None, :]\
                    .broadcast_to([128, 8, 48])
                sfv = stage_f[:].rearrange("p (h c) -> p h c", h=8)
                stv = stage_tm[:].rearrange("p (h c) -> p h c", h=8)
                r1v = rt1[:].rearrange("p (h c) -> p h c", h=8)
                r2v = rt2[:].rearrange("p (h c) -> p h c", h=8)
                t = b.DV("tensor_mul", r1v, sfv[:, :, 32:128:2], cb,
                         waits=[tcp, tok_done] + extra)
                t = b.DV("tensor_mul", r2v, sfv[:, :, 33:128:2], sb_, waits=[t])
                t = b.DV("tensor_sub", stv[:, :, 32:128:2], r1v, r2v, waits=[t])
                t = b.DV("tensor_mul", r1v, sfv[:, :, 33:128:2], cb, waits=[t])
                t = b.DV("tensor_mul", r2v, sfv[:, :, 32:128:2], sb_, waits=[t])
                t = b.DV("tensor_add", stv[:, :, 33:128:2], r1v, r2v, waits=[t])
                t = b.DV("tensor_copy", stv[:, :, 0:32], sfv[:, :, 0:32],
                         waits=[t])
                tlast = None
                for c in range(8):
                    hc = half * 8 + c
                    tp = b.TP(pb[0][:, c * 128:(c + 1) * 128],
                              stage_tm[:, c * 128:(c + 1) * 128], P["identb"][:],
                              waits=[t, tlast])
                    tlast = b.AC(hstage[:, hc * TPC + g * 128:
                                        hc * TPC + g * 128 + 128],
                                 pb[0][:, c * 128:(c + 1) * 128], waits=[tp])
                tok_done = tlast
            return tok_done

        tw = load_res("wq_b", 2048, 0, 1024, tok_ckv)
        th0 = qk_tm(tw, NCH, P["xn"], 0, QSCALE, 0, None)
        tw = load_res("wq_b", 2048, 1024, 1024, th0)
        th1 = qk_tm(tw, NCH, P["xn"], 0, QSCALE, 1, th0)
        for c in range(NCH):
            b.dma("sy", dt["ag1a_in"][c * 128:(c + 1) * 128, :],
                  hstage[:, c * TPC:(c + 1) * TPC], waits=[th1])
        tq_all = self.qtok("sy")
        tw = load_res("wku_b", 512, 0, 1024, th1)
        th0 = qk_tm(tw, NLC, ckvT, 1, 1.0, 0, tq_all)
        tw = load_res("wku_b", 512, 1024, 1024, th0)
        th1 = qk_tm(tw, NLC, ckvT, 1, 1.0, 1, th0)
        for c in range(NCH):
            b.dma("sy", dt["ag1a_in"][(16 + c) * 128:(17 + c) * 128, :],
                  hstage[:, c * TPC:(c + 1) * TPC], waits=[th1])
        tk_all = self.qtok("sy")
        tvs = None
        v_evs = []
        for half in range(2):
            tw = load_res("wvu_b", 512, half * 1024, 1024,
                          tvs)
            for cc in range(8):
                c = half * 8 + cc
                for th in range(2):
                    gi = cc * 2 + th
                    pi = gi % 6
                    tmm = None
                    for lc in range(NLC):
                        tmm = b.MM(pf[pi][:, :],
                                   wres[:, lc * 1024 + cc * 128:
                                        lc * 1024 + cc * 128 + 128],
                                   ckvT[:, lc * TPC + th * 512:
                                        lc * TPC + th * 512 + 512],
                                   lc == 0, lc == NLC - 1,
                                   waits=[tw, tok_ckv,
                                          v_evs[-6] if len(v_evs) >= 6 and lc == 0 else None]
                                   + list(tk_all))
                    tvs = b.AC(hstage[:, c * TPC + th * 512:
                                      c * TPC + th * 512 + 512],
                               pf[pi][:, :], waits=[tmm] + list(tk_all))
                    v_evs.append(tvs)
        for c in range(NCH):
            b.dma("sy", dt["ag1a_in"][(32 + c) * 128:(33 + c) * 128, :],
                  hstage[:, c * TPC:(c + 1) * TPC], waits=[tvs])
        self.tok_cc1a = b.AG("cc1a", "ag1a_in", "ag1a_out",
                             waits=list(self.qtok("sy")))

        txm = None
        for g in range(NTT):
            tps = None
            for c in range(NCH):
                tp = b.TP(pb[1][:, (c % 8) * 128:(c % 8) * 128 + 128],
                          P["xn"][:, c * TPC + g * 128: c * TPC + g * 128 + 128],
                          P["identb"][:], waits=[self.tok_xn0, txm, tps])
                tps = b.AC(stage_tm[:, (c % 8) * 128:(c % 8) * 128 + 128],
                           pb[1][:, (c % 8) * 128:(c % 8) * 128 + 128],
                           waits=[tp, tvs, txm]) if c % 8 == 7 or True else tps
                if c % 8 == 7:
                    txm = b.dma("sy",
                                dt["xn_tm"][g * 128:(g + 1) * 128,
                                            (c - 7) * 128:(c + 1) * 128],
                                stage_tm[:, 0:1024], waits=[tps])
        self.tok_xntm = self.qtok("sy")
        ar.reset()

    def phase_moe(self):
        b, ar, pf, pb, P, dt = self, self.ar, self.pf, self.pb, self.P, self.dt
        self.gprime = ar.alloc("gprime", [128, NCH * TPC], BF16)
        ar.mark()
        score64 = ar.alloc("score64", [64, TPC], F32)
        ge64 = ar.alloc("ge64", [64, TPC], BF16)
        thc = ar.alloc("thc", [64, 1], F32)
        lo8 = ar.alloc("lo8", [8, 1], F32)
        hi8 = ar.alloc("hi8", [8, 1], F32)
        mid8 = ar.alloc("mid8", [8, 1], F32)
        m8 = ar.alloc("m8", [8, 1], F32)
        d8 = ar.alloc("d8", [8, 1], F32)
        cnt64 = ar.alloc("cnt64", [64, 1], F32)
        trow = ar.alloc("trow", [1, 8], F32)
        thbc = ar.alloc("thbc", [128, 8], F32)
        kept = ar.alloc("kept", [128, NTT * 7], F32)
        keptb = ar.alloc("keptb", [128, NTT * 7], BF16)
        affw = ar.alloc("affw", [128, NTT * 7], F32)
        rank = ar.alloc("rank", [128, NTT * 7], F32)
        st_e = ar.alloc("st_e", [128, SLOTS], F32)
        idxrow = ar.alloc("idxrow", [1, SLOTS], F32)
        wrow = ar.alloc("wrow", [1, SLOTS], F32)
        idxcol = ar.alloc("idxcol", [128, 2], I32)
        wcol = ar.alloc("wcol", [128, 2], F32)
        g7 = ar.alloc("g7m", [128, 7], F32)
        wlb = [ar.alloc(f"mwb{i}", [128, 1024], BF16) for i in range(6)]
        xg_off = ar.off
        xg = ar.alloc("xg", [128, 2048], BF16)
        xgT = ar.alloc("xgT", [128, NCH * SLOTS], BF16)
        h1 = ar.alloc("h1m", [128, 2048], BF16)
        gu = ar.alloc("gum", [128, 1024], BF16)
        guT = ar.alloc("guT", [128, 8 * SLOTS], BF16)
        oe = ar.alloc("oem", [128, 2048], BF16)
        sguT = ar.alloc("sguT", [128, 8 * TPC], BF16)
        silt = ar.alloc("silt", [128, 512], BF16)
        cstg = ar.alloc("cstg", [128, 512], F32)
        rstg = [ar.alloc("rstg0", [128, NTT * 128], BF16, at=xg_off)] * 2

        tok_sgu = None
        wfree = [None] * 6
        for c in range(8):
            tmm = None
            for d in range(NCH):
                buf = (c * NCH + d) % 6
                tl1 = b.dma("sy", wlb[buf][:, 0:128],
                            dt["sw1_b"][d * 128:(d + 1) * 128,
                                        c * 128:(c + 1) * 128],
                            waits=[wfree[buf], self.wtok["sw1"]])
                tl2 = b.dma("sy", wlb[buf][:, 128:256],
                            dt["sw1_b"][d * 128:(d + 1) * 128,
                                        (c + 8) * 128:(c + 9) * 128],
                            waits=[wfree[buf]])
                for pi in range(4):
                    cc, th = divmod(pi, 2)
                    tmm = b.MM(pf[pi][:, :],
                               wlb[buf][:, cc * 128:(cc + 1) * 128],
                               P["xn"][:, d * TPC + th * 512:
                                       d * TPC + th * 512 + 512],
                               d == 0, d == NCH - 1,
                               waits=[tl1, tl2, self.tok_xn0,
                                      tok_sgu if d == 0 else None])
                wfree[buf] = tmm
            for th in range(2):
                t1 = b.AC(silt[:], pf[2 + th][:, :], waits=[tmm, tok_sgu],
                          func=ACTF.Silu)
                tok_sgu = b.DV("tensor_mul",
                               sguT[:, c * TPC + th * 512: c * TPC + th * 512 + 512],
                               pf[th][:, :], silt[:], waits=[t1])

        self.new_phase()
        tso = b.dma("sy", score64[:], dt["ag1b_out"][:, :],
                    waits=[self.tok_cc1b])
        t = b.DV("memset", lo8[:], 0.0)
        t = b.DV("memset", hi8[:], 1.0, waits=[t])
        t = b.DV("tensor_add", mid8[:], lo8[:], hi8[:], waits=[t])
        t = b.DV("tensor_scalar_mul", mid8[:], mid8[:], 0.5, waits=[t])
        tpe_prev = None
        for it in range(NSEARCH):
            tp1 = b.MM(pf[4][0:64, 0:1], P["selmap8"][:], mid8[:], True, True,
                       waits=[t, self.tok_x, tpe_prev])
            t = b.AC(thc[:], pf[4][0:64, 0:1], waits=[tp1])
            t = b.DV("tensor_single_scalar", ge64[:], score64[:], thc[:, 0:1],
                     ALU.is_ge, waits=[t, tso])
            t = b.DV("reduce_sum", cnt64[:], ge64[:], AX.X, waits=[t])
            tp2 = b.MM(pf[4][0:8, 2:3], P["sel64"][:], cnt64[:], True, True,
                       waits=[t])
            tpe_prev = tp2
            t = b.AC(d8[:], pf[4][0:8, 2:3], waits=[tp2])
            t = b.DV("tensor_single_scalar", m8[:], d8[:], float(CAP),
                     ALU.is_ge, waits=[t])
            t = b.DV("tensor_sub", d8[:], mid8[:], lo8[:], waits=[t])
            t = b.DV("tensor_mul", d8[:], d8[:], m8[:], waits=[t])
            t = b.DV("tensor_add", lo8[:], lo8[:], d8[:], waits=[t])
            t = b.DV("tensor_sub", d8[:], hi8[:], mid8[:], waits=[t])
            t = b.DV("tensor_mul", d8[:], d8[:], m8[:], waits=[t])
            t = b.DV("tensor_add", hi8[:], mid8[:], d8[:], waits=[t])
            t = b.DV("tensor_add", mid8[:], lo8[:], hi8[:], waits=[t])
            t = b.DV("tensor_scalar_mul", mid8[:], mid8[:], 0.5, waits=[t])
        tp = b.TP(pf[4][0:1, 4:12], lo8[:], P["identf"][0:8, 0:8],
                  waits=[t, tpe_prev])
        t = b.AC(trow[:], pf[4][0:1, 4:12], waits=[tp])
        tp = b.MM(pf[4][:, 4:12], P["onesf"][:], trow[:], True, True, waits=[t])
        tthbc = b.AC(thbc[:], pf[4][:, 4:12], waits=[tp])

        tk = None
        for g in range(NTT):
            t = b.DV("tensor_tensor", g7[:], self.aff[:, g * 7:(g + 1) * 7],
                     thbc[:, 0:7], ALU.is_ge, waits=[tthbc, tk])
            t = b.DV("tensor_mul", kept[:, g * 7:(g + 1) * 7], g7[:],
                     self.mask2[:, g * 7:(g + 1) * 7], waits=[t])
            t = b.DV("tensor_mul", affw[:, g * 7:(g + 1) * 7],
                     kept[:, g * 7:(g + 1) * 7],
                     self.aff[:, g * 7:(g + 1) * 7], waits=[t])
            tk = b.DV("tensor_copy", keptb[:, g * 7:(g + 1) * 7],
                      kept[:, g * 7:(g + 1) * 7], waits=[t])
        tok_kept = tk
        trk = None
        for g in range(NTT):
            tmm = None
            for gp in range(g + 1):
                tmm = b.MM(pf[4][:, 16:23],
                           (P["trib"] if gp == g else P["onesb"])[:],
                           keptb[:, gp * 7:(gp + 1) * 7],
                           gp == 0, gp == g, waits=[tok_kept, trk])
            trk = b.AC(rank[:, g * 7:(g + 1) * 7], pf[4][:, 16:23],
                       waits=[tmm])
        tok_rank = trk

        tz = b.DV("memset", oe[:], 0.0)
        for g in range(NTT):
            b.dma("sy", dt["routed"][g * 128:(g + 1) * 128, :], oe[:],
                  waits=[tz])
        tzd_all = self.qtok("sy")

        tok_scat = None
        oe_free = tz
        for ex in range(E_R):
            self.new_phase()
            tdi = None
            tdw = None
            for g in range(NTT):
                t = b.DV("tensor_scalar", st_e[:], P["iota224"][:],
                         rank[:, g * 7 + ex: g * 7 + ex + 1], None,
                         ALU.subtract, waits=[tok_rank, tdi, tdw])
                t = b.DV("tensor_single_scalar", st_e[:], st_e[:], 0.0,
                         ALU.is_equal, waits=[t])
                t = b.DV("tensor_scalar", st_e[:], st_e[:],
                         kept[:, g * 7 + ex: g * 7 + ex + 1], None,
                         ALU.mult, waits=[t])
                tdi = b.MM(pf[4][0:1, 32:32 + SLOTS], P["tokid"][:, g:g + 1],
                           st_e[:], g == 0, g == NTT - 1, waits=[t])
                tdw = b.MM(pf[5][0:1, 0:SLOTS],
                           affw[:, g * 7 + ex: g * 7 + ex + 1],
                           st_e[:], g == 0, g == NTT - 1, waits=[t])
            t1 = b.AC(idxrow[:], pf[4][0:1, 32:32 + SLOTS], waits=[tdi])
            t2 = b.AC(wrow[:], pf[5][0:1, 0:SLOTS], waits=[tdw])
            tix = None
            for s2 in range(2):
                wdt = 128 if s2 == 0 else SLOTS - 128
                tp = b.TP(pf[4][0:wdt, 128:129],
                          idxrow[:, s2 * 128: s2 * 128 + wdt],
                          P["identf"][0:1, 0:1], waits=[t1, t2, tix])
                tix = b.DV("tensor_copy", idxcol[0:wdt, s2:s2 + 1],
                           pf[4][0:wdt, 128:129], waits=[tp])
                tp = b.TP(pf[4][0:wdt, 129:130],
                          wrow[:, s2 * 128: s2 * 128 + wdt],
                          P["identf"][0:1, 0:1], waits=[tix])
                tix = b.DV("tensor_copy", wcol[0:wdt, s2:s2 + 1],
                           pf[4][0:wdt, 129:130], waits=[tp])
            tok_xgT = None
            for s2 in range(2):
                wdt = 128 if s2 == 0 else SLOTS - 128
                tg = self.idma(
                    lambda e, s2=s2, wdt=wdt: e.indirect_dma_start(
                        out=xg[0:wdt, :], out_offset=None,
                        in_=dt["xn_tm"][:, :],
                        in_offset=IndirectOffsetOnAxis(idxcol[0:wdt, s2:s2 + 1], 0)),
                    waits=[tix, tok_scat, tok_xgT] + list(self.tok_xntm))
                for c in range(NCH):
                    tp = b.TP(pb[0][:, (c % 8) * 128:(c % 8) * 128 + wdt],
                              xg[0:wdt, c * 128:(c + 1) * 128],
                              P["identb"][0:wdt, 0:wdt], waits=[tg, tok_xgT])
                    tok_xgT = b.AC(
                        xgT[:, c * SLOTS + s2 * 128: c * SLOTS + s2 * 128 + wdt],
                        pb[0][:, (c % 8) * 128:(c % 8) * 128 + wdt], waits=[tp])
            tok_glu = None
            for s2 in range(2):
                wdt = 128 if s2 == 0 else SLOTS - 128
                tmm = None
                for d in range(NCH):
                    buf = d % 6
                    tc = b.dma("sy", wlb[buf][:],
                               dt["rw1_b"][ex * 2048 + d * 128:
                                           ex * 2048 + (d + 1) * 128, 0:1024],
                               waits=[wfree[buf], self.wtok["rw1"]])
                    for cg in range(2):
                        tmm = b.MM(pf[cg][0:wdt, :],
                                   xgT[:, d * SLOTS + s2 * 128:
                                       d * SLOTS + s2 * 128 + wdt],
                                   wlb[buf][:, cg * 512:(cg + 1) * 512],
                                   d == 0, d == NCH - 1,
                                   waits=[tc, tok_xgT,
                                          tok_glu if d == 0 else None])
                    wfree[buf] = tmm
                tev = None
                for cg in range(2):
                    tev = b.AC(h1[0:wdt, cg * 512:(cg + 1) * 512],
                               pf[cg][0:wdt, :], waits=[tmm, tok_glu])
                tmm = None
                for d in range(NCH):
                    buf = d % 6
                    tc = b.dma("sy", wlb[buf][:],
                               dt["rw1_b"][ex * 2048 + d * 128:
                                           ex * 2048 + (d + 1) * 128,
                                           1024:2048],
                               waits=[wfree[buf]])
                    for cg in range(2):
                        tmm = b.MM(pf[2 + cg][0:wdt, :],
                                   xgT[:, d * SLOTS + s2 * 128:
                                       d * SLOTS + s2 * 128 + wdt],
                                   wlb[buf][:, cg * 512:(cg + 1) * 512],
                                   d == 0, d == NCH - 1,
                                   waits=[tc, tev if d == 0 else None])
                    wfree[buf] = tmm
                for cg in range(2):
                    tev = b.AC(h1[0:wdt, 1024 + cg * 512:1024 + (cg + 1) * 512],
                               pf[2 + cg][0:wdt, :], waits=[tmm])
                t = b.AC(gu[0:wdt, :], h1[0:wdt, 1024:2048],
                         waits=[tev], func=ACTF.Silu)
                t = b.DV("tensor_mul", gu[0:wdt, :], gu[0:wdt, :],
                         h1[0:wdt, 0:1024], waits=[t])
                tok_glu = b.DV("tensor_scalar", gu[0:wdt, :], gu[0:wdt, :],
                               wcol[0:wdt, s2:s2 + 1], None, ALU.mult,
                               waits=[t])
                tgt = tok_glu
                for c in range(8):
                    tp = b.TP(pb[0][:, (c % 8) * 128:(c % 8) * 128 + wdt],
                              gu[0:wdt, c * 128:(c + 1) * 128],
                              P["identb"][0:wdt, 0:wdt], waits=[tgt])
                    tgt = b.AC(
                        guT[:, c * SLOTS + s2 * 128: c * SLOTS + s2 * 128 + wdt],
                        pb[0][:, (c % 8) * 128:(c % 8) * 128 + wdt], waits=[tp])
                tok_glu = tgt
            tok_guT = tok_glu
            for s2 in range(2):
                wdt = 128 if s2 == 0 else SLOTS - 128
                for ps in range(2):
                    c0 = ps * 2
                    tmm = None
                    for f in range(8):
                        buf = f % 6
                        tc = b.dma("sy", wlb[buf][:],
                                   dt["rw2_b"][ex * 1024 + f * 128:
                                               ex * 1024 + (f + 1) * 128,
                                               c0 * 512:(c0 + 2) * 512],
                                   waits=[wfree[buf], self.wtok["rw2"]])
                        for cg in range(2):
                            tmm = b.MM(pf[cg][0:wdt, :],
                                       guT[:, f * SLOTS + s2 * 128:
                                           f * SLOTS + s2 * 128 + wdt],
                                       wlb[buf][:, cg * 512:(cg + 1) * 512],
                                       f == 0, f == 7,
                                       waits=[tc, tok_guT,
                                              tok_scat if f == 0 else None])
                        wfree[buf] = tmm
                    tev = None
                    for cg in range(2):
                        cc = c0 + cg
                        tev = b.AC(oe[0:wdt, cc * 512:(cc + 1) * 512],
                                   pf[cg][0:wdt, :], waits=[tmm, tok_scat])
                    tok_scat = self.idma(
                        lambda e, s2=s2, wdt=wdt, c0=c0: e.indirect_dma_start(
                            out=dt["routed"][:, :],
                            out_offset=IndirectOffsetOnAxis(
                                idxcol[0:wdt, s2:s2 + 1], 0),
                            in_=oe[0:wdt, c0 * 512:(c0 + 2) * 512],
                            in_offset=None, compute_op=ALU.add,
                            element_offset=c0 * 512),
                        waits=[tev, tok_scat] + list(tzd_all))
        tok_scatter = None
        scat_all = self.qtok("gp")
        self.new_phase()

        tok_g = None
        trb_prev = None
        for c in range(NCH):
            for g in range(NTT):
                b.dma("sc", rstg[c % 2][:, g * 128:(g + 1) * 128],
                      dt["routed"][g * 128:(g + 1) * 128,
                                   c * 128:(c + 1) * 128],
                      waits=[trb_prev] + list(scat_all))
            trb_all = self.qtok("sc")
            trb_prev = None
            ttp = None
            for g in range(NTT):
                tp = b.TP(pb[1][:, g * 128:(g + 1) * 128],
                          rstg[c % 2][:, g * 128:(g + 1) * 128],
                          P["identb"][:], waits=[ttp, tok_g] + list(trb_all))
                ttp = tp
            trb_prev = ttp
            tmm = None
            for f in range(8):
                buf = (c * 8 + f) % 6
                tc = b.dma("sy", wlb[buf][:, 0:128],
                           dt["sw2_b"][f * 128:(f + 1) * 128,
                                       c * 128:(c + 1) * 128],
                           waits=[wfree[buf], self.wtok["sw2"]])
                for th in range(2):
                    tmm = b.MM(pf[4 + th][:, :],
                               wlb[buf][:, 0:128],
                               sguT[:, f * TPC + th * 512: f * TPC + th * 512 + 512],
                               f == 0, f == 7,
                               waits=[tc, tok_sgu, tok_g if f == 0 else None])
                wfree[buf] = tmm
            for th in range(2):
                trt = b.AC(silt[:], pb[1][:, th * 512:(th + 1) * 512],
                           waits=[ttp, tok_g])
                t = b.DV("tensor_add", cstg[:], pf[4 + th][:, :],
                         silt[:], waits=[tmm, trt])
                tok_g = b.AC(
                    self.gprime[:, c * TPC + th * 512: c * TPC + th * 512 + 512],
                    cstg[:], waits=[t], func=ACTF.Silu)
        self.tok_gprime = tok_g
        self.dbg_dump("gprime", self.gprime, waits=[tok_g])
        ar.reset()

    def phase_attn(self):
        b, ar, pf, pb, P, dt = self, self.ar, self.pf, self.pb, self.P, self.dt
        ar.mark()
        qfT = ar.alloc("qfT", [128, S], BF16)
        kfT = ar.alloc("kfT", [128, S], BF16)
        vfT_off = ar.off
        vfT = ar.alloc("vfT", [128, S], BF16)
        v_tm = ar.alloc("v_tm", [128, S], BF16)
        PT = [ar.alloc(f"PT{i}", [128, 512], BF16,
                       at=vfT_off + i * 1024) for i in range(2)]
        o_stage = ar.alloc("o_stage", [128, 512], BF16, at=vfT_off + 2048)
        lrow = ar.alloc("lrow", [1, 512], F32, at=vfT_off + 3072)
        linv = ar.alloc("linv", [1, 512], F32, at=vfT_off + 5120)
        binv = ar.alloc("binv", [128, 512], F32, at=vfT_off + 7168)

        ag1av = dt["ag1a_out"]
        tok_ag2in = None
        tok_prev_head = None
        for lh in range(2):
            self.new_phase()
            tg = {0: None, 1: None, 2: None}
            for sec, buf in [(0, qfT), (1, kfT), (2, vfT)]:
                for r in range(NCORES):
                    tg[sec] = self.idma(
                        lambda e, sec=sec, buf=buf, r=r, lh=lh:
                        e.indirect_dma_start(
                            out=buf[:, r * TPC:(r + 1) * TPC], out_offset=None,
                            in_=ag1av[:, :],
                            in_offset=IndirectOffsetOnAxis(
                                P["idxqkv"][:, (sec * 2 + lh) * 8 + r:(sec * 2 + lh) * 8 + r + 1], 0)),
                        waits=[self.tok_cc1a, self.tok_x, tok_prev_head])
            gath_all = self.qtok("gp")
            tvt = None
            for kt in range(64):
                tp = b.TP(pb[0][:, (kt % 8) * 128:(kt % 8) * 128 + 128],
                          vfT[:, kt * 128:(kt + 1) * 128], P["identb"][:],
                          waits=[tvt] + list(gath_all))
                tvt = b.AC(v_tm[:, kt * 128:(kt + 1) * 128],
                           pb[0][:, (kt % 8) * 128:(kt % 8) * 128 + 128],
                           waits=[tp])
            tok_norm = None
            exp_t = {}
            pv_t = {}
            for qt in range(16):
                nkc = 4 * (qt + 1)
                rhsq = qfT[:, qt * 512:(qt + 1) * 512]
                for kc in range(nkc):
                    sbank = kc % 2
                    tsc = b.MM(pf[sbank][:, :], kfT[:, kc * 128:(kc + 1) * 128],
                               rhsq, True, True,
                               waits=[exp_t.get((qt, kc - 2)), tok_norm]
                               + list(gath_all))
                    te = b.AC(PT[sbank][:], pf[sbank][:, :],
                              waits=[tsc, pv_t.get((qt, kc - 2))], func=ACTF.Exp)
                    if kc >= 4 * qt:
                        o = kc - 4 * qt
                        te = b.DV("tensor_mul", PT[sbank][:], PT[sbank][:],
                                  P["causal"][:, 384 - o * 128: 896 - o * 128],
                                  waits=[te])
                    exp_t[(qt, kc)] = te
                    b.MM(pf[2][:, :], v_tm[:, kc * 128:(kc + 1) * 128],
                         PT[sbank][:], kc == 0, kc == nkc - 1,
                         waits=[te, tvt, tok_norm])
                    pv_t[(qt, kc)] = b.MM(pf[3][0:1, :], P["onesb"][:, 0:1],
                                          PT[sbank][:], kc == 0, kc == nkc - 1,
                                          waits=[te, tok_norm if kc == 0 else None])
                t = b.AC(lrow[:], pf[3][0:1, :], waits=[pv_t[(qt, nkc - 1)]])
                t = b.DV("reciprocal", linv[:], lrow[:], waits=[t])
                tb = b.MM(pf[4][:, :], P["onesf"][:], linv[:], True, True,
                          waits=[t, tok_norm])
                tb = b.AC(binv[:], pf[4][:, :], waits=[tb])
                tok_norm = b.DV("tensor_mul", o_stage[:], pf[2][:, :],
                                binv[:], waits=[tb, tok_ag2in])
                tok_ag2in = b.dma("sy", dt["ag2_in"][lh * 128:(lh + 1) * 128,
                                                     qt * 512:(qt + 1) * 512],
                                  o_stage[:], waits=[tok_norm])
            tok_prev_head = tok_ag2in
        self.tok_cc2 = b.AG("cc2", "ag2_in", "ag2_out",
                            waits=list(self.qtok("sy")))
        ar.reset()

    def phase_combine0(self):
        b, ar, pf, P, dt = self, self.ar, self.pf, self.P, self.dt
        ar.mark()
        s_t = [ar.alloc(f"s_t{i}", [128, TPC], BF16) for i in range(2)]
        fstg = ar.alloc("fstg", [128, TPC], F32)
        tok_x1 = None
        tprev = [None, None]
        for c in range(NCH):
            buf = c % 2
            tgt = self.idma(
                lambda e, c=c, buf=buf: e.indirect_dma_start(
                    out=s_t[buf][:], out_offset=None,
                    in_=dt["ag2_out"][:, :],
                    in_offset=IndirectOffsetOnAxis(P["idxs2"][:, c:c + 1], 0)),
                waits=[self.tok_cc2, tprev[buf]])
            t = b.DV("tensor_mul", fstg[:], s_t[buf][:],
                     self.gprime[:, c * TPC:(c + 1) * TPC],
                     waits=[tgt, self.tok_gprime, tok_x1])
            tprev[buf] = t
            tok_x1 = b.DV("tensor_add", P["x"][:, c * TPC:(c + 1) * TPC],
                          P["x"][:, c * TPC:(c + 1) * TPC], fstg[:], waits=[t])
        self.tok_x1 = tok_x1
        self.dbg_dump("x1", P["x"], waits=[tok_x1])
        ar.reset()

    def phase_l1(self):
        b, ar, pf, pb, P, dt = self, self.ar, self.pf, self.pb, self.P, self.dt
        mu1 = ar.alloc("mu1", [128, NTT], F32)
        inv1 = ar.alloc("inv1", [128, NTT], F32)
        tok_xn1, _ = self.ln(P["x"], P["xn"], mu1, inv1, 0,
                             self.tok_x1, "ln1")
        self.new_phase()
        self.dbg_dump("xn1", P["xn"], waits=[tok_xn1])
        tok_xd = None
        for c in range(NCH):
            tok_xd = b.dma("sy", dt["x_d"][:, c * TPC:(c + 1) * TPC],
                           P["x"][:, c * TPC:(c + 1) * TPC],
                           waits=[self.tok_x1])
        ar.mark()
        regA = ar.alloc("regA", [128, 32 * TPC], BF16, at=0)
        gp_off = None
        def slot(j):
            if j < 32:
                return regA[:, j * TPC:(j + 1) * TPC]
            return self.gprime[:, (j - 32) * TPC:(j - 31) * TPC]

        wlb = [ar.alloc(f"l1wb{i}", [128, 384], BF16) for i in range(4)]

        wfree = [None] * 4
        tprev = None
        for cp in range(16):
            tmm = None
            for d in range(NCH):
                buf = (cp * NCH + d) % 4
                tc = b.dma("sy", wlb[buf][:, 0:384],
                           dt["wqkv_b"][d * 128:(d + 1) * 128,
                                        cp * 384:(cp + 1) * 384],
                           waits=[wfree[buf], self.wtok["wqkv"]])
                for cg in range(3):
                    for th in range(2):
                        tmm = b.MM(pf[cg * 2 + th][:, :],
                                   wlb[buf][:, cg * 128:(cg + 1) * 128],
                                   P["xn"][:, d * TPC + th * 512:
                                           d * TPC + th * 512 + 512],
                                   d == 0, d == NCH - 1,
                                   waits=[tc, tok_xn1, tok_xd,
                                          tprev if d == 0 else None])
                wfree[buf] = tmm
            for cg in range(3):
                j = cp * 3 + cg
                for th in range(2):
                    tprev = b.AC(slot(j)[:, th * 512:(th + 1) * 512],
                                 pf[cg * 2 + th][:, :], waits=[tmm],
                                 func=ACTF.Silu)
        tok_qkv = tprev

        self.new_phase()
        ktm = ar.alloc("ktm", [128, 256], BF16)
        vtm = ar.alloc("vtm", [128, 256], BF16)
        kmt = ar.alloc("kmt", [128, 256], BF16)
        cstage = ar.alloc("cstage", [128, 128], F32)
        tstage = ar.alloc("tstage", [128, 128], F32)
        slopes = _slopes(HEADS)
        tok_ag3in = None
        tcd = None
        for h in range(HEADS):
            bd = math.exp(-slopes[h] * BLK)
            tT = None
            for blk in range(NBLK_C):
                ttp = None
                for ts in range(2):
                    tp = b.TP(pb[0][:, ts * 128: ts * 128 + 128],
                              slot(3 * h + 1)[:, blk * 256 + ts * 128:
                                              blk * 256 + (ts + 1) * 128],
                              P["identb"][:], waits=[tok_qkv, ttp])
                    ttp = b.AC(ktm[:, ts * 128:(ts + 1) * 128],
                               pb[0][:, ts * 128: ts * 128 + 128], waits=[tp])
                    tp = b.TP(pb[0][:, 256 + ts * 128: 256 + ts * 128 + 128],
                              slot(3 * h + 2)[:, blk * 256 + ts * 128:
                                              blk * 256 + (ts + 1) * 128],
                              P["identb"][:], waits=[tok_qkv, ttp])
                    ttp = b.AC(vtm[:, ts * 128:(ts + 1) * 128],
                               pb[0][:, 256 + ts * 128: 256 + ts * 128 + 128],
                               waits=[tp])
                tkm = None
                for ts in range(2):
                    tkm = b.DV("tensor_scalar",
                               kmt[:, ts * 128:(ts + 1) * 128],
                               ktm[:, ts * 128:(ts + 1) * 128],
                               P["kdec"][:, h * 2 + ts: h * 2 + ts + 1], None,
                               ALU.mult, waits=[ttp, tkm])
                tmm = None
                for ts in range(2):
                    tmm = b.MM(pf[5][:, 0:128],
                               kmt[:, ts * 128:(ts + 1) * 128],
                               vtm[:, ts * 128:(ts + 1) * 128],
                               ts == 0, ts == 1, waits=[tkm, tT])
                tcs = b.AC(cstage[:], pf[5][:, 0:128], waits=[tmm, tcd])
                tcd = b.dma("sc", dt["cown"][(h * NBLK_C + blk) * 128:
                                             (h * NBLK_C + blk + 1) * 128, :],
                            cstage[:], waits=[tcs])
                if blk == 0:
                    tT = b.DV("tensor_copy", tstage[:], cstage[:],
                              waits=[tcs, tok_ag3in])
                else:
                    tT = b.DV("scalar_tensor_tensor", tstage[:], tstage[:],
                              bd, cstage[:], ALU.mult, ALU.add, waits=[tcs])
            tok_ag3in = b.dma("sc", dt["ag3_in"][h * 128:(h + 1) * 128, :],
                              tstage[:], waits=[tT, tok_ag3in])
        tok_cc3 = b.AG("cc3", "ag3_in", "ag3_out",
                       waits=list(self.qtok("sc")))

        kv = ar.alloc("kv", [128, 128], F32)
        kvb = ar.alloc("kvb", [128, 128], BF16)
        Tld = [ar.alloc(f"Tld{i}", [128, 128], F32) for i in range(2)]
        Cld = [ar.alloc(f"Cld{i}", [128, 128], F32) for i in range(2)]
        qdr = ar.alloc("qdr", [1, 256], F32)
        qm = ar.alloc("qm", [128, 256], BF16)
        dd = [ar.alloc(f"dd{i}", [128, 512], F32) for i in range(2)]
        PT = [ar.alloc(f"l1PT{i}", [128, 256], BF16) for i in range(2)]
        tqm_last = None
        for h in range(HEADS):
            self.new_phase()
            bd = math.exp(-slopes[h] * BLK)
            tkv = None
            tldp = [None, None]
            for r in range(NCORES - 1):
                tl = b.dma("sy", Tld[r % 2][:],
                           dt["ag3_out"][(r * HEADS + h) * 128:
                                         (r * HEADS + h + 1) * 128, :],
                           waits=[tok_cc3, tldp[r % 2]])
                if r == 0:
                    tkv = b.DV("tensor_scalar", kv[:], Tld[0][:],
                               P["coef"][:, 0 * HEADS + h: 0 * HEADS + h + 1],
                               None, ALU.mult, waits=[tl])
                else:
                    tkv = b.DV("scalar_tensor_tensor", kv[:], Tld[r % 2][:],
                               P["coef"][:, r * HEADS + h: r * HEADS + h + 1],
                               kv[:], ALU.mult, ALU.add, waits=[tl, tkv])
                tldp[r % 2] = tkv
            tql = b.dma("sy", qdr[:], dt["qdec_d"][h:h + 1, :],
                        waits=[tqm_last])
            tqb = b.MM(pf[4][:, 0:256], P["onesf"][:], qdr[:], True, True,
                       waits=[tql, tqm_last])
            tclp = [None, None]
            tddp = [None, None]
            tevac = None
            for blk in range(NBLK_C):
                ttp = None
                for ts in range(2):
                    tp = b.TP(pb[0][:, 512 + ts * 128: 512 + ts * 128 + 128],
                              slot(3 * h + 2)[:, blk * 256 + ts * 128:
                                              blk * 256 + (ts + 1) * 128],
                              P["identb"][:], waits=[tok_qkv, ttp])
                    ttp = b.AC(vtm[:, ts * 128:(ts + 1) * 128],
                               pb[0][:, 512 + ts * 128: 512 + ts * 128 + 128],
                               waits=[tp])
                tkvb = b.DV("tensor_copy", kvb[:], kv[:], waits=[tkv, tevac])
                tqm = b.DV("tensor_mul", qm[:],
                           slot(3 * h)[:, blk * 256:(blk + 1) * 256],
                           pf[4][:, 0:256], waits=[tqb, tok_qkv, tevac])
                tqm_last = tqm
                tnd = b.MM(pf[3][:, 0:256], kvb[:], qm[:], True, False,
                           waits=[tkvb, tqm])
                tpt = None
                for ks in range(2):
                    tdd = b.dma("sy", dd[ks][:, 0:256],
                                dt["ddec_d"][(h * 2 + ks) * 128:
                                             (h * 2 + ks + 1) * 128, :],
                                waits=[tddp[ks]])
                    tqk = b.MM(pf[ks][:, 0:256],
                               slot(3 * h + 1)[:, blk * 256 + ks * 128:
                                               blk * 256 + (ks + 1) * 128],
                               slot(3 * h)[:, blk * 256:(blk + 1) * 256],
                               True, True, waits=[tok_qkv, tpt, tevac])
                    tpt = b.DV("tensor_mul", PT[ks][:], pf[ks][:, 0:256],
                               dd[ks][:, 0:256], waits=[tqk, tdd])
                    tddp[ks] = tpt
                    b.MM(pf[3][:, 0:256], vtm[:, ks * 128:(ks + 1) * 128],
                         PT[ks][:], False, ks == 1, waits=[tpt, ttp])
                    if ks == 1:
                        tdone = (self.sem["pe"], self.sem["pe"].n)
                tevac = b.AC(slot(3 * h)[:, blk * 256:(blk + 1) * 256],
                             pf[3][:, 0:256], waits=[tdone])
                tcl = b.dma("sy", Cld[blk % 2][:],
                            dt["cown"][(h * NBLK_C + blk) * 128:
                                       (h * NBLK_C + blk + 1) * 128, :],
                            waits=[tclp[blk % 2], tok_ag3in])
                tkv = b.DV("scalar_tensor_tensor", kv[:], kv[:], bd,
                           Cld[blk % 2][:], ALU.mult, ALU.add,
                           waits=[tcl, tkvb, tnd])
                tclp[blk % 2] = tkv
            tok_scan = tevac

        self.new_phase()
        sqs = ar.alloc("sqs", [128, TPC], BF16)
        nrows = ar.alloc("nrows", [1, 2 * TPC], F32)
        rs_bc = ar.alloc("rs_bc", [128, TPC], F32)
        tvar = None
        tsq = None
        for hc in range(NCH):
            tsq = b.AC(sqs[:], slot(3 * (hc // 1) + 0)[:, 0:0]
                       if False else sqs[:], waits=[]) if False else tsq
        tvarp = [None, None]
        for hc in range(NCH):
            tsq = b.AC(sqs[:], slot(3 * hc)[:, 0:TPC],
                       waits=[tok_scan, tsq, tvarp[0], tvarp[1]],
                       func=ACTF.Square)
            for th in range(2):
                tvp = b.MM(pf[th][0:1, :], P["onesb"][:, 0:1],
                           sqs[:, th * 512:(th + 1) * 512],
                           hc == 0, hc == NCH - 1, waits=[tsq])
                tvarp[th] = tvp
        tbc = None
        for th in range(2):
            o = th * 512
            t = b.DV("tensor_scalar_mul", nrows[:, o:o + 512],
                     pf[th][0:1, :], 1.0 / D, waits=[tvarp[th]])
            t = b.AC(nrows[:, o:o + 512], nrows[:, o:o + 512],
                     waits=[t], func=ACTF.Sqrt,
                     bias=P["fcon"][0:1, 1:2])
            t = b.DV("reciprocal", nrows[:, TPC + o:TPC + o + 512],
                     nrows[:, o:o + 512], waits=[t])
            tb = b.MM(pf[2 + th][:, :], P["onesf"][:],
                      nrows[:, TPC + o:TPC + o + 512], True, True, waits=[t])
            tbc = b.AC(rs_bc[:, o:o + 512], pf[2 + th][:, :], waits=[tb])

        self.new_phase()
        sigst = ar.alloc("sigst", [128, 512], BF16)
        gtst = ar.alloc("gtst", [128, 512], F32)
        wfree = [None] * 4
        tprev = None
        for cp in range(6):
            ch0 = cp * 3
            ncg = 3 if cp < 5 else 1
            tmm = None
            for d in range(NCH):
                buf = (cp * NCH + d) % 4
                tc = b.dma("sy", wlb[buf][:, 0:ncg * 128],
                           dt["wg_b"][d * 128:(d + 1) * 128,
                                      ch0 * 128:(ch0 + ncg) * 128],
                           waits=[wfree[buf], self.wtok["wg"]])
                for cg in range(ncg):
                    for th in range(2):
                        tmm = b.MM(pf[cg * 2 + th][:, :],
                                   wlb[buf][:, cg * 128:(cg + 1) * 128],
                                   P["xn"][:, d * TPC + th * 512:
                                           d * TPC + th * 512 + 512],
                                   d == 0, d == NCH - 1,
                                   waits=[tc, tprev if d == 0 else None])
                wfree[buf] = tmm
            for cg in range(ncg):
                hc = ch0 + cg
                for th in range(2):
                    t = b.AC(sigst[:], pf[cg * 2 + th][:, :],
                             waits=[tmm, tprev], func=ACTF.Sigmoid)
                    t = b.DV("scalar_tensor_tensor", gtst[:],
                             slot(3 * hc)[:, th * 512:(th + 1) * 512],
                             P["nwv"][:, hc:hc + 1],
                             rs_bc[:, th * 512:(th + 1) * 512],
                             ALU.mult, ALU.mult, waits=[t, tbc, tok_scan])
                    tprev = b.DV("tensor_mul",
                                 slot(3 * hc)[:, th * 512:(th + 1) * 512],
                                 gtst[:], sigst[:], waits=[tprev, t])
        tok_gated = tprev

        self.new_phase()
        wfree = [None] * 4
        tprev = None
        for cp in range(6):
            ch0 = cp * 3
            ncg = 3 if cp < 5 else 1
            tmm = None
            for d in range(NCH):
                buf = (cp * NCH + d) % 4
                tc = b.dma("sy", wlb[buf][:, 0:ncg * 128],
                           dt["wout_b"][d * 128:(d + 1) * 128,
                                        ch0 * 128:(ch0 + ncg) * 128],
                           waits=[wfree[buf], self.wtok["wout"]])
                for cg in range(ncg):
                    for th in range(2):
                        tmm = b.MM(pf[cg * 2 + th][:, :],
                                   wlb[buf][:, cg * 128:(cg + 1) * 128],
                                   slot(3 * d)[:, th * 512:(th + 1) * 512],
                                   d == 0, d == NCH - 1,
                                   waits=[tc, tok_gated,
                                          tprev if d == 0 else None])
                wfree[buf] = tmm
            for cg in range(ncg):
                hc = ch0 + cg
                for th in range(2):
                    t = b.AC(sigst[:], pf[cg * 2 + th][:, :],
                             waits=[tmm, tprev])
                    tprev = b.dma("sc", dt["s1_d"][hc * 128:(hc + 1) * 128,
                                                   th * 512:(th + 1) * 512],
                                  sigst[:], waits=[t])
        tok_s1 = tprev
        self.tok_s1 = tok_s1
        self.tok_l1ffn_deps = (tok_s1, tok_xn1)
        ar.reset()

    def phase_out(self):
        b, ar, pf, pb, P, dt = self, self.ar, self.pf, self.pb, self.P, self.dt
        tok_s1, tok_xn1 = self.tok_l1ffn_deps
        ar.mark()
        gust = ar.alloc("gust", [128, L1HC * 512], BF16)
        gstg_off = ar.off
        gstg = ar.alloc("gstg", [128, 6 * 512], BF16)
        wl = [ar.alloc(f"fwl{i}", [128, 768], I32) for i in range(3)]
        wlb = [ar.alloc(f"fwb{i}", [128, 768], BF16) for i in range(3)]
        s1t = [ar.alloc("s1t0", [128, 512], BF16)] * 2
        xt = [ar.alloc("xt0", [128, 512], F32, at=gstg_off)] * 2
        g1s = ar.alloc("g1s", [128, 512], BF16)
        fst = ar.alloc("fst", [128, 512], F32, at=gstg_off + 2048)

        tok_out = None
        for th in range(2):
            self.new_phase()
            wfree = [None] * 3
            tprev = None
            for cp in range(8):
                hc0 = cp * 6
                ncg = 6 if cp < 7 else 2
                for which, wsrc, scl, fn in [
                        (0, "wgT_i", P["sgv"], ACTF.Silu),
                        (1, "wuT_i", P["suv"], ACTF.Copy)]:
                    tmm = None
                    for d in range(NCH):
                        buf = d % 3
                        tl = b.dma("sy", wl[buf][:, 0:ncg * 128],
                                   dt[wsrc][d * 128:(d + 1) * 128,
                                            hc0 * 128:(hc0 + ncg) * 128],
                                   waits=[wfree[buf]])
                        tc = b.DV("tensor_scalar_add", wlb[buf][:, 0:ncg * 128],
                                  wl[buf][:, 0:ncg * 128], -128.0, waits=[tl])
                        for cg in range(ncg):
                            tmm = b.MM(pf[cg % 6][:, :],
                                       wlb[buf][:, cg * 128:(cg + 1) * 128],
                                       P["xn"][:, d * TPC + th * 512:
                                               d * TPC + th * 512 + 512],
                                       d == 0, d == NCH - 1,
                                       waits=[tc, tok_xn1,
                                              tprev if d == 0 else None])
                        wfree[buf] = tmm
                    for cg in range(ncg):
                        hc = hc0 + cg
                        if which == 0:
                            tprev = b.AC(gstg[:, cg * 512:(cg + 1) * 512],
                                         pf[cg % 6][:, :], waits=[tmm],
                                         func=fn, scale=scl[:, hc:hc + 1])
                        else:
                            t = b.AC(g1s[:], pf[cg % 6][:, :],
                                     waits=[tmm, tprev],
                                     func=fn, scale=scl[:, hc:hc + 1])
                            tprev = b.DV("tensor_mul",
                                         gust[:, hc * 512:(hc + 1) * 512],
                                         gstg[:, cg * 512:(cg + 1) * 512],
                                         g1s[:], waits=[t])
            tok_gu = tprev
            self.new_phase()
            wfree = [None] * 3
            tprev = None
            for cp in range(6):
                ch0 = cp * 3
                ncg = 3 if cp < 5 else 1
                tmm = None
                for f in range(L1HC):
                    buf = f % 3
                    tl = b.dma("sy", wl[buf][:, 0:ncg * 128],
                               dt["wdT_i"][f * 128:(f + 1) * 128,
                                           ch0 * 128:(ch0 + ncg) * 128],
                               waits=[wfree[buf]])
                    tc = b.DV("tensor_scalar_add", wlb[buf][:, 0:ncg * 128],
                              wl[buf][:, 0:ncg * 128], -128.0, waits=[tl])
                    for cg in range(ncg):
                        tmm = b.MM(pf[cg * 2][:, :],
                                   wlb[buf][:, cg * 128:(cg + 1) * 128],
                                   gust[:, f * 512:(f + 1) * 512],
                                   f == 0, f == L1HC - 1,
                                   waits=[tc, tok_gu,
                                          tprev if f == 0 else None])
                    wfree[buf] = tmm
                for cg in range(ncg):
                    c = ch0 + cg
                    tsl = b.dma("sc", s1t[cg % 2][:],
                                dt["s1_d"][c * 128:(c + 1) * 128,
                                           th * 512:(th + 1) * 512],
                                waits=[tok_s1, tprev])
                    txl = b.dma("sc", xt[cg % 2][:],
                                dt["x_d"][:, c * TPC + th * 512:
                                          c * TPC + th * 512 + 512],
                                waits=[tprev])
                    t = b.AC(g1s[:], pf[cg * 2][:, :], waits=[tmm, tprev],
                             func=ACTF.Silu, scale=P["sdv"][:, c:c + 1])
                    t = b.DV("tensor_mul", fst[:], g1s[:], s1t[cg % 2][:],
                             waits=[t, tsl, tok_out])
                    t = b.DV("tensor_add", fst[:], fst[:], xt[cg % 2][:],
                             waits=[t, txl])
                    tprev = t
                    tok_out = self.dma(
                        "gp", self.out_fm[c * 128:(c + 1) * 128,
                                          th * 512:(th + 1) * 512],
                        fst[:], waits=[t])
        fins = [(s, s.n) for s in self.pool["gp"]]

        def fin(e, _f=tuple(fins)):
            for ws, wn in _f:
                if wn > 0:
                    e.wait_ge(ws.h, wn)
        self.q["gp"].append(fin)
        ar.reset()



_BUILD_CACHE = {}


def _host_tables(core):
    t0 = core * TPC
    pos = np.arange(t0, t0 + TPC, dtype=np.float64)
    freqs = 1.0 / THETA ** (np.arange(0, 96, 2, dtype=np.float64) / 96.0)
    emb = pos[:, None] * freqs[None, :]
    cos = np.cos(emb).astype(np.float32)
    sin = np.sin(emb).astype(np.float32)
    ropec = np.zeros((128, NTT * 48), np.float32)
    ropes = np.zeros((128, NTT * 48), np.float32)
    for g in range(NTT):
        ropec[:, g * 48:(g + 1) * 48] = cos[g * 128:(g + 1) * 128]
        ropes[:, g * 48:(g + 1) * 48] = sin[g * 128:(g + 1) * 128]
    slopes = np.array(_slopes(HEADS), np.float64)
    arr = np.arange(BLK, dtype=np.float64) + 1.0
    qdec = np.exp(-slopes[:, None] * arr[None, :]).astype(np.float32)
    kdec = np.zeros((128, HEADS * 2), np.float32)
    for h in range(HEADS):
        kd = np.exp(-slopes[h] * (BLK - arr)).astype(np.float32)
        kdec[:, h * 2] = kd[0:128]
        kdec[:, h * 2 + 1] = kd[128:256]
    ddec = np.zeros((HEADS * 2 * 128, BLK), np.float32)
    idx = arr[:, None] - arr[None, :]
    for h in range(HEADS):
        m_ = np.where(idx >= 0, np.exp(-slopes[h] * idx), 0.0)
        mT = m_.T.astype(np.float32)
        ddec[(h * 2) * 128:(h * 2 + 1) * 128] = mT[0:128]
        ddec[(h * 2 + 1) * 128:(h * 2 + 2) * 128] = mT[128:256]
    coef = np.zeros((128, 7 * HEADS), np.float32)
    for r in range(NCORES - 1):
        for h in range(HEADS):
            if r < core:
                bd = math.exp(-slopes[h] * BLK)
                coef[:, r * HEADS + h] = bd ** (NBLK_C * (core - 1 - r))
    causal = np.zeros((128, 896), np.float32)
    for p in range(128):
        causal[p, p + 384:] = 1.0
    trib = np.triu(np.ones((128, 128), np.float32), 1)
    iota224 = np.broadcast_to(np.arange(SLOTS, dtype=np.float32), (128, SLOTS)).copy()
    tokid = (np.arange(NTT, dtype=np.float32)[None, :] * 128
             + np.arange(128, dtype=np.float32)[:, None]).astype(np.float32)
    sel64 = np.zeros((64, 8), np.float32)
    sel64[np.arange(64), np.arange(64) % 8] = 1.0
    selmap8 = np.zeros((8, 64), np.float32)
    selmap8[np.arange(64) % 8, np.arange(64)] = 1.0
    idxqkv = np.zeros((128, 48), np.int32)
    for sec in range(3):
        for lh in range(2):
            hme = 2 * core + lh
            for r in range(NCORES):
                idxqkv[:, (sec * 2 + lh) * 8 + r] = \
                    (r * 48 + sec * 16 + hme) * 128 + np.arange(128)
    idxs2 = np.zeros((128, NCH), np.int32)
    for hc in range(NCH):
        idxs2[:, hc] = (hc * 128 + np.arange(128)) * S + TPC * core
    fcon = np.zeros((128, 4), np.float32)
    fcon[:, 0] = 1e-5
    fcon[:, 1] = 1e-6
    fcon[:, 2] = QSCALE
    bf = ml_dtypes.bfloat16
    return dict(
        fcon_d=fcon,
        ropec=ropec, ropes=ropes, qdec_d=qdec, kdec_d=kdec, ddec_d=ddec,
        coef_d=coef, causal_d=causal.astype(bf), identb_d=np.eye(128, dtype=bf),
        identf_d=np.eye(128, dtype=np.float32), trib_d=trib.astype(bf),
        onesb_d=np.ones((128, 128), bf), onesf_d=np.ones((1, 128), np.float32),
        onescol_d=np.ones((128, 1), np.float32), iota224_d=iota224,
        tokid_d=tokid, sel64_d=sel64, selmap8_d=selmap8,
        idxqkv_d=idxqkv, idxs2_d=idxs2)


def kernel(**inputs):
    global LAST_DEBUG
    key = _DEBUG
    if key not in _BUILD_CACHE:
        bld = Builder(debug_point=key)
        _BUILD_CACHE[key] = (bld.build(), bld)
    nc, bld = _BUILD_CACHE[key]

    f32 = np.float32
    x = np.asarray(inputs["L0_attn_x"], f32).reshape(S, D)
    gate_w = np.asarray(inputs["L0_ffn_gate_w"], f32)
    gatew = np.zeros((128, NCH * 7), f32)
    for c in range(NCH):
        gatew[:, c * 7:(c + 1) * 7] = gate_w[c * 128:(c + 1) * 128]
    ebias = np.asarray(inputs["L0_ffn_expert_bias"], f32).reshape(7, 1)
    sgv = np.ascontiguousarray(
        np.asarray(inputs["L1_ffn_s_gate"], f32).reshape(L1HC, 128).T)
    suv = np.ascontiguousarray(
        np.asarray(inputs["L1_ffn_s_up"], f32).reshape(L1HC, 128).T)
    sdv = np.ascontiguousarray(
        np.asarray(inputs["L1_ffn_s_down"], f32).reshape(NCH, 128).T)
    nwv = np.ascontiguousarray(
        np.asarray(inputs["L1_attn_norm_w"], f32).reshape(NCH, 128).T)
    shared = dict(
        wq_w=np.asarray(inputs["L0_attn_Wq"], f32),
        wkv_w=np.asarray(inputs["L0_attn_Wkv_down"], f32),
        wku_w=np.asarray(inputs["L0_attn_Wk_up"], f32),
        wvu_w=np.asarray(inputs["L0_attn_Wv_up"], f32),
        sw1_w=np.asarray(inputs["L0_ffn_shared_w1"], f32).reshape(D, 2048),
        sw2_w=np.asarray(inputs["L0_ffn_shared_w2"], f32).reshape(1024, D),
        rw1_w=np.asarray(inputs["L0_ffn_routed_w1"], f32).reshape(E_R * D, 2048),
        rw2_w=np.asarray(inputs["L0_ffn_routed_w2"], f32).reshape(E_R * 1024, D),
        wqkv_w=np.asarray(inputs["L1_attn_Wqkv"], f32),
        wg_w=np.asarray(inputs["L1_attn_Wg"], f32),
        wout_w=np.asarray(inputs["L1_attn_Wout"], f32),
        wgT_i=np.ascontiguousarray(np.asarray(inputs["L1_ffn_wq_gate"], np.int32).T),
        wuT_i=np.ascontiguousarray(np.asarray(inputs["L1_ffn_wq_up"], np.int32).T),
        wdT_i=np.ascontiguousarray(np.asarray(inputs["L1_ffn_wq_down"], np.int32).T),
        gatew=gatew, ebias=ebias, sgv=sgv, suv=suv, sdv=sdv, nwv=nwv)

    in_maps = []
    for r in range(NCORES):
        m = dict(shared)
        m["x_t"] = np.ascontiguousarray(x[r * TPC:(r + 1) * TPC].T)
        m.update(_host_tables(r))
        in_maps.append(m)
    res = run_bass_kernel_spmd(nc, in_maps, core_ids=list(range(NCORES)))
    out = np.zeros((S, D), f32)
    dbg = {}
    for r in range(NCORES):
        out[r * TPC:(r + 1) * TPC] = res.results[r]["out_fm"].T
        if _DEBUG is not None:
            dbg[r] = res.results[r]["dbg"].T
    LAST_DEBUG = dbg
    return out.reshape(1, S, D)


# version 10
# speedup vs baseline: 1.1743x; 1.0095x over previous
import math
import contextlib
import numpy as np

try:
    import concourse.bass as bass
except ImportError:
    import sys
    sys.path.insert(0, "/opt/trn_rl_repo")
    import concourse.bass as bass
import concourse.mybir as mybir
from concourse.bass_core_rust import IndirectOffsetOnAxis
from concourse.bass_utils import run_bass_kernel_spmd
import ml_dtypes

F32 = mybir.dt.float32
BF16 = mybir.dt.bfloat16
I32 = mybir.dt.int32
ACTF = mybir.ActivationFunctionType
ALU = mybir.AluOpType
AX = mybir.AxisListType

NCORES = 8
S = 8192
D = 2048
TPC = S // NCORES
NCH = D // 128
NTT = TPC // 128
HEADS = 16
HD = 128
LATENT = 512
NLC = LATENT // 128
E_R = 7
CAP = int(S / E_R)
SLOTS = 224
L1H = 5632
L1HC = L1H // 128
BLK = 256
NBLK_C = TPC // BLK
THETA = 10000.0
QSCALE = 1.0 / math.sqrt(HD)
NSEARCH = 36

_DEBUG = None
LAST_DEBUG = None

ENG = ["sy", "sc", "pe", "act", "dve", "gp"]


def _slopes(n):
    def p2(m):
        start = 2 ** (-(2 ** -(math.log2(m) - 3)))
        return [start * start ** i for i in range(m)]
    if math.log2(n).is_integer():
        return p2(n)
    c = 2 ** math.floor(math.log2(n))
    return p2(c) + _slopes(2 * c)[0::2][: n - c]


class Ctr:
    def __init__(self, nc, ctx, name):
        self.h = ctx.enter_context(nc.semaphore(name))
        self.n = 0
        self.name = name


class LazyTok:
    __slots__ = ("sem",)

    def __init__(self, sem):
        self.sem = sem


class Arena:
    def __init__(self, nc, ctx, size_bytes):
        self.nc = nc
        self.size = size_bytes
        self.big = ctx.enter_context(
            nc.sbuf_tensor("arena", [128, size_bytes], mybir.dt.uint8))
        self.base = nc.lookup_mloc(self.big).addr
        self.off = 0
        self.marks = []
        self.peak = 0
        self.cnt = 0

    def alloc(self, name, shape, dtype, at=None):
        per_part = int(np.prod(shape[1:])) * mybir.dt.size(dtype)
        per_part = (per_part + 31) & ~31
        if at is None:
            at = self.off
            self.off += per_part
        self.peak = max(self.peak, at + per_part)
        assert at + per_part <= self.size, (name, at, per_part)
        self.cnt += 1
        h = self.nc.virtual_sbuf_tensor(f"{name}_{self.cnt}", shape, dtype)
        self.nc._tensor(h.name, h.shape, h.dtype, type="SB")
        mloc = self.nc.lookup_mloc(h)
        mloc.allocated = True
        mloc.addr = self.base + at
        return h

    def mark(self):
        self.marks.append(self.off)

    def reset(self):
        self.off = self.marks.pop()


class Builder:
    def __init__(self, debug_point=None):
        self.nc = bass.Bass("TRN2", target_bir_lowering=False)
        self.dbgp = debug_point
        self.dt = {}
        self.q = {e: [] for e in ENG}
        self.bar = ()
        self.pool = {}
        self.rr = {"sy": 0, "sc": 0, "gp": 0}

    def emit(self, eng, build, waits=(), inc=None, k=1):
        eng = "act" if eng == "sc" else eng
        s = self.sem[inc] if isinstance(inc, str) else inc
        flat = []
        for w in waits:
            if w is None:
                continue
            if isinstance(w, tuple) and w and isinstance(w[0], tuple):
                flat.extend(w)
            else:
                flat.append(w)
        wl = tuple(flat) + self.bar

        def run(e, _b=build, _w=wl, _s=s, _k=k, _eng=eng, _seenref=self):
            seen = _seenref._seen[_eng]
            for (ws, wn) in _w:
                if wn > 0 and seen.get(ws.name, -1) < wn:
                    e.wait_ge(ws.h, wn)
                    seen[ws.name] = wn
            ins = _b(e)
            if _s is not None:
                ins.then_inc(_s.h, _k)
        self.q[eng].append(run)
        if s is not None:
            s.n += k
            return (s, s.n)
        return None

    def dma(self, eng, out, in_, waits=()):
        qn = eng
        s = self.pool[qn][self.rr[qn] % 6]
        self.rr[qn] += 1
        return self.emit(eng, lambda e, o=out, i=in_: e.dma_start(out=o, in_=i),
                         waits=tuple(waits) + ((s, s.n),), inc=s, k=16)

    def qtok(self, qn):
        return tuple((s, s.n) for s in self.pool[qn])

    def idma(self, build, waits=()):
        s = self.pool["gp"][self.rr["gp"] % 6]
        self.rr["gp"] += 1
        return self.emit("gp", build, waits=tuple(waits) + ((s, s.n),),
                         inc=s, k=16)

    def MM(self, psum, lhsT, rhs, start, stop, waits=(), inc=True):
        return self.emit(
            "pe", lambda e, p=psum, l=lhsT, r=rhs, s1=start, s2=stop:
            e.matmul(p, l, r, start=s1, stop=s2, skip_group_check=True),
            waits=waits, inc=("pe" if inc else None))

    def TP(self, psum, in_, ident, waits=()):
        return self.emit("pe", lambda e, p=psum, i=in_, d=ident:
                         e.transpose(p, i, d), waits=waits, inc="pe")

    def AC(self, out, in_, waits=(), func=ACTF.Copy, scale=1.0, bias=0.0):
        return self.emit("act", lambda e, o=out, i=in_, f=func, s=scale, bb=bias:
                         e.activation(o, i, f, bias=bb, scale=s),
                         waits=waits, inc="act")

    def DV(self, op, *args, waits=()):
        return self.emit("dve", lambda e, o=op, a=args: getattr(e, o)(*a),
                         waits=waits, inc="dve")

    def AG(self, name, inname, outname, waits=()):
        return self.emit("gp", lambda e, i=inname, o=outname:
                         e.collective_compute(
                             "AllGather", ALU.bypass,
                             replica_groups=[list(range(NCORES))],
                             ins=[self.dt[i].ap()], outs=[self.dt[o].ap()]),
                         waits=waits, inc=name)

    def new_phase(self):
        toks = [(self.sem[s], self.sem[s].n) for s in ("pe", "act", "dve")]
        for qn in ("sy", "sc", "gp"):
            toks += [(s, s.n) for s in self.pool[qn]]
        self.bar = tuple(toks) + getattr(self, "tok_x_all", ())

    def declare(self):
        nc, dt = self.nc, self.dt

        def din(name, shape, dtype):
            dt[name] = nc.dram_tensor(name, shape, dtype, kind="ExternalInput")

        def dint(name, shape, dtype):
            dt[name] = nc.dram_tensor(name, shape, dtype)

        din("x_t", [D, TPC], F32)
        din("wq_w", [D, D], F32)
        din("wkv_w", [D, LATENT], F32)
        din("wku_w", [LATENT, D], F32)
        din("wvu_w", [LATENT, D], F32)
        din("sw1_w", [D, D], F32)
        din("sw2_w", [1024, D], F32)
        din("rw1_w", [E_R * D, D], F32)
        din("rw2_w", [E_R * 1024, D], F32)
        din("wqkv_w", [D, 3 * D], F32)
        din("wg_w", [D, D], F32)
        din("wout_w", [D, D], F32)
        din("wgT_i", [D, L1H], I32)
        din("wuT_i", [D, L1H], I32)
        din("wdT_i", [L1H, D], I32)
        din("gatew", [128, NCH * 7], F32)
        din("ebias", [7, 1], F32)
        din("sgv", [128, L1HC], F32)
        din("suv", [128, L1HC], F32)
        din("sdv", [128, NCH], F32)
        din("nwv", [128, NCH], F32)
        din("ropec", [128, NTT * 48], F32)
        din("ropes", [128, NTT * 48], F32)
        din("qdec_d", [HEADS, BLK], F32)
        din("kdec_d", [128, HEADS * 2], F32)
        din("ddec_d", [HEADS * 2 * 128, BLK], F32)
        din("coef_d", [128, 7 * HEADS], F32)
        din("causal_d", [128, 896], BF16)
        din("identb_d", [128, 128], BF16)
        din("identf_d", [128, 128], F32)
        din("trib_d", [128, 128], BF16)
        din("onesb_d", [128, 128], BF16)
        din("onesf_d", [1, 128], F32)
        din("onescol_d", [128, 1], F32)
        din("iota224_d", [128, SLOTS], F32)
        din("tokid_d", [128, NTT], F32)
        din("sel64_d", [64, 8], F32)
        din("fcon_d", [128, 4], F32)
        din("selmap8_d", [8, 64], F32)
        din("idxqkv_d", [128, 48], I32)
        din("idxs2_d", [128, NCH], I32)

        self.out_fm = nc.dram_tensor("out_fm", [D, TPC], F32,
                                     kind="ExternalOutput")
        self.dbg = None
        if self.dbgp is not None:
            self.dbg = nc.dram_tensor("dbg", [D, TPC], F32,
                                      kind="ExternalOutput")

        for wn, sh in [("wq_b", [D, D]), ("wkv_b", [D, LATENT]),
                       ("wku_b", [LATENT, D]), ("wvu_b", [LATENT, D]),
                       ("sw1_b", [D, D]), ("sw2_b", [1024, D]),
                       ("rw1_b", [E_R * D, D]), ("rw2_b", [E_R * 1024, D]),
                       ("wqkv_b", [D, 3 * D]), ("wg_b", [D, D]),
                       ("wout_b", [D, D])]:
            dint(wn, sh, BF16)
        dint("ag1b_in", [8, TPC], F32)
        dint("ag1b_out", [8 * NCORES, TPC], F32)
        dint("ag1a_in", [48 * 128, TPC], BF16)
        dint("ag1a_out", [48 * 128 * NCORES, TPC], BF16)
        dint("ag2_in", [2 * 128, S], BF16)
        dint("ag2_out", [2 * 128 * NCORES * S, 1], BF16)
        dint("ag3_in", [HEADS * 128, 128], F32)
        dint("ag3_out", [NCORES * HEADS * 128, 128], F32)
        dint("xn_tm", [TPC, D], BF16)
        dint("routed", [TPC, D], BF16)
        dint("cown", [HEADS * NBLK_C * 128, 128], F32)
        dint("s1_d", [D, TPC], BF16)
        dint("x_d", [128, NCH * TPC], F32)

    def build(self):
        self.declare()
        nc, dt = self.nc, self.dt
        ctx = contextlib.ExitStack()
        self._ctx = ctx
        ctx.__enter__()
        ar = Arena(nc, ctx, 212800)
        self.ar = ar
        self.sem = {n: Ctr(nc, ctx, n) for n in
                    ["ld_sy", "ld_sc", "ld_gp", "pe", "act", "dve", "st",
                     "cc1a", "cc1b", "cc2", "cc3"]}
        for qn in ("sy", "sc", "gp"):
            self.pool[qn] = [Ctr(nc, ctx, f"p_{qn}{i}") for i in range(6)]
        sem = self.sem

        P = {}
        for nm, sh, dty in [
                ("x", [128, NCH * TPC], F32), ("identb", [128, 128], BF16),
                ("identf", [128, 128], F32), ("causal", [128, 896], BF16),
                ("trib", [128, 128], BF16), ("onesb", [128, 128], BF16),
                ("onesf", [1, 128], F32), ("onescol", [128, 1], F32),
                ("iota224", [128, SLOTS], F32), ("tokid", [128, NTT], F32),
                ("sel64", [64, 8], F32), ("selmap8", [8, 64], F32),
                ("fcon", [128, 4], F32),
                ("gatew", [128, NCH * 7], F32), ("ebias", [7, 1], F32),
                ("sgv", [128, L1HC], F32), ("suv", [128, L1HC], F32),
                ("sdv", [128, NCH], F32), ("nwv", [128, NCH], F32),
                ("ropec", [128, NTT * 48], F32), ("ropes", [128, NTT * 48], F32),
                ("kdec", [128, HEADS * 2], F32), ("coef", [128, 7 * HEADS], F32),
                ("idxqkv", [128, 48], I32), ("idxs2", [128, NCH], I32),
                ("xn", [128, NCH * TPC], BF16)]:
            P[nm] = ar.alloc(nm, sh, dty)
        self.P = P

        self.pf = [ctx.enter_context(nc.psum_tensor(f"pf{i}", [128, 512], F32))
                   for i in range(6)]
        self.pb = [ctx.enter_context(nc.psum_tensor(f"pb{i}", [128, 1024], BF16))
                   for i in range(2)]

        for nm, src in [("identb", "identb_d"), ("identf", "identf_d"),
                        ("causal", "causal_d"), ("trib", "trib_d"),
                        ("onesb", "onesb_d"), ("onesf", "onesf_d"),
                        ("onescol", "onescol_d"), ("iota224", "iota224_d"),
                        ("tokid", "tokid_d"), ("sel64", "sel64_d"),
                        ("selmap8", "selmap8_d"), ("fcon", "fcon_d"), ("gatew", "gatew"),
                        ("ebias", "ebias"), ("sgv", "sgv"), ("suv", "suv"),
                        ("sdv", "sdv"), ("nwv", "nwv"), ("ropec", "ropec"),
                        ("ropes", "ropes"), ("kdec", "kdec_d"),
                        ("coef", "coef_d"), ("idxqkv", "idxqkv_d"),
                        ("idxs2", "idxs2_d")]:
            self.dma("sy", P[nm][:], dt[src][:, :])
        for c in range(0, NCH, 4):
            self.dma("sy", P["x"][:, c * TPC:(c + 4) * TPC].rearrange(
                         "p (c t) -> p c t", c=4),
                     dt["x_t"][c * 128:(c + 4) * 128, :].rearrange(
                         "(c p) t -> p c t", p=128))
        self.tok_x = None
        self.tok_x_all = self.qtok("sy")
        self.bar = self.bar + self.tok_x_all
        self.wtok = {}
        for wn, rows in [("wkv", 2048), ("wq", 2048), ("wku", 512),
                         ("wvu", 512), ("sw1", 2048), ("rw1", E_R * 2048),
                         ("rw2", E_R * 1024), ("sw2", 1024),
                         ("wqkv", 2048), ("wg", 2048), ("wout", 2048)]:
            for r0 in range(0, rows, 2048):
                r1 = min(rows, r0 + 2048)
                self.dma("gp", dt[wn + "_b"][r0:r1, :], dt[wn + "_w"][r0:r1, :])
            self.wtok[wn] = self.qtok("gp")

        for ph in [self.phase_ln0, self.phase_routing, self.phase_qkv,
                   self.phase_moe, self.phase_attn, self.phase_combine0,
                   self.phase_l1, self.phase_out]:
            self.new_phase()
            ph()

        self._seen = {e: {} for e in ENG}
        block = ctx.enter_context(nc.Block())
        qs = self.q

        @block.sync
        def _(e):
            for f in qs["sy"]:
                f(e)

        @block.scalar
        def _(e):
            assert not qs["sc"]
            for f in qs["act"]:
                f(e)

        @block.tensor
        def _(e):
            for f in qs["pe"]:
                f(e)

        @block.vector
        def _(e):
            for f in qs["dve"]:
                f(e)

        @block.gpsimd
        def _(e):
            for f in qs["gp"]:
                f(e)

        ctx.__exit__(None, None, None)
        assert ar.peak <= 212800
        return nc

    def dbg_dump(self, point, sb, dtype=BF16, waits=()):
        if self.dbgp != point:
            return waits[0] if waits else None
        last = None
        for c in range(NCH):
            last = self.dma("gp", self.dbg[c * 128:(c + 1) * 128, :],
                            sb[:, c * TPC:(c + 1) * TPC], waits=waits)
        return last

    def ln(self, src, xn_out, mu_tm, inv_tm, eps_idx, tok_src, name):
        b, ar, pf = self, self.ar, self.pf
        ar.mark()
        sq = ar.alloc(f"{name}sq", [128, 512], F32)
        xtmp = ar.alloc(f"{name}xtmp", [128, 512], F32)
        rows = ar.alloc(f"{name}rows", [1, 5 * TPC], F32)
        SUM, SSQ, MU, INV, NEGM = 0, TPC, 2 * TPC, 3 * TPC, 4 * TPC
        tok_xn = None
        tok_tp = None
        prev_bc = None
        for th in range(2):
            o = th * 512
            tsq = None
            tmm = None
            for c in range(NCH):
                xa = src[:, c * TPC + o: c * TPC + o + 512]
                tsq = b.AC(sq[:], xa, waits=[tok_src, tsq, tmm], func=ACTF.Square)
                b.MM(pf[0][0:1, :], self.P["onescol"][:], xa,
                     c == 0, c == NCH - 1, waits=[tok_src, prev_bc])
                tmm = b.MM(pf[1][0:1, :], self.P["onescol"][:], sq[:],
                           c == 0, c == NCH - 1, waits=[tsq, prev_bc])
            t = b.DV("tensor_scalar_mul", rows[:, MU + o:MU + o + 512],
                     pf[0][0:1, :], 1.0 / D, waits=[tmm])
            t = b.DV("tensor_scalar_mul", rows[:, SSQ + o:SSQ + o + 512],
                     pf[1][0:1, :], 1.0 / D, waits=[t])
            t = b.DV("tensor_mul", rows[:, SUM + o:SUM + o + 512],
                     rows[:, MU + o:MU + o + 512],
                     rows[:, MU + o:MU + o + 512], waits=[t])
            t = b.DV("tensor_sub", rows[:, SSQ + o:SSQ + o + 512],
                     rows[:, SSQ + o:SSQ + o + 512],
                     rows[:, SUM + o:SUM + o + 512], waits=[t])
            t = b.AC(rows[:, SSQ + o:SSQ + o + 512],
                     rows[:, SSQ + o:SSQ + o + 512],
                     waits=[t], func=ACTF.Sqrt,
                     bias=self.P["fcon"][0:1, eps_idx:eps_idx + 1])
            t = b.DV("reciprocal", rows[:, INV + o:INV + o + 512],
                     rows[:, SSQ + o:SSQ + o + 512], waits=[t])
            t = b.DV("scalar_tensor_tensor", rows[:, NEGM + o:NEGM + o + 512],
                     rows[:, MU + o:MU + o + 512], -1.0,
                     rows[:, INV + o:INV + o + 512], ALU.mult, ALU.mult,
                     waits=[t])
            tb1 = b.MM(pf[2][:, :], self.P["onesf"][:],
                       rows[:, INV + o:INV + o + 512], True, True, waits=[t])
            tb2 = b.MM(pf[3][:, :], self.P["onesf"][:],
                       rows[:, NEGM + o:NEGM + o + 512], True, True, waits=[t])
            last_dve = prev_bc
            for c in range(NCH):
                xa = src[:, c * TPC + o: c * TPC + o + 512]
                t1 = b.DV("tensor_mul", xtmp[:], xa, pf[2][:, :],
                          waits=[tb1, tb2, last_dve])
                last_dve = b.DV("tensor_add",
                                xn_out[:, c * TPC + o: c * TPC + o + 512],
                                xtmp[:], pf[3][:, :], waits=[t1])
            prev_bc = last_dve
            tok_xn = last_dve
            for g in range(4):
                tt = th * 4 + g
                tp = b.TP(pf[4][:, 0:1],
                          rows[:, MU + o + g * 128: MU + o + g * 128 + 128],
                          self.P["identf"][0:1, 0:1], waits=[t, tok_tp])
                tok_tp = b.AC(mu_tm[:, tt:tt + 1], pf[4][:, 0:1], waits=[tp])
                tp = b.TP(pf[4][:, 1:2],
                          rows[:, INV + o + g * 128: INV + o + g * 128 + 128],
                          self.P["identf"][0:1, 0:1], waits=[t, tok_tp])
                tok_tp = b.AC(inv_tm[:, tt:tt + 1], pf[4][:, 1:2], waits=[tp])
        ar.reset()
        return tok_xn, tok_tp

    def phase_ln0(self):
        ar = self.ar
        self.mu0 = ar.alloc("mu0", [128, NTT], F32)
        self.inv0 = ar.alloc("inv0", [128, NTT], F32)
        self.tok_xn0, self.tok_mi0 = self.ln(
            self.P["x"], self.P["xn"], self.mu0, self.inv0, 0,
            self.tok_x, "ln0")
        self.dbg_dump("xn0", self.P["xn"], waits=[self.tok_xn0])

    def phase_routing(self):
        b, ar, pf, P = self, self.ar, self.pf, self.P
        self.aff = ar.alloc("aff", [128, NTT * 7], F32)
        self.mask2 = ar.alloc("mask2", [128, NTT * 7], F32)
        srow = ar.alloc("srow", [1, 16], F32)
        csum_bc = ar.alloc("csum_bc", [128, 7], F32)
        bias_bc = ar.alloc("bias_bc", [128, 7], F32)
        sc7 = ar.alloc("sc7", [8, TPC], F32)
        g1 = ar.alloc("g1t", [128, 7], F32)
        g2 = ar.alloc("g2t", [128, 7], F32)

        t = None
        for c in range(NCH):
            t = b.MM(pf[4][0:1, 0:7], P["onescol"][:],
                     P["gatew"][:, c * 7:(c + 1) * 7], c == 0, c == NCH - 1,
                     waits=[self.tok_x])
        t = b.DV("tensor_scalar_mul", srow[:, 0:7], pf[4][0:1, 0:7], -1.0,
                 waits=[t])
        t1 = b.MM(pf[4][:, 0:7], P["onesf"][:], srow[:, 0:7], True, True,
                  waits=[t])
        tcs = b.AC(csum_bc[:], pf[4][:, 0:7], waits=[t1])
        t = b.TP(pf[4][0:1, 8:15], P["ebias"][:], P["identf"][0:7, 0:7],
                 waits=[self.tok_x, tcs])
        t = b.AC(srow[:, 8:15], pf[4][0:1, 8:15], waits=[t])
        t = b.MM(pf[4][:, 8:15], P["onesf"][:], srow[:, 8:15], True, True,
                 waits=[t])
        tbs = b.AC(bias_bc[:], pf[4][:, 8:15], waits=[t])

        tmem = b.DV("memset", sc7[:, :], -1.0)
        tok_sc = None
        for g in range(NTT):
            tmm = None
            for c in range(NCH):
                tmm = b.MM(pf[5][:, 0:7],
                           P["x"][:, c * TPC + g * 128: c * TPC + g * 128 + 128],
                           P["gatew"][:, c * 7:(c + 1) * 7],
                           c == 0, c == NCH - 1,
                           waits=[self.tok_x, tcs, tbs, tok_sc])
            t = b.DV("scalar_tensor_tensor", g1[:], csum_bc[:],
                     self.mu0[:, g:g + 1], pf[5][:, 0:7], ALU.mult, ALU.add,
                     waits=[tmm, self.tok_mi0])
            t = b.DV("scalar_tensor_tensor", g2[:], g1[:],
                     self.inv0[:, g:g + 1], bias_bc[:], ALU.mult, ALU.add,
                     waits=[t])
            t = b.AC(self.aff[:, g * 7:(g + 1) * 7], g2[:], waits=[t],
                     func=ACTF.Sigmoid)
            t = b.DV("reduce_max", g1[:, 0:1], self.aff[:, g * 7:(g + 1) * 7],
                     AX.X, waits=[t])
            t = b.DV("tensor_single_scalar", g2[:],
                     self.aff[:, g * 7:(g + 1) * 7], g1[:, 0:1], ALU.is_lt,
                     waits=[t])
            t = b.DV("tensor_scalar_add", g1[:],
                     self.aff[:, g * 7:(g + 1) * 7], 1.0, waits=[t])
            t = b.DV("tensor_mul", g1[:], g1[:], g2[:], waits=[t])
            t = b.DV("tensor_scalar_sub", g1[:], g1[:], 1.0, waits=[t])
            t = b.DV("reduce_max", g2[:, 0:1], g1[:], AX.X, waits=[t])
            t = b.DV("tensor_single_scalar", self.mask2[:, g * 7:(g + 1) * 7],
                     self.aff[:, g * 7:(g + 1) * 7], g2[:, 0:1], ALU.is_ge,
                     waits=[t])
            t = b.DV("tensor_scalar_add", g1[:],
                     self.aff[:, g * 7:(g + 1) * 7], 1.0, waits=[t])
            t = b.DV("tensor_mul", g1[:], g1[:],
                     self.mask2[:, g * 7:(g + 1) * 7], waits=[t])
            t = b.DV("tensor_scalar_sub", g1[:], g1[:], 1.0, waits=[t])
            t = b.TP(pf[5][0:7, 128:256], g1[:], P["identf"][:],
                     waits=[t, tok_sc])
            tok_sc = b.AC(sc7[0:7, g * 128:(g + 1) * 128], pf[5][0:7, 128:256],
                          waits=[t, tmem])
        tw = b.dma("sy", self.dt["ag1b_in"][:, :], sc7[:],
                   waits=[tok_sc, tmem])
        self.tok_cc1b = b.AG("cc1b", "ag1b_in", "ag1b_out", waits=[tw])

    def phase_qkv(self):
        b, ar, pf, pb, P, dt = self, self.ar, self.pf, self.pb, self.P, self.dt
        ar.mark()
        wres = ar.alloc("wres", [128, NCH * 1024], BF16)
        wldf = [ar.alloc(f"wldf{i}", [128, 1024], F32) for i in range(2)]
        ckvT = ar.alloc("ckvT", [128, NLC * TPC], BF16)
        stage_f = ar.alloc("stage_f", [128, 1024], F32)
        stage_tm = ar.alloc("stage_tm", [128, 1024], BF16)
        hstage = ar.alloc("hstage", [128, NCH * TPC], BF16)
        rt1 = ar.alloc("rt1", [128, 384], F32)
        rt2 = ar.alloc("rt2", [128, 384], F32)

        def load_res(dram, rows, c0, cw, prev_users_done):
            wt = self.wtok[dram[:-2]]
            for i in range(rows // 128):
                b.dma("sy", wres[:, i * 1024:i * 1024 + cw],
                      dt[dram][i * 128:(i + 1) * 128, c0:c0 + cw],
                      waits=[prev_users_done, wt])
            return self.qtok("sy")

        twkv = load_res("wkv_b", 2048, 0, 512, None)
        tok_ck = None
        ck_evs = []
        for lc in range(NLC):
            for th in range(2):
                pi = lc * 2 + th
                tmm = None
                for c in range(NCH):
                    tmm = b.MM(pf[pi % 6][:, :],
                               wres[:, c * 1024 + lc * 128: c * 1024 + lc * 128 + 128],
                               P["xn"][:, c * TPC + th * 512: c * TPC + th * 512 + 512],
                               c == 0, c == NCH - 1,
                               waits=[twkv, self.tok_xn0,
                                      ck_evs[pi - 6] if pi >= 6 and c == 0 else None])
                tok_ck = b.AC(ckvT[:, lc * TPC + th * 512: lc * TPC + th * 512 + 512],
                              pf[pi % 6][:, :], waits=[tmm])
                ck_evs.append(tok_ck)
        tok_ckv = tok_ck

        def qk_tm(wtok, nk, src, sec, scale, half, stage_free):
            if stage_free is None or isinstance(stage_free, tuple) and \
                    stage_free and isinstance(stage_free[0], tuple):
                extra = list(stage_free) if stage_free else []
                tok_done = None
            else:
                extra = [stage_free]
                tok_done = None
            for g in range(NTT):
                tmm = None
                for kk in range(nk):
                    for cg in range(2):
                        tmm = b.MM(pf[cg][:, :],
                                   src[:, kk * TPC + g * 128: kk * TPC + g * 128 + 128],
                                   wres[:, kk * 1024 + cg * 512:
                                        kk * 1024 + (cg + 1) * 512],
                                   kk == 0, kk == nk - 1,
                                   waits=[wtok, self.tok_xn0, tok_done] + extra)
                tcp = None
                for cg in range(2):
                    tcp = b.AC(stage_f[:, cg * 512:(cg + 1) * 512], pf[cg][:, :],
                               waits=[tmm, tcp, tok_done] + extra,
                               scale=(self.P["fcon"][:, 2:3] if scale != 1.0
                                      else 1.0))
                cb = P["ropec"][:, g * 48:(g + 1) * 48][:, None, :]\
                    .broadcast_to([128, 8, 48])
                sb_ = P["ropes"][:, g * 48:(g + 1) * 48][:, None, :]\
                    .broadcast_to([128, 8, 48])
                sfv = stage_f[:].rearrange("p (h c) -> p h c", h=8)
                stv = stage_tm[:].rearrange("p (h c) -> p h c", h=8)
                r1v = rt1[:].rearrange("p (h c) -> p h c", h=8)
                r2v = rt2[:].rearrange("p (h c) -> p h c", h=8)
                t = b.DV("tensor_mul", r1v, sfv[:, :, 32:128:2], cb,
                         waits=[tcp, tok_done] + extra)
                t = b.DV("tensor_mul", r2v, sfv[:, :, 33:128:2], sb_, waits=[t])
                t = b.DV("tensor_sub", stv[:, :, 32:128:2], r1v, r2v, waits=[t])
                t = b.DV("tensor_mul", r1v, sfv[:, :, 33:128:2], cb, waits=[t])
                t = b.DV("tensor_mul", r2v, sfv[:, :, 32:128:2], sb_, waits=[t])
                t = b.DV("tensor_add", stv[:, :, 33:128:2], r1v, r2v, waits=[t])
                t = b.DV("tensor_copy", stv[:, :, 0:32], sfv[:, :, 0:32],
                         waits=[t])
                tlast = None
                for c in range(8):
                    hc = half * 8 + c
                    tp = b.TP(pb[0][:, c * 128:(c + 1) * 128],
                              stage_tm[:, c * 128:(c + 1) * 128], P["identb"][:],
                              waits=[t, tlast])
                    tlast = b.AC(hstage[:, hc * TPC + g * 128:
                                        hc * TPC + g * 128 + 128],
                                 pb[0][:, c * 128:(c + 1) * 128], waits=[tp])
                tok_done = tlast
            return tok_done

        tw = load_res("wq_b", 2048, 0, 1024, tok_ckv)
        th0 = qk_tm(tw, NCH, P["xn"], 0, QSCALE, 0, None)
        tw = load_res("wq_b", 2048, 1024, 1024, th0)
        th1 = qk_tm(tw, NCH, P["xn"], 0, QSCALE, 1, th0)
        for c in range(NCH):
            b.dma("sy", dt["ag1a_in"][c * 128:(c + 1) * 128, :],
                  hstage[:, c * TPC:(c + 1) * TPC], waits=[th1])
        tq_all = self.qtok("sy")
        tw = load_res("wku_b", 512, 0, 1024, th1)
        th0 = qk_tm(tw, NLC, ckvT, 1, 1.0, 0, tq_all)
        tw = load_res("wku_b", 512, 1024, 1024, th0)
        th1 = qk_tm(tw, NLC, ckvT, 1, 1.0, 1, th0)
        for c in range(NCH):
            b.dma("sy", dt["ag1a_in"][(16 + c) * 128:(17 + c) * 128, :],
                  hstage[:, c * TPC:(c + 1) * TPC], waits=[th1])
        tk_all = self.qtok("sy")
        tvs = None
        v_evs = []
        for half in range(2):
            tw = load_res("wvu_b", 512, half * 1024, 1024,
                          tvs)
            for cc in range(8):
                c = half * 8 + cc
                for th in range(2):
                    gi = cc * 2 + th
                    pi = gi % 6
                    tmm = None
                    for lc in range(NLC):
                        tmm = b.MM(pf[pi][:, :],
                                   wres[:, lc * 1024 + cc * 128:
                                        lc * 1024 + cc * 128 + 128],
                                   ckvT[:, lc * TPC + th * 512:
                                        lc * TPC + th * 512 + 512],
                                   lc == 0, lc == NLC - 1,
                                   waits=[tw, tok_ckv,
                                          v_evs[-6] if len(v_evs) >= 6 and lc == 0 else None]
                                   + list(tk_all))
                    tvs = b.AC(hstage[:, c * TPC + th * 512:
                                      c * TPC + th * 512 + 512],
                               pf[pi][:, :], waits=[tmm] + list(tk_all))
                    v_evs.append(tvs)
        for c in range(NCH):
            b.dma("sy", dt["ag1a_in"][(32 + c) * 128:(33 + c) * 128, :],
                  hstage[:, c * TPC:(c + 1) * TPC], waits=[tvs])
        self.tok_cc1a = b.AG("cc1a", "ag1a_in", "ag1a_out",
                             waits=list(self.qtok("sy")))

        txm = None
        for g in range(NTT):
            tps = None
            for c in range(NCH):
                tp = b.TP(pb[1][:, (c % 8) * 128:(c % 8) * 128 + 128],
                          P["xn"][:, c * TPC + g * 128: c * TPC + g * 128 + 128],
                          P["identb"][:], waits=[self.tok_xn0, txm, tps])
                tps = b.AC(stage_tm[:, (c % 8) * 128:(c % 8) * 128 + 128],
                           pb[1][:, (c % 8) * 128:(c % 8) * 128 + 128],
                           waits=[tp, tvs, txm]) if c % 8 == 7 or True else tps
                if c % 8 == 7:
                    txm = b.dma("sy",
                                dt["xn_tm"][g * 128:(g + 1) * 128,
                                            (c - 7) * 128:(c + 1) * 128],
                                stage_tm[:, 0:1024], waits=[tps])
        self.tok_xntm = self.qtok("sy")
        ar.reset()

    def phase_moe(self):
        b, ar, pf, pb, P, dt = self, self.ar, self.pf, self.pb, self.P, self.dt
        self.gprime = ar.alloc("gprime", [128, NCH * TPC], BF16)
        ar.mark()
        score64 = ar.alloc("score64", [64, TPC], F32)
        ge64 = ar.alloc("ge64", [64, TPC], BF16)
        thc = ar.alloc("thc", [64, 1], F32)
        lo8 = ar.alloc("lo8", [8, 1], F32)
        hi8 = ar.alloc("hi8", [8, 1], F32)
        mid8 = ar.alloc("mid8", [8, 1], F32)
        m8 = ar.alloc("m8", [8, 1], F32)
        d8 = ar.alloc("d8", [8, 1], F32)
        cnt64 = ar.alloc("cnt64", [64, 1], F32)
        trow = ar.alloc("trow", [1, 8], F32)
        thbc = ar.alloc("thbc", [128, 8], F32)
        kept = ar.alloc("kept", [128, NTT * 7], F32)
        keptb = ar.alloc("keptb", [128, NTT * 7], BF16)
        affw = ar.alloc("affw", [128, NTT * 7], F32)
        rank = ar.alloc("rank", [128, NTT * 7], F32)
        st_e = ar.alloc("st_e", [128, SLOTS], F32)
        idxrow = ar.alloc("idxrow", [1, SLOTS], F32)
        wrow = ar.alloc("wrow", [1, SLOTS], F32)
        idxcol = ar.alloc("idxcol", [128, 2], I32)
        wcol = ar.alloc("wcol", [128, 2], F32)
        g7 = ar.alloc("g7m", [128, 7], F32)
        wlb = [ar.alloc(f"mwb{i}", [128, 1024], BF16) for i in range(6)]
        xg_off = ar.off
        xg = ar.alloc("xg", [128, 2048], BF16)
        xgT = ar.alloc("xgT", [128, NCH * SLOTS], BF16)
        h1 = ar.alloc("h1m", [128, 2048], BF16)
        gu = ar.alloc("gum", [128, 1024], BF16)
        guT = ar.alloc("guT", [128, 8 * SLOTS], BF16)
        oe = ar.alloc("oem", [128, 2048], BF16)
        sguT = ar.alloc("sguT", [128, 8 * TPC], BF16)
        silt = ar.alloc("silt", [128, 512], BF16)
        cstg = ar.alloc("cstg", [128, 512], F32)
        rstg = [ar.alloc("rstg0", [128, NTT * 128], BF16, at=xg_off)] * 2

        tso = b.dma("sy", score64[:], dt["ag1b_out"][:, :],
                    waits=[self.tok_cc1b])
        tth = b.DV("memset", lo8[:], 0.0)
        tth = b.DV("memset", hi8[:], 1.0, waits=[tth])
        tth = b.DV("tensor_add", mid8[:], lo8[:], hi8[:], waits=[tth])
        tth = b.DV("tensor_scalar_mul", mid8[:], mid8[:], 0.5, waits=[tth])
        th_state = {"t": tth, "pe": None, "it": 0}

        def theta_iter():
            if th_state["it"] >= NSEARCH:
                return
            th_state["it"] += 1
            t = th_state["t"]
            tp1 = b.MM(pf[4][0:64, 0:1], P["selmap8"][:], mid8[:], True, True,
                       waits=[t, th_state["pe"]])
            t = b.AC(thc[:], pf[4][0:64, 0:1], waits=[tp1])
            t = b.DV("tensor_single_scalar", ge64[:], score64[:], thc[:, 0:1],
                     ALU.is_ge, waits=[t, tso])
            t = b.DV("reduce_sum", cnt64[:], ge64[:], AX.X, waits=[t])
            tp2 = b.MM(pf[4][0:8, 2:3], P["sel64"][:], cnt64[:], True, True,
                       waits=[t])
            th_state["pe"] = tp2
            t = b.AC(d8[:], pf[4][0:8, 2:3], waits=[tp2])
            t = b.DV("tensor_single_scalar", m8[:], d8[:], float(CAP),
                     ALU.is_ge, waits=[t])
            t = b.DV("tensor_sub", d8[:], mid8[:], lo8[:], waits=[t])
            t = b.DV("tensor_mul", d8[:], d8[:], m8[:], waits=[t])
            t = b.DV("tensor_add", lo8[:], lo8[:], d8[:], waits=[t])
            t = b.DV("tensor_sub", d8[:], hi8[:], mid8[:], waits=[t])
            t = b.DV("tensor_mul", d8[:], d8[:], m8[:], waits=[t])
            t = b.DV("tensor_add", hi8[:], mid8[:], d8[:], waits=[t])
            t = b.DV("tensor_add", mid8[:], lo8[:], hi8[:], waits=[t])
            t = b.DV("tensor_scalar_mul", mid8[:], mid8[:], 0.5, waits=[t])
            th_state["t"] = t

        tok_sgu = None
        wfree = [None] * 6
        for c in range(8):
            tmm = None
            for d in range(NCH):
                buf = (c * NCH + d) % 6
                tl1 = b.dma("sy", wlb[buf][:, 0:128],
                            dt["sw1_b"][d * 128:(d + 1) * 128,
                                        c * 128:(c + 1) * 128],
                            waits=[wfree[buf], self.wtok["sw1"]])
                tl2 = b.dma("sy", wlb[buf][:, 128:256],
                            dt["sw1_b"][d * 128:(d + 1) * 128,
                                        (c + 8) * 128:(c + 9) * 128],
                            waits=[wfree[buf]])
                for pi in range(4):
                    cc, th = divmod(pi, 2)
                    tmm = b.MM(pf[pi][:, :],
                               wlb[buf][:, cc * 128:(cc + 1) * 128],
                               P["xn"][:, d * TPC + th * 512:
                                       d * TPC + th * 512 + 512],
                               d == 0, d == NCH - 1,
                               waits=[tl1, tl2, self.tok_xn0,
                                      tok_sgu if d == 0 else None])
                wfree[buf] = tmm
                if (c * NCH + d) % 3 == 2:
                    theta_iter()
            for th in range(2):
                t1 = b.AC(silt[:], pf[2 + th][:, :], waits=[tmm, tok_sgu],
                          func=ACTF.Silu)
                tok_sgu = b.DV("tensor_mul",
                               sguT[:, c * TPC + th * 512: c * TPC + th * 512 + 512],
                               pf[th][:, :], silt[:], waits=[t1])
        while th_state["it"] < NSEARCH:
            theta_iter()

        self.new_phase()
        tp = b.TP(pf[4][0:1, 4:12], lo8[:], P["identf"][0:8, 0:8],
                  waits=[th_state["t"], th_state["pe"]])
        t = b.AC(trow[:], pf[4][0:1, 4:12], waits=[tp])
        tp = b.MM(pf[4][:, 4:12], P["onesf"][:], trow[:], True, True, waits=[t])
        tthbc = b.AC(thbc[:], pf[4][:, 4:12], waits=[tp])

        tk = None
        for g in range(NTT):
            t = b.DV("tensor_tensor", g7[:], self.aff[:, g * 7:(g + 1) * 7],
                     thbc[:, 0:7], ALU.is_ge, waits=[tthbc, tk])
            t = b.DV("tensor_mul", kept[:, g * 7:(g + 1) * 7], g7[:],
                     self.mask2[:, g * 7:(g + 1) * 7], waits=[t])
            t = b.DV("tensor_mul", affw[:, g * 7:(g + 1) * 7],
                     kept[:, g * 7:(g + 1) * 7],
                     self.aff[:, g * 7:(g + 1) * 7], waits=[t])
            tk = b.DV("tensor_copy", keptb[:, g * 7:(g + 1) * 7],
                      kept[:, g * 7:(g + 1) * 7], waits=[t])
        tok_kept = tk
        trk = None
        for g in range(NTT):
            tmm = None
            for gp in range(g + 1):
                tmm = b.MM(pf[4][:, 16:23],
                           (P["trib"] if gp == g else P["onesb"])[:],
                           keptb[:, gp * 7:(gp + 1) * 7],
                           gp == 0, gp == g, waits=[tok_kept, trk])
            trk = b.AC(rank[:, g * 7:(g + 1) * 7], pf[4][:, 16:23],
                       waits=[tmm])
        tok_rank = trk

        tz = b.DV("memset", oe[:], 0.0)
        for g in range(NTT):
            b.dma("sy", dt["routed"][g * 128:(g + 1) * 128, :], oe[:],
                  waits=[tz])
        tzd_all = self.qtok("sy")

        tok_scat = None
        oe_free = tz
        for ex in range(E_R):
            self.new_phase()
            tdi = None
            tdw = None
            for g in range(NTT):
                t = b.DV("tensor_scalar", st_e[:], P["iota224"][:],
                         rank[:, g * 7 + ex: g * 7 + ex + 1], None,
                         ALU.subtract, waits=[tok_rank, tdi, tdw])
                t = b.DV("tensor_single_scalar", st_e[:], st_e[:], 0.0,
                         ALU.is_equal, waits=[t])
                t = b.DV("tensor_scalar", st_e[:], st_e[:],
                         kept[:, g * 7 + ex: g * 7 + ex + 1], None,
                         ALU.mult, waits=[t])
                tdi = b.MM(pf[4][0:1, 32:32 + SLOTS], P["tokid"][:, g:g + 1],
                           st_e[:], g == 0, g == NTT - 1, waits=[t])
                tdw = b.MM(pf[5][0:1, 0:SLOTS],
                           affw[:, g * 7 + ex: g * 7 + ex + 1],
                           st_e[:], g == 0, g == NTT - 1, waits=[t])
            t1 = b.AC(idxrow[:], pf[4][0:1, 32:32 + SLOTS], waits=[tdi])
            t2 = b.AC(wrow[:], pf[5][0:1, 0:SLOTS], waits=[tdw])
            tix = None
            for s2 in range(2):
                wdt = 128 if s2 == 0 else SLOTS - 128
                tp = b.TP(pf[4][0:wdt, 128:129],
                          idxrow[:, s2 * 128: s2 * 128 + wdt],
                          P["identf"][0:1, 0:1], waits=[t1, t2, tix])
                tix = b.DV("tensor_copy", idxcol[0:wdt, s2:s2 + 1],
                           pf[4][0:wdt, 128:129], waits=[tp])
                tp = b.TP(pf[4][0:wdt, 129:130],
                          wrow[:, s2 * 128: s2 * 128 + wdt],
                          P["identf"][0:1, 0:1], waits=[tix])
                tix = b.DV("tensor_copy", wcol[0:wdt, s2:s2 + 1],
                           pf[4][0:wdt, 129:130], waits=[tp])
            tok_xgT = None
            for s2 in range(2):
                wdt = 128 if s2 == 0 else SLOTS - 128
                tg = self.idma(
                    lambda e, s2=s2, wdt=wdt: e.indirect_dma_start(
                        out=xg[0:wdt, :], out_offset=None,
                        in_=dt["xn_tm"][:, :],
                        in_offset=IndirectOffsetOnAxis(idxcol[0:wdt, s2:s2 + 1], 0)),
                    waits=[tix, tok_scat, tok_xgT] + list(self.tok_xntm))
                for c in range(NCH):
                    tp = b.TP(pb[0][:, (c % 8) * 128:(c % 8) * 128 + wdt],
                              xg[0:wdt, c * 128:(c + 1) * 128],
                              P["identb"][0:wdt, 0:wdt], waits=[tg, tok_xgT])
                    tok_xgT = b.AC(
                        xgT[:, c * SLOTS + s2 * 128: c * SLOTS + s2 * 128 + wdt],
                        pb[0][:, (c % 8) * 128:(c % 8) * 128 + wdt], waits=[tp])
            tok_glu = None
            for s2 in range(2):
                wdt = 128 if s2 == 0 else SLOTS - 128
                tmm = None
                for d in range(NCH):
                    buf = d % 6
                    tc = b.dma("sy", wlb[buf][:],
                               dt["rw1_b"][ex * 2048 + d * 128:
                                           ex * 2048 + (d + 1) * 128, 0:1024],
                               waits=[wfree[buf], self.wtok["rw1"]])
                    for cg in range(2):
                        tmm = b.MM(pf[cg][0:wdt, :],
                                   xgT[:, d * SLOTS + s2 * 128:
                                       d * SLOTS + s2 * 128 + wdt],
                                   wlb[buf][:, cg * 512:(cg + 1) * 512],
                                   d == 0, d == NCH - 1,
                                   waits=[tc, tok_xgT,
                                          tok_glu if d == 0 else None])
                    wfree[buf] = tmm
                tev = None
                for cg in range(2):
                    tev = b.AC(h1[0:wdt, cg * 512:(cg + 1) * 512],
                               pf[cg][0:wdt, :], waits=[tmm, tok_glu])
                tmm = None
                for d in range(NCH):
                    buf = d % 6
                    tc = b.dma("sy", wlb[buf][:],
                               dt["rw1_b"][ex * 2048 + d * 128:
                                           ex * 2048 + (d + 1) * 128,
                                           1024:2048],
                               waits=[wfree[buf]])
                    for cg in range(2):
                        tmm = b.MM(pf[2 + cg][0:wdt, :],
                                   xgT[:, d * SLOTS + s2 * 128:
                                       d * SLOTS + s2 * 128 + wdt],
                                   wlb[buf][:, cg * 512:(cg + 1) * 512],
                                   d == 0, d == NCH - 1,
                                   waits=[tc, tev if d == 0 else None])
                    wfree[buf] = tmm
                for cg in range(2):
                    tev = b.AC(h1[0:wdt, 1024 + cg * 512:1024 + (cg + 1) * 512],
                               pf[2 + cg][0:wdt, :], waits=[tmm])
                t = b.AC(gu[0:wdt, :], h1[0:wdt, 1024:2048],
                         waits=[tev], func=ACTF.Silu)
                t = b.DV("tensor_mul", gu[0:wdt, :], gu[0:wdt, :],
                         h1[0:wdt, 0:1024], waits=[t])
                tok_glu = b.DV("tensor_scalar", gu[0:wdt, :], gu[0:wdt, :],
                               wcol[0:wdt, s2:s2 + 1], None, ALU.mult,
                               waits=[t])
                tgt = tok_glu
                for c in range(8):
                    tp = b.TP(pb[0][:, (c % 8) * 128:(c % 8) * 128 + wdt],
                              gu[0:wdt, c * 128:(c + 1) * 128],
                              P["identb"][0:wdt, 0:wdt], waits=[tgt])
                    tgt = b.AC(
                        guT[:, c * SLOTS + s2 * 128: c * SLOTS + s2 * 128 + wdt],
                        pb[0][:, (c % 8) * 128:(c % 8) * 128 + wdt], waits=[tp])
                tok_glu = tgt
            tok_guT = tok_glu
            for s2 in range(2):
                wdt = 128 if s2 == 0 else SLOTS - 128
                for ps in range(2):
                    c0 = ps * 2
                    tmm = None
                    for f in range(8):
                        buf = f % 6
                        tc = b.dma("sy", wlb[buf][:],
                                   dt["rw2_b"][ex * 1024 + f * 128:
                                               ex * 1024 + (f + 1) * 128,
                                               c0 * 512:(c0 + 2) * 512],
                                   waits=[wfree[buf], self.wtok["rw2"]])
                        for cg in range(2):
                            tmm = b.MM(pf[cg][0:wdt, :],
                                       guT[:, f * SLOTS + s2 * 128:
                                           f * SLOTS + s2 * 128 + wdt],
                                       wlb[buf][:, cg * 512:(cg + 1) * 512],
                                       f == 0, f == 7,
                                       waits=[tc, tok_guT,
                                              tok_scat if f == 0 else None])
                        wfree[buf] = tmm
                    tev = None
                    for cg in range(2):
                        cc = c0 + cg
                        tev = b.AC(oe[0:wdt, cc * 512:(cc + 1) * 512],
                                   pf[cg][0:wdt, :], waits=[tmm, tok_scat])
                    tok_scat = self.idma(
                        lambda e, s2=s2, wdt=wdt, c0=c0: e.indirect_dma_start(
                            out=dt["routed"][:, :],
                            out_offset=IndirectOffsetOnAxis(
                                idxcol[0:wdt, s2:s2 + 1], 0),
                            in_=oe[0:wdt, c0 * 512:(c0 + 2) * 512],
                            in_offset=None, compute_op=ALU.add,
                            element_offset=c0 * 512),
                        waits=[tev, tok_scat] + list(tzd_all))
        tok_scatter = None
        scat_all = self.qtok("gp")
        self.new_phase()

        tok_g = None
        trb_prev = None
        for c in range(NCH):
            for g in range(NTT):
                b.dma("sc", rstg[c % 2][:, g * 128:(g + 1) * 128],
                      dt["routed"][g * 128:(g + 1) * 128,
                                   c * 128:(c + 1) * 128],
                      waits=[trb_prev] + list(scat_all))
            trb_all = self.qtok("sc")
            trb_prev = None
            ttp = None
            for g in range(NTT):
                tp = b.TP(pb[1][:, g * 128:(g + 1) * 128],
                          rstg[c % 2][:, g * 128:(g + 1) * 128],
                          P["identb"][:], waits=[ttp, tok_g] + list(trb_all))
                ttp = tp
            trb_prev = ttp
            tmm = None
            for f in range(8):
                buf = (c * 8 + f) % 6
                tc = b.dma("sy", wlb[buf][:, 0:128],
                           dt["sw2_b"][f * 128:(f + 1) * 128,
                                       c * 128:(c + 1) * 128],
                           waits=[wfree[buf], self.wtok["sw2"]])
                for th in range(2):
                    tmm = b.MM(pf[4 + th][:, :],
                               wlb[buf][:, 0:128],
                               sguT[:, f * TPC + th * 512: f * TPC + th * 512 + 512],
                               f == 0, f == 7,
                               waits=[tc, tok_sgu, tok_g if f == 0 else None])
                wfree[buf] = tmm
            for th in range(2):
                trt = b.AC(silt[:], pb[1][:, th * 512:(th + 1) * 512],
                           waits=[ttp, tok_g])
                t = b.DV("tensor_add", cstg[:], pf[4 + th][:, :],
                         silt[:], waits=[tmm, trt])
                tok_g = b.AC(
                    self.gprime[:, c * TPC + th * 512: c * TPC + th * 512 + 512],
                    cstg[:], waits=[t], func=ACTF.Silu)
        self.tok_gprime = tok_g
        self.dbg_dump("gprime", self.gprime, waits=[tok_g])
        ar.reset()

    def phase_attn(self):
        b, ar, pf, pb, P, dt = self, self.ar, self.pf, self.pb, self.P, self.dt
        ar.mark()
        qfT = ar.alloc("qfT", [128, S], BF16)
        kfT = ar.alloc("kfT", [128, S], BF16)
        vfT_off = ar.off
        vfT = ar.alloc("vfT", [128, S], BF16)
        v_tm = ar.alloc("v_tm", [128, S], BF16)
        PT = [ar.alloc(f"PT{i}", [128, 512], BF16,
                       at=vfT_off + i * 1024) for i in range(2)]
        o_stage = ar.alloc("o_stage", [128, 512], BF16, at=vfT_off + 2048)
        lrow = ar.alloc("lrow", [1, 512], F32, at=vfT_off + 3072)
        linv = ar.alloc("linv", [1, 512], F32, at=vfT_off + 5120)
        binv = ar.alloc("binv", [128, 512], F32, at=vfT_off + 7168)

        ag1av = dt["ag1a_out"]
        tok_ag2in = None
        tok_prev_head = None
        for lh in range(2):
            self.new_phase()
            tg = {0: None, 1: None, 2: None}
            for sec, buf in [(0, qfT), (1, kfT), (2, vfT)]:
                for r in range(NCORES):
                    tg[sec] = self.idma(
                        lambda e, sec=sec, buf=buf, r=r, lh=lh:
                        e.indirect_dma_start(
                            out=buf[:, r * TPC:(r + 1) * TPC], out_offset=None,
                            in_=ag1av[:, :],
                            in_offset=IndirectOffsetOnAxis(
                                P["idxqkv"][:, (sec * 2 + lh) * 8 + r:(sec * 2 + lh) * 8 + r + 1], 0)),
                        waits=[self.tok_cc1a, self.tok_x, tok_prev_head])
            gath_all = self.qtok("gp")
            tvt = None
            for kt in range(64):
                tp = b.TP(pb[0][:, (kt % 8) * 128:(kt % 8) * 128 + 128],
                          vfT[:, kt * 128:(kt + 1) * 128], P["identb"][:],
                          waits=[tvt] + list(gath_all))
                tvt = b.AC(v_tm[:, kt * 128:(kt + 1) * 128],
                           pb[0][:, (kt % 8) * 128:(kt % 8) * 128 + 128],
                           waits=[tp])
            tok_norm = None
            exp_t = {}
            pv_t = {}
            for qt in range(16):
                nkc = 4 * (qt + 1)
                rhsq = qfT[:, qt * 512:(qt + 1) * 512]
                for kc in range(nkc):
                    sbank = kc % 2
                    tsc = b.MM(pf[sbank][:, :], kfT[:, kc * 128:(kc + 1) * 128],
                               rhsq, True, True,
                               waits=[exp_t.get((qt, kc - 2)), tok_norm]
                               + list(gath_all))
                    te = b.AC(PT[sbank][:], pf[sbank][:, :],
                              waits=[tsc, pv_t.get((qt, kc - 2))], func=ACTF.Exp)
                    if kc >= 4 * qt:
                        o = kc - 4 * qt
                        te = b.DV("tensor_mul", PT[sbank][:], PT[sbank][:],
                                  P["causal"][:, 384 - o * 128: 896 - o * 128],
                                  waits=[te])
                    exp_t[(qt, kc)] = te
                    b.MM(pf[2][:, :], v_tm[:, kc * 128:(kc + 1) * 128],
                         PT[sbank][:], kc == 0, kc == nkc - 1,
                         waits=[te, tvt, tok_norm])
                    pv_t[(qt, kc)] = b.MM(pf[3][0:1, :], P["onesb"][:, 0:1],
                                          PT[sbank][:], kc == 0, kc == nkc - 1,
                                          waits=[te, tok_norm if kc == 0 else None])
                t = b.AC(lrow[:], pf[3][0:1, :], waits=[pv_t[(qt, nkc - 1)]])
                t = b.DV("reciprocal", linv[:], lrow[:], waits=[t])
                tb = b.MM(pf[4][:, :], P["onesf"][:], linv[:], True, True,
                          waits=[t, tok_norm])
                tb = b.AC(binv[:], pf[4][:, :], waits=[tb])
                tok_norm = b.DV("tensor_mul", o_stage[:], pf[2][:, :],
                                binv[:], waits=[tb, tok_ag2in])
                tok_ag2in = b.dma("sy", dt["ag2_in"][lh * 128:(lh + 1) * 128,
                                                     qt * 512:(qt + 1) * 512],
                                  o_stage[:], waits=[tok_norm])
            tok_prev_head = tok_ag2in
        self.tok_cc2 = b.AG("cc2", "ag2_in", "ag2_out",
                            waits=list(self.qtok("sy")))
        ar.reset()

    def phase_combine0(self):
        b, ar, pf, P, dt = self, self.ar, self.pf, self.P, self.dt
        ar.mark()
        s_t = [ar.alloc(f"s_t{i}", [128, TPC], BF16) for i in range(2)]
        fstg = ar.alloc("fstg", [128, TPC], F32)
        tok_x1 = None
        tprev = [None, None]
        for c in range(NCH):
            buf = c % 2
            tgt = self.idma(
                lambda e, c=c, buf=buf: e.indirect_dma_start(
                    out=s_t[buf][:], out_offset=None,
                    in_=dt["ag2_out"][:, :],
                    in_offset=IndirectOffsetOnAxis(P["idxs2"][:, c:c + 1], 0)),
                waits=[self.tok_cc2, tprev[buf]])
            t = b.DV("tensor_mul", fstg[:], s_t[buf][:],
                     self.gprime[:, c * TPC:(c + 1) * TPC],
                     waits=[tgt, self.tok_gprime, tok_x1])
            tprev[buf] = t
            tok_x1 = b.DV("tensor_add", P["x"][:, c * TPC:(c + 1) * TPC],
                          P["x"][:, c * TPC:(c + 1) * TPC], fstg[:], waits=[t])
        self.tok_x1 = tok_x1
        self.dbg_dump("x1", P["x"], waits=[tok_x1])
        ar.reset()

    def phase_l1(self):
        b, ar, pf, pb, P, dt = self, self.ar, self.pf, self.pb, self.P, self.dt
        mu1 = ar.alloc("mu1", [128, NTT], F32)
        inv1 = ar.alloc("inv1", [128, NTT], F32)
        tok_xn1, _ = self.ln(P["x"], P["xn"], mu1, inv1, 0,
                             self.tok_x1, "ln1")
        self.new_phase()
        self.dbg_dump("xn1", P["xn"], waits=[tok_xn1])
        tok_xd = None
        for c in range(NCH):
            tok_xd = b.dma("sy", dt["x_d"][:, c * TPC:(c + 1) * TPC],
                           P["x"][:, c * TPC:(c + 1) * TPC],
                           waits=[self.tok_x1])
        ar.mark()
        regA = ar.alloc("regA", [128, 32 * TPC], BF16, at=0)
        gp_off = None
        def slot(j):
            if j < 32:
                return regA[:, j * TPC:(j + 1) * TPC]
            return self.gprime[:, (j - 32) * TPC:(j - 31) * TPC]

        wlb = [ar.alloc(f"l1wb{i}", [128, 384], BF16) for i in range(4)]

        wfree = [None] * 4
        tprev = None
        for cp in range(16):
            tmm = None
            for d in range(NCH):
                buf = (cp * NCH + d) % 4
                tc = b.dma("sy", wlb[buf][:, 0:384],
                           dt["wqkv_b"][d * 128:(d + 1) * 128,
                                        cp * 384:(cp + 1) * 384],
                           waits=[wfree[buf], self.wtok["wqkv"]])
                for cg in range(3):
                    for th in range(2):
                        tmm = b.MM(pf[cg * 2 + th][:, :],
                                   wlb[buf][:, cg * 128:(cg + 1) * 128],
                                   P["xn"][:, d * TPC + th * 512:
                                           d * TPC + th * 512 + 512],
                                   d == 0, d == NCH - 1,
                                   waits=[tc, tok_xn1, tok_xd,
                                          tprev if d == 0 else None])
                wfree[buf] = tmm
            for cg in range(3):
                j = cp * 3 + cg
                for th in range(2):
                    tprev = b.AC(slot(j)[:, th * 512:(th + 1) * 512],
                                 pf[cg * 2 + th][:, :], waits=[tmm],
                                 func=ACTF.Silu)
        tok_qkv = tprev

        self.new_phase()
        ktm = ar.alloc("ktm", [128, 256], BF16)
        vtm = ar.alloc("vtm", [128, 256], BF16)
        kmt = ar.alloc("kmt", [128, 256], BF16)
        cstage = ar.alloc("cstage", [128, 128], F32)
        tstage = ar.alloc("tstage", [128, 128], F32)
        slopes = _slopes(HEADS)
        tok_ag3in = None
        tcd = None
        for h in range(HEADS):
            bd = math.exp(-slopes[h] * BLK)
            tT = None
            for blk in range(NBLK_C):
                ttp = None
                for ts in range(2):
                    tp = b.TP(pb[0][:, ts * 128: ts * 128 + 128],
                              slot(3 * h + 1)[:, blk * 256 + ts * 128:
                                              blk * 256 + (ts + 1) * 128],
                              P["identb"][:], waits=[tok_qkv, ttp])
                    ttp = b.AC(ktm[:, ts * 128:(ts + 1) * 128],
                               pb[0][:, ts * 128: ts * 128 + 128], waits=[tp])
                    tp = b.TP(pb[0][:, 256 + ts * 128: 256 + ts * 128 + 128],
                              slot(3 * h + 2)[:, blk * 256 + ts * 128:
                                              blk * 256 + (ts + 1) * 128],
                              P["identb"][:], waits=[tok_qkv, ttp])
                    ttp = b.AC(vtm[:, ts * 128:(ts + 1) * 128],
                               pb[0][:, 256 + ts * 128: 256 + ts * 128 + 128],
                               waits=[tp])
                tkm = None
                for ts in range(2):
                    tkm = b.DV("tensor_scalar",
                               kmt[:, ts * 128:(ts + 1) * 128],
                               ktm[:, ts * 128:(ts + 1) * 128],
                               P["kdec"][:, h * 2 + ts: h * 2 + ts + 1], None,
                               ALU.mult, waits=[ttp, tkm])
                tmm = None
                for ts in range(2):
                    tmm = b.MM(pf[5][:, 0:128],
                               kmt[:, ts * 128:(ts + 1) * 128],
                               vtm[:, ts * 128:(ts + 1) * 128],
                               ts == 0, ts == 1, waits=[tkm, tT])
                tcs = b.AC(cstage[:], pf[5][:, 0:128], waits=[tmm, tcd])
                tcd = b.dma("sc", dt["cown"][(h * NBLK_C + blk) * 128:
                                             (h * NBLK_C + blk + 1) * 128, :],
                            cstage[:], waits=[tcs])
                if blk == 0:
                    tT = b.DV("tensor_copy", tstage[:], cstage[:],
                              waits=[tcs, tok_ag3in])
                else:
                    tT = b.DV("scalar_tensor_tensor", tstage[:], tstage[:],
                              bd, cstage[:], ALU.mult, ALU.add, waits=[tcs])
            tok_ag3in = b.dma("sc", dt["ag3_in"][h * 128:(h + 1) * 128, :],
                              tstage[:], waits=[tT, tok_ag3in])
        tok_cc3 = b.AG("cc3", "ag3_in", "ag3_out",
                       waits=list(self.qtok("sc")))

        kv = ar.alloc("kv", [128, 128], F32)
        kvb = ar.alloc("kvb", [128, 128], BF16)
        Tld = [ar.alloc(f"Tld{i}", [128, 128], F32) for i in range(2)]
        Cld = [ar.alloc(f"Cld{i}", [128, 128], F32) for i in range(2)]
        qdr = ar.alloc("qdr", [1, 256], F32)
        qm = ar.alloc("qm", [128, 256], BF16)
        dd = [ar.alloc(f"dd{i}", [128, 512], F32) for i in range(2)]
        PT = [ar.alloc(f"l1PT{i}", [128, 256], BF16) for i in range(2)]
        tqm_last = None
        for h in range(HEADS):
            self.new_phase()
            bd = math.exp(-slopes[h] * BLK)
            tkv = None
            tldp = [None, None]
            for r in range(NCORES - 1):
                tl = b.dma("sy", Tld[r % 2][:],
                           dt["ag3_out"][(r * HEADS + h) * 128:
                                         (r * HEADS + h + 1) * 128, :],
                           waits=[tok_cc3, tldp[r % 2]])
                if r == 0:
                    tkv = b.DV("tensor_scalar", kv[:], Tld[0][:],
                               P["coef"][:, 0 * HEADS + h: 0 * HEADS + h + 1],
                               None, ALU.mult, waits=[tl])
                else:
                    tkv = b.DV("scalar_tensor_tensor", kv[:], Tld[r % 2][:],
                               P["coef"][:, r * HEADS + h: r * HEADS + h + 1],
                               kv[:], ALU.mult, ALU.add, waits=[tl, tkv])
                tldp[r % 2] = tkv
            tql = b.dma("sy", qdr[:], dt["qdec_d"][h:h + 1, :],
                        waits=[tqm_last])
            tqb = b.MM(pf[4][:, 0:256], P["onesf"][:], qdr[:], True, True,
                       waits=[tql, tqm_last])
            tclp = [None, None]
            tddp = [None, None]
            tevac = None
            for blk in range(NBLK_C):
                ttp = None
                for ts in range(2):
                    tp = b.TP(pb[0][:, 512 + ts * 128: 512 + ts * 128 + 128],
                              slot(3 * h + 2)[:, blk * 256 + ts * 128:
                                              blk * 256 + (ts + 1) * 128],
                              P["identb"][:], waits=[tok_qkv, ttp])
                    ttp = b.AC(vtm[:, ts * 128:(ts + 1) * 128],
                               pb[0][:, 512 + ts * 128: 512 + ts * 128 + 128],
                               waits=[tp])
                tkvb = b.DV("tensor_copy", kvb[:], kv[:], waits=[tkv, tevac])
                tqm = b.DV("tensor_mul", qm[:],
                           slot(3 * h)[:, blk * 256:(blk + 1) * 256],
                           pf[4][:, 0:256], waits=[tqb, tok_qkv, tevac])
                tqm_last = tqm
                tnd = b.MM(pf[3][:, 0:256], kvb[:], qm[:], True, False,
                           waits=[tkvb, tqm])
                tpt = None
                for ks in range(2):
                    tdd = b.dma("sy", dd[ks][:, 0:256],
                                dt["ddec_d"][(h * 2 + ks) * 128:
                                             (h * 2 + ks + 1) * 128, :],
                                waits=[tddp[ks]])
                    tqk = b.MM(pf[ks][:, 0:256],
                               slot(3 * h + 1)[:, blk * 256 + ks * 128:
                                               blk * 256 + (ks + 1) * 128],
                               slot(3 * h)[:, blk * 256:(blk + 1) * 256],
                               True, True, waits=[tok_qkv, tpt, tevac])
                    tpt = b.DV("tensor_mul", PT[ks][:], pf[ks][:, 0:256],
                               dd[ks][:, 0:256], waits=[tqk, tdd])
                    tddp[ks] = tpt
                    b.MM(pf[3][:, 0:256], vtm[:, ks * 128:(ks + 1) * 128],
                         PT[ks][:], False, ks == 1, waits=[tpt, ttp])
                    if ks == 1:
                        tdone = (self.sem["pe"], self.sem["pe"].n)
                tevac = b.AC(slot(3 * h)[:, blk * 256:(blk + 1) * 256],
                             pf[3][:, 0:256], waits=[tdone])
                tcl = b.dma("sy", Cld[blk % 2][:],
                            dt["cown"][(h * NBLK_C + blk) * 128:
                                       (h * NBLK_C + blk + 1) * 128, :],
                            waits=[tclp[blk % 2], tok_ag3in])
                tkv = b.DV("scalar_tensor_tensor", kv[:], kv[:], bd,
                           Cld[blk % 2][:], ALU.mult, ALU.add,
                           waits=[tcl, tkvb, tnd])
                tclp[blk % 2] = tkv
            tok_scan = tevac

        self.new_phase()
        sqs = ar.alloc("sqs", [128, TPC], BF16)
        nrows = ar.alloc("nrows", [1, 2 * TPC], F32)
        rs_bc = ar.alloc("rs_bc", [128, TPC], F32)
        tvar = None
        tsq = None
        for hc in range(NCH):
            tsq = b.AC(sqs[:], slot(3 * (hc // 1) + 0)[:, 0:0]
                       if False else sqs[:], waits=[]) if False else tsq
        tvarp = [None, None]
        for hc in range(NCH):
            tsq = b.AC(sqs[:], slot(3 * hc)[:, 0:TPC],
                       waits=[tok_scan, tsq, tvarp[0], tvarp[1]],
                       func=ACTF.Square)
            for th in range(2):
                tvp = b.MM(pf[th][0:1, :], P["onesb"][:, 0:1],
                           sqs[:, th * 512:(th + 1) * 512],
                           hc == 0, hc == NCH - 1, waits=[tsq])
                tvarp[th] = tvp
        tbc = None
        for th in range(2):
            o = th * 512
            t = b.DV("tensor_scalar_mul", nrows[:, o:o + 512],
                     pf[th][0:1, :], 1.0 / D, waits=[tvarp[th]])
            t = b.AC(nrows[:, o:o + 512], nrows[:, o:o + 512],
                     waits=[t], func=ACTF.Sqrt,
                     bias=P["fcon"][0:1, 1:2])
            t = b.DV("reciprocal", nrows[:, TPC + o:TPC + o + 512],
                     nrows[:, o:o + 512], waits=[t])
            tb = b.MM(pf[2 + th][:, :], P["onesf"][:],
                      nrows[:, TPC + o:TPC + o + 512], True, True, waits=[t])
            tbc = b.AC(rs_bc[:, o:o + 512], pf[2 + th][:, :], waits=[tb])

        self.new_phase()
        sigst = ar.alloc("sigst", [128, 512], BF16)
        gtst = ar.alloc("gtst", [128, 512], F32)
        wfree = [None] * 4
        tprev = None
        for cp in range(6):
            ch0 = cp * 3
            ncg = 3 if cp < 5 else 1
            tmm = None
            for d in range(NCH):
                buf = (cp * NCH + d) % 4
                tc = b.dma("sy", wlb[buf][:, 0:ncg * 128],
                           dt["wg_b"][d * 128:(d + 1) * 128,
                                      ch0 * 128:(ch0 + ncg) * 128],
                           waits=[wfree[buf], self.wtok["wg"]])
                for cg in range(ncg):
                    for th in range(2):
                        tmm = b.MM(pf[cg * 2 + th][:, :],
                                   wlb[buf][:, cg * 128:(cg + 1) * 128],
                                   P["xn"][:, d * TPC + th * 512:
                                           d * TPC + th * 512 + 512],
                                   d == 0, d == NCH - 1,
                                   waits=[tc, tprev if d == 0 else None])
                wfree[buf] = tmm
            for cg in range(ncg):
                hc = ch0 + cg
                for th in range(2):
                    t = b.AC(sigst[:], pf[cg * 2 + th][:, :],
                             waits=[tmm, tprev], func=ACTF.Sigmoid)
                    t = b.DV("scalar_tensor_tensor", gtst[:],
                             slot(3 * hc)[:, th * 512:(th + 1) * 512],
                             P["nwv"][:, hc:hc + 1],
                             rs_bc[:, th * 512:(th + 1) * 512],
                             ALU.mult, ALU.mult, waits=[t, tbc, tok_scan])
                    tprev = b.DV("tensor_mul",
                                 slot(3 * hc)[:, th * 512:(th + 1) * 512],
                                 gtst[:], sigst[:], waits=[tprev, t])
        tok_gated = tprev

        self.new_phase()
        wfree = [None] * 4
        tprev = None
        for cp in range(6):
            ch0 = cp * 3
            ncg = 3 if cp < 5 else 1
            tmm = None
            for d in range(NCH):
                buf = (cp * NCH + d) % 4
                tc = b.dma("sy", wlb[buf][:, 0:ncg * 128],
                           dt["wout_b"][d * 128:(d + 1) * 128,
                                        ch0 * 128:(ch0 + ncg) * 128],
                           waits=[wfree[buf], self.wtok["wout"]])
                for cg in range(ncg):
                    for th in range(2):
                        tmm = b.MM(pf[cg * 2 + th][:, :],
                                   wlb[buf][:, cg * 128:(cg + 1) * 128],
                                   slot(3 * d)[:, th * 512:(th + 1) * 512],
                                   d == 0, d == NCH - 1,
                                   waits=[tc, tok_gated,
                                          tprev if d == 0 else None])
                wfree[buf] = tmm
            for cg in range(ncg):
                hc = ch0 + cg
                for th in range(2):
                    t = b.AC(sigst[:], pf[cg * 2 + th][:, :],
                             waits=[tmm, tprev])
                    tprev = b.dma("sc", dt["s1_d"][hc * 128:(hc + 1) * 128,
                                                   th * 512:(th + 1) * 512],
                                  sigst[:], waits=[t])
        tok_s1 = tprev
        self.tok_s1 = tok_s1
        self.tok_l1ffn_deps = (tok_s1, tok_xn1)
        ar.reset()

    def phase_out(self):
        b, ar, pf, pb, P, dt = self, self.ar, self.pf, self.pb, self.P, self.dt
        tok_s1, tok_xn1 = self.tok_l1ffn_deps
        ar.mark()
        gust = ar.alloc("gust", [128, L1HC * 512], BF16)
        gstg_off = ar.off
        gstg = ar.alloc("gstg", [128, 6 * 512], BF16)
        wl = [ar.alloc(f"fwl{i}", [128, 768], I32) for i in range(3)]
        wlb = [ar.alloc(f"fwb{i}", [128, 768], BF16) for i in range(3)]
        s1t = [ar.alloc("s1t0", [128, 512], BF16)] * 2
        xt = [ar.alloc("xt0", [128, 512], F32, at=gstg_off)] * 2
        g1s = ar.alloc("g1s", [128, 512], BF16)
        fst = ar.alloc("fst", [128, 512], F32, at=gstg_off + 2048)

        tok_out = None
        for th in range(2):
            self.new_phase()
            wfree = [None] * 3
            tprev = None
            for cp in range(8):
                hc0 = cp * 6
                ncg = 6 if cp < 7 else 2
                for which, wsrc, scl, fn in [
                        (0, "wgT_i", P["sgv"], ACTF.Silu),
                        (1, "wuT_i", P["suv"], ACTF.Copy)]:
                    tmm = None
                    for d in range(NCH):
                        buf = d % 3
                        tl = b.dma("sy", wl[buf][:, 0:ncg * 128],
                                   dt[wsrc][d * 128:(d + 1) * 128,
                                            hc0 * 128:(hc0 + ncg) * 128],
                                   waits=[wfree[buf]])
                        tc = b.DV("tensor_scalar_add", wlb[buf][:, 0:ncg * 128],
                                  wl[buf][:, 0:ncg * 128], -128.0, waits=[tl])
                        for cg in range(ncg):
                            tmm = b.MM(pf[cg % 6][:, :],
                                       wlb[buf][:, cg * 128:(cg + 1) * 128],
                                       P["xn"][:, d * TPC + th * 512:
                                               d * TPC + th * 512 + 512],
                                       d == 0, d == NCH - 1,
                                       waits=[tc, tok_xn1,
                                              tprev if d == 0 else None])
                        wfree[buf] = tmm
                    for cg in range(ncg):
                        hc = hc0 + cg
                        if which == 0:
                            tprev = b.AC(gstg[:, cg * 512:(cg + 1) * 512],
                                         pf[cg % 6][:, :], waits=[tmm],
                                         func=fn, scale=scl[:, hc:hc + 1])
                        else:
                            t = b.AC(g1s[:], pf[cg % 6][:, :],
                                     waits=[tmm, tprev],
                                     func=fn, scale=scl[:, hc:hc + 1])
                            tprev = b.DV("tensor_mul",
                                         gust[:, hc * 512:(hc + 1) * 512],
                                         gstg[:, cg * 512:(cg + 1) * 512],
                                         g1s[:], waits=[t])
            tok_gu = tprev
            self.new_phase()
            wfree = [None] * 3
            tprev = None
            for cp in range(6):
                ch0 = cp * 3
                ncg = 3 if cp < 5 else 1
                tmm = None
                for f in range(L1HC):
                    buf = f % 3
                    tl = b.dma("sy", wl[buf][:, 0:ncg * 128],
                               dt["wdT_i"][f * 128:(f + 1) * 128,
                                           ch0 * 128:(ch0 + ncg) * 128],
                               waits=[wfree[buf]])
                    tc = b.DV("tensor_scalar_add", wlb[buf][:, 0:ncg * 128],
                              wl[buf][:, 0:ncg * 128], -128.0, waits=[tl])
                    for cg in range(ncg):
                        tmm = b.MM(pf[cg * 2][:, :],
                                   wlb[buf][:, cg * 128:(cg + 1) * 128],
                                   gust[:, f * 512:(f + 1) * 512],
                                   f == 0, f == L1HC - 1,
                                   waits=[tc, tok_gu,
                                          tprev if f == 0 else None])
                    wfree[buf] = tmm
                for cg in range(ncg):
                    c = ch0 + cg
                    tsl = b.dma("sc", s1t[cg % 2][:],
                                dt["s1_d"][c * 128:(c + 1) * 128,
                                           th * 512:(th + 1) * 512],
                                waits=[tok_s1, tprev])
                    txl = b.dma("sc", xt[cg % 2][:],
                                dt["x_d"][:, c * TPC + th * 512:
                                          c * TPC + th * 512 + 512],
                                waits=[tprev])
                    t = b.AC(g1s[:], pf[cg * 2][:, :], waits=[tmm, tprev],
                             func=ACTF.Silu, scale=P["sdv"][:, c:c + 1])
                    t = b.DV("tensor_mul", fst[:], g1s[:], s1t[cg % 2][:],
                             waits=[t, tsl, tok_out])
                    t = b.DV("tensor_add", fst[:], fst[:], xt[cg % 2][:],
                             waits=[t, txl])
                    tprev = t
                    tok_out = self.dma(
                        "gp", self.out_fm[c * 128:(c + 1) * 128,
                                          th * 512:(th + 1) * 512],
                        fst[:], waits=[t])
        fins = [(s, s.n) for s in self.pool["gp"]]

        def fin(e, _f=tuple(fins)):
            for ws, wn in _f:
                if wn > 0:
                    e.wait_ge(ws.h, wn)
        self.q["gp"].append(fin)
        ar.reset()



_BUILD_CACHE = {}


def _host_tables(core):
    t0 = core * TPC
    pos = np.arange(t0, t0 + TPC, dtype=np.float64)
    freqs = 1.0 / THETA ** (np.arange(0, 96, 2, dtype=np.float64) / 96.0)
    emb = pos[:, None] * freqs[None, :]
    cos = np.cos(emb).astype(np.float32)
    sin = np.sin(emb).astype(np.float32)
    ropec = np.zeros((128, NTT * 48), np.float32)
    ropes = np.zeros((128, NTT * 48), np.float32)
    for g in range(NTT):
        ropec[:, g * 48:(g + 1) * 48] = cos[g * 128:(g + 1) * 128]
        ropes[:, g * 48:(g + 1) * 48] = sin[g * 128:(g + 1) * 128]
    slopes = np.array(_slopes(HEADS), np.float64)
    arr = np.arange(BLK, dtype=np.float64) + 1.0
    qdec = np.exp(-slopes[:, None] * arr[None, :]).astype(np.float32)
    kdec = np.zeros((128, HEADS * 2), np.float32)
    for h in range(HEADS):
        kd = np.exp(-slopes[h] * (BLK - arr)).astype(np.float32)
        kdec[:, h * 2] = kd[0:128]
        kdec[:, h * 2 + 1] = kd[128:256]
    ddec = np.zeros((HEADS * 2 * 128, BLK), np.float32)
    idx = arr[:, None] - arr[None, :]
    for h in range(HEADS):
        m_ = np.where(idx >= 0, np.exp(-slopes[h] * idx), 0.0)
        mT = m_.T.astype(np.float32)
        ddec[(h * 2) * 128:(h * 2 + 1) * 128] = mT[0:128]
        ddec[(h * 2 + 1) * 128:(h * 2 + 2) * 128] = mT[128:256]
    coef = np.zeros((128, 7 * HEADS), np.float32)
    for r in range(NCORES - 1):
        for h in range(HEADS):
            if r < core:
                bd = math.exp(-slopes[h] * BLK)
                coef[:, r * HEADS + h] = bd ** (NBLK_C * (core - 1 - r))
    causal = np.zeros((128, 896), np.float32)
    for p in range(128):
        causal[p, p + 384:] = 1.0
    trib = np.triu(np.ones((128, 128), np.float32), 1)
    iota224 = np.broadcast_to(np.arange(SLOTS, dtype=np.float32), (128, SLOTS)).copy()
    tokid = (np.arange(NTT, dtype=np.float32)[None, :] * 128
             + np.arange(128, dtype=np.float32)[:, None]).astype(np.float32)
    sel64 = np.zeros((64, 8), np.float32)
    sel64[np.arange(64), np.arange(64) % 8] = 1.0
    selmap8 = np.zeros((8, 64), np.float32)
    selmap8[np.arange(64) % 8, np.arange(64)] = 1.0
    idxqkv = np.zeros((128, 48), np.int32)
    for sec in range(3):
        for lh in range(2):
            hme = 2 * core + lh
            for r in range(NCORES):
                idxqkv[:, (sec * 2 + lh) * 8 + r] = \
                    (r * 48 + sec * 16 + hme) * 128 + np.arange(128)
    idxs2 = np.zeros((128, NCH), np.int32)
    for hc in range(NCH):
        idxs2[:, hc] = (hc * 128 + np.arange(128)) * S + TPC * core
    fcon = np.zeros((128, 4), np.float32)
    fcon[:, 0] = 1e-5
    fcon[:, 1] = 1e-6
    fcon[:, 2] = QSCALE
    bf = ml_dtypes.bfloat16
    return dict(
        fcon_d=fcon,
        ropec=ropec, ropes=ropes, qdec_d=qdec, kdec_d=kdec, ddec_d=ddec,
        coef_d=coef, causal_d=causal.astype(bf), identb_d=np.eye(128, dtype=bf),
        identf_d=np.eye(128, dtype=np.float32), trib_d=trib.astype(bf),
        onesb_d=np.ones((128, 128), bf), onesf_d=np.ones((1, 128), np.float32),
        onescol_d=np.ones((128, 1), np.float32), iota224_d=iota224,
        tokid_d=tokid, sel64_d=sel64, selmap8_d=selmap8,
        idxqkv_d=idxqkv, idxs2_d=idxs2)


def kernel(**inputs):
    global LAST_DEBUG
    key = _DEBUG
    if key not in _BUILD_CACHE:
        bld = Builder(debug_point=key)
        _BUILD_CACHE[key] = (bld.build(), bld)
    nc, bld = _BUILD_CACHE[key]

    f32 = np.float32
    x = np.asarray(inputs["L0_attn_x"], f32).reshape(S, D)
    gate_w = np.asarray(inputs["L0_ffn_gate_w"], f32)
    gatew = np.zeros((128, NCH * 7), f32)
    for c in range(NCH):
        gatew[:, c * 7:(c + 1) * 7] = gate_w[c * 128:(c + 1) * 128]
    ebias = np.asarray(inputs["L0_ffn_expert_bias"], f32).reshape(7, 1)
    sgv = np.ascontiguousarray(
        np.asarray(inputs["L1_ffn_s_gate"], f32).reshape(L1HC, 128).T)
    suv = np.ascontiguousarray(
        np.asarray(inputs["L1_ffn_s_up"], f32).reshape(L1HC, 128).T)
    sdv = np.ascontiguousarray(
        np.asarray(inputs["L1_ffn_s_down"], f32).reshape(NCH, 128).T)
    nwv = np.ascontiguousarray(
        np.asarray(inputs["L1_attn_norm_w"], f32).reshape(NCH, 128).T)
    shared = dict(
        wq_w=np.asarray(inputs["L0_attn_Wq"], f32),
        wkv_w=np.asarray(inputs["L0_attn_Wkv_down"], f32),
        wku_w=np.asarray(inputs["L0_attn_Wk_up"], f32),
        wvu_w=np.asarray(inputs["L0_attn_Wv_up"], f32),
        sw1_w=np.asarray(inputs["L0_ffn_shared_w1"], f32).reshape(D, 2048),
        sw2_w=np.asarray(inputs["L0_ffn_shared_w2"], f32).reshape(1024, D),
        rw1_w=np.asarray(inputs["L0_ffn_routed_w1"], f32).reshape(E_R * D, 2048),
        rw2_w=np.asarray(inputs["L0_ffn_routed_w2"], f32).reshape(E_R * 1024, D),
        wqkv_w=np.asarray(inputs["L1_attn_Wqkv"], f32),
        wg_w=np.asarray(inputs["L1_attn_Wg"], f32),
        wout_w=np.asarray(inputs["L1_attn_Wout"], f32),
        wgT_i=np.ascontiguousarray(np.asarray(inputs["L1_ffn_wq_gate"], np.int32).T),
        wuT_i=np.ascontiguousarray(np.asarray(inputs["L1_ffn_wq_up"], np.int32).T),
        wdT_i=np.ascontiguousarray(np.asarray(inputs["L1_ffn_wq_down"], np.int32).T),
        gatew=gatew, ebias=ebias, sgv=sgv, suv=suv, sdv=sdv, nwv=nwv)

    in_maps = []
    for r in range(NCORES):
        m = dict(shared)
        m["x_t"] = np.ascontiguousarray(x[r * TPC:(r + 1) * TPC].T)
        m.update(_host_tables(r))
        in_maps.append(m)
    res = run_bass_kernel_spmd(nc, in_maps, core_ids=list(range(NCORES)))
    out = np.zeros((S, D), f32)
    dbg = {}
    for r in range(NCORES):
        out[r * TPC:(r + 1) * TPC] = res.results[r]["out_fm"].T
        if _DEBUG is not None:
            dbg[r] = res.results[r]["dbg"].T
    LAST_DEBUG = dbg
    return out.reshape(1, S, D)


# version 11
# speedup vs baseline: 1.1793x; 1.0043x over previous
import math
import contextlib
import numpy as np

try:
    import concourse.bass as bass
except ImportError:
    import sys
    sys.path.insert(0, "/opt/trn_rl_repo")
    import concourse.bass as bass
import concourse.mybir as mybir
from concourse.bass_core_rust import IndirectOffsetOnAxis
from concourse.bass_utils import run_bass_kernel_spmd
import ml_dtypes

F32 = mybir.dt.float32
BF16 = mybir.dt.bfloat16
I32 = mybir.dt.int32
ACTF = mybir.ActivationFunctionType
ALU = mybir.AluOpType
AX = mybir.AxisListType

NCORES = 8
S = 8192
D = 2048
TPC = S // NCORES
NCH = D // 128
NTT = TPC // 128
HEADS = 16
HD = 128
LATENT = 512
NLC = LATENT // 128
E_R = 7
CAP = int(S / E_R)
SLOTS = 224
L1H = 5632
L1HC = L1H // 128
BLK = 256
NBLK_C = TPC // BLK
THETA = 10000.0
QSCALE = 1.0 / math.sqrt(HD)
NSEARCH = 36

_DEBUG = None
LAST_DEBUG = None

ENG = ["sy", "sc", "pe", "act", "dve", "gp"]


def _slopes(n):
    def p2(m):
        start = 2 ** (-(2 ** -(math.log2(m) - 3)))
        return [start * start ** i for i in range(m)]
    if math.log2(n).is_integer():
        return p2(n)
    c = 2 ** math.floor(math.log2(n))
    return p2(c) + _slopes(2 * c)[0::2][: n - c]


class Ctr:
    def __init__(self, nc, ctx, name):
        self.h = ctx.enter_context(nc.semaphore(name))
        self.n = 0
        self.name = name


class LazyTok:
    __slots__ = ("sem",)

    def __init__(self, sem):
        self.sem = sem


class Arena:
    def __init__(self, nc, ctx, size_bytes):
        self.nc = nc
        self.size = size_bytes
        self.big = ctx.enter_context(
            nc.sbuf_tensor("arena", [128, size_bytes], mybir.dt.uint8))
        self.base = nc.lookup_mloc(self.big).addr
        self.off = 0
        self.marks = []
        self.peak = 0
        self.cnt = 0

    def alloc(self, name, shape, dtype, at=None):
        per_part = int(np.prod(shape[1:])) * mybir.dt.size(dtype)
        per_part = (per_part + 31) & ~31
        if at is None:
            at = self.off
            self.off += per_part
        self.peak = max(self.peak, at + per_part)
        assert at + per_part <= self.size, (name, at, per_part)
        self.cnt += 1
        h = self.nc.virtual_sbuf_tensor(f"{name}_{self.cnt}", shape, dtype)
        self.nc._tensor(h.name, h.shape, h.dtype, type="SB")
        mloc = self.nc.lookup_mloc(h)
        mloc.allocated = True
        mloc.addr = self.base + at
        return h

    def mark(self):
        self.marks.append(self.off)

    def reset(self):
        self.off = self.marks.pop()


class Builder:
    def __init__(self, debug_point=None):
        self.nc = bass.Bass("TRN2", target_bir_lowering=False)
        self.dbgp = debug_point
        self.dt = {}
        self.q = {e: [] for e in ENG}
        self.bar = ()
        self.pool = {}
        self.rr = {"sy": 0, "sc": 0, "gp": 0}

    def emit(self, eng, build, waits=(), inc=None, k=1):
        eng = "act" if eng == "sc" else eng
        s = self.sem[inc] if isinstance(inc, str) else inc
        flat = []
        for w in waits:
            if w is None:
                continue
            if isinstance(w, tuple) and w and isinstance(w[0], tuple):
                flat.extend(w)
            else:
                flat.append(w)
        wl = tuple(flat) + self.bar

        def run(e, _b=build, _w=wl, _s=s, _k=k, _eng=eng, _seenref=self):
            seen = _seenref._seen[_eng]
            for (ws, wn) in _w:
                if wn > 0 and seen.get(ws.name, -1) < wn:
                    e.wait_ge(ws.h, wn)
                    seen[ws.name] = wn
            ins = _b(e)
            if _s is not None:
                ins.then_inc(_s.h, _k)
        self.q[eng].append(run)
        if s is not None:
            s.n += k
            return (s, s.n)
        return None

    def dma(self, eng, out, in_, waits=()):
        qn = eng
        s = self.pool[qn][self.rr[qn] % 6]
        self.rr[qn] += 1
        return self.emit(eng, lambda e, o=out, i=in_: e.dma_start(out=o, in_=i),
                         waits=tuple(waits) + ((s, s.n),), inc=s, k=16)

    def qtok(self, qn):
        return tuple((s, s.n) for s in self.pool[qn])

    def idma(self, build, waits=()):
        s = self.pool["gp"][self.rr["gp"] % 6]
        self.rr["gp"] += 1
        return self.emit("gp", build, waits=tuple(waits) + ((s, s.n),),
                         inc=s, k=16)

    def MM(self, psum, lhsT, rhs, start, stop, waits=(), inc=True):
        return self.emit(
            "pe", lambda e, p=psum, l=lhsT, r=rhs, s1=start, s2=stop:
            e.matmul(p, l, r, start=s1, stop=s2, skip_group_check=True),
            waits=waits, inc=("pe" if inc else None))

    def TP(self, psum, in_, ident, waits=()):
        return self.emit("pe", lambda e, p=psum, i=in_, d=ident:
                         e.transpose(p, i, d), waits=waits, inc="pe")

    def AC(self, out, in_, waits=(), func=ACTF.Copy, scale=1.0, bias=0.0):
        return self.emit("act", lambda e, o=out, i=in_, f=func, s=scale, bb=bias:
                         e.activation(o, i, f, bias=bb, scale=s),
                         waits=waits, inc="act")

    def DV(self, op, *args, waits=()):
        return self.emit("dve", lambda e, o=op, a=args: getattr(e, o)(*a),
                         waits=waits, inc="dve")

    def AG(self, name, inname, outname, waits=()):
        return self.emit("gp", lambda e, i=inname, o=outname:
                         e.collective_compute(
                             "AllGather", ALU.bypass,
                             replica_groups=[list(range(NCORES))],
                             ins=[self.dt[i].ap()], outs=[self.dt[o].ap()]),
                         waits=waits, inc=name)

    def new_phase(self):
        toks = [(self.sem[s], self.sem[s].n) for s in ("pe", "act", "dve")]
        for qn in ("sy", "sc", "gp"):
            toks += [(s, s.n) for s in self.pool[qn]]
        self.bar = tuple(toks) + getattr(self, "tok_x_all", ())

    def declare(self):
        nc, dt = self.nc, self.dt

        def din(name, shape, dtype):
            dt[name] = nc.dram_tensor(name, shape, dtype, kind="ExternalInput")

        def dint(name, shape, dtype):
            dt[name] = nc.dram_tensor(name, shape, dtype)

        din("x_t", [D, TPC], F32)
        din("wq_w", [D, D], F32)
        din("wkv_w", [D, LATENT], F32)
        din("wku_w", [LATENT, D], F32)
        din("wvu_w", [LATENT, D], F32)
        din("sw1_w", [D, D], F32)
        din("sw2_w", [1024, D], F32)
        din("rw1_w", [E_R * D, D], F32)
        din("rw2_w", [E_R * 1024, D], F32)
        din("wqkv_w", [D, 3 * D], F32)
        din("wg_w", [D, D], F32)
        din("wout_w", [D, D], F32)
        din("wgT_i", [D, L1H], I32)
        din("wuT_i", [D, L1H], I32)
        din("wdT_i", [L1H, D], I32)
        din("gatew", [128, NCH * 7], F32)
        din("ebias", [7, 1], F32)
        din("sgv", [128, L1HC], F32)
        din("suv", [128, L1HC], F32)
        din("sdv", [128, NCH], F32)
        din("nwv", [128, NCH], F32)
        din("ropec", [128, NTT * 48], F32)
        din("ropes", [128, NTT * 48], F32)
        din("qdec_d", [HEADS, BLK], F32)
        din("kdec_d", [128, HEADS * 2], F32)
        din("ddec_d", [HEADS * 2 * 128, BLK], F32)
        din("coef_d", [128, 7 * HEADS], F32)
        din("causal_d", [128, 896], BF16)
        din("identb_d", [128, 128], BF16)
        din("identf_d", [128, 128], F32)
        din("trib_d", [128, 128], BF16)
        din("onesb_d", [128, 128], BF16)
        din("onesf_d", [1, 128], F32)
        din("onescol_d", [128, 1], F32)
        din("iota224_d", [128, SLOTS], F32)
        din("tokid_d", [128, NTT], F32)
        din("sel64_d", [64, 8], F32)
        din("fcon_d", [128, 4], F32)
        din("selmap8_d", [8, 64], F32)
        din("idxqkv_d", [128, 48], I32)
        din("idxs2_d", [128, NCH], I32)

        self.out_fm = nc.dram_tensor("out_fm", [D, TPC], F32,
                                     kind="ExternalOutput")
        self.dbg = None
        if self.dbgp is not None:
            self.dbg = nc.dram_tensor("dbg", [D, TPC], F32,
                                      kind="ExternalOutput")

        for wn, sh in [("wq_b", [D, D]), ("wkv_b", [D, LATENT]),
                       ("wku_b", [LATENT, D]), ("wvu_b", [LATENT, D]),
                       ("sw1_b", [D, D]), ("sw2_b", [1024, D]),
                       ("rw1_b", [E_R * D, D]), ("rw2_b", [E_R * 1024, D]),
                       ("wqkv_b", [D, 3 * D]), ("wg_b", [D, D]),
                       ("wout_b", [D, D])]:
            dint(wn, sh, BF16)
        dint("ag1b_in", [8, TPC], F32)
        dint("ag1b_out", [8 * NCORES, TPC], F32)
        dint("ag1a_in", [48 * 128, TPC], BF16)
        dint("ag1a_out", [48 * 128 * NCORES, TPC], BF16)
        dint("ag2_in", [2 * 128, S], BF16)
        dint("ag2_out", [2 * 128 * NCORES * S, 1], BF16)
        dint("ag3_in", [HEADS * 128, 128], F32)
        dint("ag3_out", [NCORES * HEADS * 128, 128], F32)
        dint("xn_tm", [TPC, D], BF16)
        dint("routed", [TPC, D], BF16)
        dint("cown", [HEADS * NBLK_C * 128, 128], F32)
        dint("s1_d", [D, TPC], BF16)
        dint("x_d", [128, NCH * TPC], F32)

    def build(self):
        self.declare()
        nc, dt = self.nc, self.dt
        ctx = contextlib.ExitStack()
        self._ctx = ctx
        ctx.__enter__()
        ar = Arena(nc, ctx, 212800)
        self.ar = ar
        self.sem = {n: Ctr(nc, ctx, n) for n in
                    ["ld_sy", "ld_sc", "ld_gp", "pe", "act", "dve", "st",
                     "cc1a", "cc1b", "cc2", "cc3"]}
        for qn in ("sy", "sc", "gp"):
            self.pool[qn] = [Ctr(nc, ctx, f"p_{qn}{i}") for i in range(6)]
        sem = self.sem

        P = {}
        for nm, sh, dty in [
                ("x", [128, NCH * TPC], F32), ("identb", [128, 128], BF16),
                ("identf", [128, 128], F32), ("causal", [128, 896], BF16),
                ("trib", [128, 128], BF16), ("onesb", [128, 128], BF16),
                ("onesf", [1, 128], F32), ("onescol", [128, 1], F32),
                ("iota224", [128, SLOTS], F32), ("tokid", [128, NTT], F32),
                ("sel64", [64, 8], F32), ("selmap8", [8, 64], F32),
                ("fcon", [128, 4], F32),
                ("gatew", [128, NCH * 7], F32), ("ebias", [7, 1], F32),
                ("sgv", [128, L1HC], F32), ("suv", [128, L1HC], F32),
                ("sdv", [128, NCH], F32), ("nwv", [128, NCH], F32),
                ("ropec", [128, NTT * 48], F32), ("ropes", [128, NTT * 48], F32),
                ("kdec", [128, HEADS * 2], F32), ("coef", [128, 7 * HEADS], F32),
                ("idxqkv", [128, 48], I32), ("idxs2", [128, NCH], I32),
                ("xn", [128, NCH * TPC], BF16)]:
            P[nm] = ar.alloc(nm, sh, dty)
        self.P = P

        self.pf = [ctx.enter_context(nc.psum_tensor(f"pf{i}", [128, 512], F32))
                   for i in range(6)]
        self.pb = [ctx.enter_context(nc.psum_tensor(f"pb{i}", [128, 1024], BF16))
                   for i in range(2)]

        for nm, src in [("identb", "identb_d"), ("identf", "identf_d"),
                        ("causal", "causal_d"), ("trib", "trib_d"),
                        ("onesb", "onesb_d"), ("onesf", "onesf_d"),
                        ("onescol", "onescol_d"), ("iota224", "iota224_d"),
                        ("tokid", "tokid_d"), ("sel64", "sel64_d"),
                        ("selmap8", "selmap8_d"), ("fcon", "fcon_d"), ("gatew", "gatew"),
                        ("ebias", "ebias"), ("sgv", "sgv"), ("suv", "suv"),
                        ("sdv", "sdv"), ("nwv", "nwv"), ("ropec", "ropec"),
                        ("ropes", "ropes"), ("kdec", "kdec_d"),
                        ("coef", "coef_d"), ("idxqkv", "idxqkv_d"),
                        ("idxs2", "idxs2_d")]:
            self.dma("sy", P[nm][:], dt[src][:, :])
        for c in range(0, NCH, 4):
            self.dma("sy", P["x"][:, c * TPC:(c + 4) * TPC].rearrange(
                         "p (c t) -> p c t", c=4),
                     dt["x_t"][c * 128:(c + 4) * 128, :].rearrange(
                         "(c p) t -> p c t", p=128))
        self.tok_x = None
        self.tok_x_all = self.qtok("sy")
        self.bar = self.bar + self.tok_x_all
        self.wtok = {}
        for wn, rows in [("wkv", 2048), ("wq", 2048), ("wku", 512),
                         ("wvu", 512), ("sw1", 2048), ("rw1", E_R * 2048),
                         ("rw2", E_R * 1024), ("sw2", 1024),
                         ("wqkv", 2048), ("wg", 2048), ("wout", 2048)]:
            for r0 in range(0, rows, 2048):
                r1 = min(rows, r0 + 2048)
                self.dma("gp", dt[wn + "_b"][r0:r1, :], dt[wn + "_w"][r0:r1, :])
            self.wtok[wn] = self.qtok("gp")

        for ph in [self.phase_ln0, self.phase_routing, self.phase_qkv,
                   self.phase_moe, self.phase_attn, self.phase_combine0,
                   self.phase_l1, self.phase_out]:
            self.new_phase()
            ph()

        self._seen = {e: {} for e in ENG}
        block = ctx.enter_context(nc.Block())
        qs = self.q

        @block.sync
        def _(e):
            for f in qs["sy"]:
                f(e)

        @block.scalar
        def _(e):
            assert not qs["sc"]
            for f in qs["act"]:
                f(e)

        @block.tensor
        def _(e):
            for f in qs["pe"]:
                f(e)

        @block.vector
        def _(e):
            for f in qs["dve"]:
                f(e)

        @block.gpsimd
        def _(e):
            for f in qs["gp"]:
                f(e)

        ctx.__exit__(None, None, None)
        assert ar.peak <= 212800
        return nc

    def dbg_dump(self, point, sb, dtype=BF16, waits=()):
        if self.dbgp != point:
            return waits[0] if waits else None
        last = None
        for c in range(NCH):
            last = self.dma("gp", self.dbg[c * 128:(c + 1) * 128, :],
                            sb[:, c * TPC:(c + 1) * TPC], waits=waits)
        return last

    def ln(self, src, xn_out, mu_tm, inv_tm, eps_idx, tok_src, name):
        b, ar, pf = self, self.ar, self.pf
        ar.mark()
        sq = ar.alloc(f"{name}sq", [128, 512], F32)
        xtmp = ar.alloc(f"{name}xtmp", [128, 512], F32)
        rows = ar.alloc(f"{name}rows", [1, 5 * TPC], F32)
        SUM, SSQ, MU, INV, NEGM = 0, TPC, 2 * TPC, 3 * TPC, 4 * TPC
        tok_xn = None
        tok_tp = None
        prev_bc = None
        for th in range(2):
            o = th * 512
            tsq = None
            tmm = None
            for c in range(NCH):
                xa = src[:, c * TPC + o: c * TPC + o + 512]
                tsq = b.AC(sq[:], xa, waits=[tok_src, tsq, tmm], func=ACTF.Square)
                b.MM(pf[0][0:1, :], self.P["onescol"][:], xa,
                     c == 0, c == NCH - 1, waits=[tok_src, prev_bc])
                tmm = b.MM(pf[1][0:1, :], self.P["onescol"][:], sq[:],
                           c == 0, c == NCH - 1, waits=[tsq, prev_bc])
            t = b.DV("tensor_scalar_mul", rows[:, MU + o:MU + o + 512],
                     pf[0][0:1, :], 1.0 / D, waits=[tmm])
            t = b.DV("tensor_scalar_mul", rows[:, SSQ + o:SSQ + o + 512],
                     pf[1][0:1, :], 1.0 / D, waits=[t])
            t = b.DV("tensor_mul", rows[:, SUM + o:SUM + o + 512],
                     rows[:, MU + o:MU + o + 512],
                     rows[:, MU + o:MU + o + 512], waits=[t])
            t = b.DV("tensor_sub", rows[:, SSQ + o:SSQ + o + 512],
                     rows[:, SSQ + o:SSQ + o + 512],
                     rows[:, SUM + o:SUM + o + 512], waits=[t])
            t = b.AC(rows[:, SSQ + o:SSQ + o + 512],
                     rows[:, SSQ + o:SSQ + o + 512],
                     waits=[t], func=ACTF.Sqrt,
                     bias=self.P["fcon"][0:1, eps_idx:eps_idx + 1])
            t = b.DV("reciprocal", rows[:, INV + o:INV + o + 512],
                     rows[:, SSQ + o:SSQ + o + 512], waits=[t])
            t = b.DV("scalar_tensor_tensor", rows[:, NEGM + o:NEGM + o + 512],
                     rows[:, MU + o:MU + o + 512], -1.0,
                     rows[:, INV + o:INV + o + 512], ALU.mult, ALU.mult,
                     waits=[t])
            tb1 = b.MM(pf[2][:, :], self.P["onesf"][:],
                       rows[:, INV + o:INV + o + 512], True, True, waits=[t])
            tb2 = b.MM(pf[3][:, :], self.P["onesf"][:],
                       rows[:, NEGM + o:NEGM + o + 512], True, True, waits=[t])
            last_dve = prev_bc
            for c in range(NCH):
                xa = src[:, c * TPC + o: c * TPC + o + 512]
                t1 = b.DV("tensor_mul", xtmp[:], xa, pf[2][:, :],
                          waits=[tb1, tb2, last_dve])
                last_dve = b.DV("tensor_add",
                                xn_out[:, c * TPC + o: c * TPC + o + 512],
                                xtmp[:], pf[3][:, :], waits=[t1])
            prev_bc = last_dve
            tok_xn = last_dve
            for g in range(4):
                tt = th * 4 + g
                tp = b.TP(pf[4][:, 0:1],
                          rows[:, MU + o + g * 128: MU + o + g * 128 + 128],
                          self.P["identf"][0:1, 0:1], waits=[t, tok_tp])
                tok_tp = b.AC(mu_tm[:, tt:tt + 1], pf[4][:, 0:1], waits=[tp])
                tp = b.TP(pf[4][:, 1:2],
                          rows[:, INV + o + g * 128: INV + o + g * 128 + 128],
                          self.P["identf"][0:1, 0:1], waits=[t, tok_tp])
                tok_tp = b.AC(inv_tm[:, tt:tt + 1], pf[4][:, 1:2], waits=[tp])
        ar.reset()
        return tok_xn, tok_tp

    def phase_ln0(self):
        ar = self.ar
        self.mu0 = ar.alloc("mu0", [128, NTT], F32)
        self.inv0 = ar.alloc("inv0", [128, NTT], F32)
        self.tok_xn0, self.tok_mi0 = self.ln(
            self.P["x"], self.P["xn"], self.mu0, self.inv0, 0,
            self.tok_x, "ln0")
        self.dbg_dump("xn0", self.P["xn"], waits=[self.tok_xn0])

    def phase_routing(self):
        b, ar, pf, P = self, self.ar, self.pf, self.P
        self.aff = ar.alloc("aff", [128, NTT * 7], F32)
        self.mask2 = ar.alloc("mask2", [128, NTT * 7], F32)
        srow = ar.alloc("srow", [1, 16], F32)
        csum_bc = ar.alloc("csum_bc", [128, 7], F32)
        bias_bc = ar.alloc("bias_bc", [128, 7], F32)
        sc7 = ar.alloc("sc7", [8, TPC], F32)
        g1 = ar.alloc("g1t", [128, 7], F32)
        g2 = ar.alloc("g2t", [128, 7], F32)

        t = None
        for c in range(NCH):
            t = b.MM(pf[4][0:1, 0:7], P["onescol"][:],
                     P["gatew"][:, c * 7:(c + 1) * 7], c == 0, c == NCH - 1,
                     waits=[self.tok_x])
        t = b.DV("tensor_scalar_mul", srow[:, 0:7], pf[4][0:1, 0:7], -1.0,
                 waits=[t])
        t1 = b.MM(pf[4][:, 0:7], P["onesf"][:], srow[:, 0:7], True, True,
                  waits=[t])
        tcs = b.AC(csum_bc[:], pf[4][:, 0:7], waits=[t1])
        t = b.TP(pf[4][0:1, 8:15], P["ebias"][:], P["identf"][0:7, 0:7],
                 waits=[self.tok_x, tcs])
        t = b.AC(srow[:, 8:15], pf[4][0:1, 8:15], waits=[t])
        t = b.MM(pf[4][:, 8:15], P["onesf"][:], srow[:, 8:15], True, True,
                 waits=[t])
        tbs = b.AC(bias_bc[:], pf[4][:, 8:15], waits=[t])

        tmem = b.DV("memset", sc7[:, :], -1.0)
        tok_sc = None
        for g in range(NTT):
            tmm = None
            for c in range(NCH):
                tmm = b.MM(pf[5][:, 0:7],
                           P["x"][:, c * TPC + g * 128: c * TPC + g * 128 + 128],
                           P["gatew"][:, c * 7:(c + 1) * 7],
                           c == 0, c == NCH - 1,
                           waits=[self.tok_x, tcs, tbs, tok_sc])
            t = b.DV("scalar_tensor_tensor", g1[:], csum_bc[:],
                     self.mu0[:, g:g + 1], pf[5][:, 0:7], ALU.mult, ALU.add,
                     waits=[tmm, self.tok_mi0])
            t = b.DV("scalar_tensor_tensor", g2[:], g1[:],
                     self.inv0[:, g:g + 1], bias_bc[:], ALU.mult, ALU.add,
                     waits=[t])
            t = b.AC(self.aff[:, g * 7:(g + 1) * 7], g2[:], waits=[t],
                     func=ACTF.Sigmoid)
            t = b.DV("reduce_max", g1[:, 0:1], self.aff[:, g * 7:(g + 1) * 7],
                     AX.X, waits=[t])
            t = b.DV("tensor_single_scalar", g2[:],
                     self.aff[:, g * 7:(g + 1) * 7], g1[:, 0:1], ALU.is_lt,
                     waits=[t])
            t = b.DV("tensor_scalar_add", g1[:],
                     self.aff[:, g * 7:(g + 1) * 7], 1.0, waits=[t])
            t = b.DV("tensor_mul", g1[:], g1[:], g2[:], waits=[t])
            t = b.DV("tensor_scalar_sub", g1[:], g1[:], 1.0, waits=[t])
            t = b.DV("reduce_max", g2[:, 0:1], g1[:], AX.X, waits=[t])
            t = b.DV("tensor_single_scalar", self.mask2[:, g * 7:(g + 1) * 7],
                     self.aff[:, g * 7:(g + 1) * 7], g2[:, 0:1], ALU.is_ge,
                     waits=[t])
            t = b.DV("tensor_scalar_add", g1[:],
                     self.aff[:, g * 7:(g + 1) * 7], 1.0, waits=[t])
            t = b.DV("tensor_mul", g1[:], g1[:],
                     self.mask2[:, g * 7:(g + 1) * 7], waits=[t])
            t = b.DV("tensor_scalar_sub", g1[:], g1[:], 1.0, waits=[t])
            t = b.TP(pf[5][0:7, 128:256], g1[:], P["identf"][:],
                     waits=[t, tok_sc])
            tok_sc = b.AC(sc7[0:7, g * 128:(g + 1) * 128], pf[5][0:7, 128:256],
                          waits=[t, tmem])
        tw = b.dma("sy", self.dt["ag1b_in"][:, :], sc7[:],
                   waits=[tok_sc, tmem])
        self.tok_cc1b = b.AG("cc1b", "ag1b_in", "ag1b_out", waits=[tw])

    def phase_qkv(self):
        b, ar, pf, pb, P, dt = self, self.ar, self.pf, self.pb, self.P, self.dt
        ar.mark()
        wres = ar.alloc("wres", [128, NCH * 1024], BF16)
        wldf = [ar.alloc(f"wldf{i}", [128, 1024], F32) for i in range(2)]
        ckvT = ar.alloc("ckvT", [128, NLC * TPC], BF16)
        stage_f = ar.alloc("stage_f", [128, 1024], F32)
        stage_tm = ar.alloc("stage_tm", [128, 1024], BF16)
        hstage = ar.alloc("hstage", [128, NCH * TPC], BF16)
        rt1 = ar.alloc("rt1", [128, 384], F32)
        rt2 = ar.alloc("rt2", [128, 384], F32)

        def load_res(dram, rows, c0, cw, prev_users_done):
            wt = self.wtok[dram[:-2]]
            for i in range(rows // 128):
                b.dma("sy", wres[:, i * 1024:i * 1024 + cw],
                      dt[dram][i * 128:(i + 1) * 128, c0:c0 + cw],
                      waits=[prev_users_done, wt])
            return self.qtok("sy")

        twkv = load_res("wkv_b", 2048, 0, 512, None)
        tok_ck = None
        ck_evs = []
        for lc in range(NLC):
            for th in range(2):
                pi = lc * 2 + th
                tmm = None
                for c in range(NCH):
                    tmm = b.MM(pf[pi % 6][:, :],
                               wres[:, c * 1024 + lc * 128: c * 1024 + lc * 128 + 128],
                               P["xn"][:, c * TPC + th * 512: c * TPC + th * 512 + 512],
                               c == 0, c == NCH - 1,
                               waits=[twkv, self.tok_xn0,
                                      ck_evs[pi - 6] if pi >= 6 and c == 0 else None])
                tok_ck = b.AC(ckvT[:, lc * TPC + th * 512: lc * TPC + th * 512 + 512],
                              pf[pi % 6][:, :], waits=[tmm])
                ck_evs.append(tok_ck)
        tok_ckv = tok_ck

        def qk_tm(wtok, nk, src, sec, scale, half, stage_free):
            if stage_free is None or isinstance(stage_free, tuple) and \
                    stage_free and isinstance(stage_free[0], tuple):
                extra = list(stage_free) if stage_free else []
                tok_done = None
            else:
                extra = [stage_free]
                tok_done = None
            for g in range(NTT):
                tmm = None
                for kk in range(nk):
                    for cg in range(2):
                        tmm = b.MM(pf[cg][:, :],
                                   src[:, kk * TPC + g * 128: kk * TPC + g * 128 + 128],
                                   wres[:, kk * 1024 + cg * 512:
                                        kk * 1024 + (cg + 1) * 512],
                                   kk == 0, kk == nk - 1,
                                   waits=[wtok, self.tok_xn0, tok_done] + extra)
                tcp = None
                for cg in range(2):
                    tcp = b.AC(stage_f[:, cg * 512:(cg + 1) * 512], pf[cg][:, :],
                               waits=[tmm, tcp, tok_done] + extra,
                               scale=(self.P["fcon"][:, 2:3] if scale != 1.0
                                      else 1.0))
                cb = P["ropec"][:, g * 48:(g + 1) * 48][:, None, :]\
                    .broadcast_to([128, 8, 48])
                sb_ = P["ropes"][:, g * 48:(g + 1) * 48][:, None, :]\
                    .broadcast_to([128, 8, 48])
                sfv = stage_f[:].rearrange("p (h c) -> p h c", h=8)
                stv = stage_tm[:].rearrange("p (h c) -> p h c", h=8)
                r1v = rt1[:].rearrange("p (h c) -> p h c", h=8)
                r2v = rt2[:].rearrange("p (h c) -> p h c", h=8)
                t = b.DV("tensor_mul", r1v, sfv[:, :, 32:128:2], cb,
                         waits=[tcp, tok_done] + extra)
                t = b.DV("tensor_mul", r2v, sfv[:, :, 33:128:2], sb_, waits=[t])
                t = b.DV("tensor_sub", stv[:, :, 32:128:2], r1v, r2v, waits=[t])
                t = b.DV("tensor_mul", r1v, sfv[:, :, 33:128:2], cb, waits=[t])
                t = b.DV("tensor_mul", r2v, sfv[:, :, 32:128:2], sb_, waits=[t])
                t = b.DV("tensor_add", stv[:, :, 33:128:2], r1v, r2v, waits=[t])
                t = b.DV("tensor_copy", stv[:, :, 0:32], sfv[:, :, 0:32],
                         waits=[t])
                tlast = None
                for c in range(8):
                    hc = half * 8 + c
                    tp = b.TP(pb[0][:, c * 128:(c + 1) * 128],
                              stage_tm[:, c * 128:(c + 1) * 128], P["identb"][:],
                              waits=[t, tlast])
                    tlast = b.AC(hstage[:, hc * TPC + g * 128:
                                        hc * TPC + g * 128 + 128],
                                 pb[0][:, c * 128:(c + 1) * 128], waits=[tp])
                tok_done = tlast
            return tok_done

        tw = load_res("wq_b", 2048, 0, 1024, tok_ckv)
        th0 = qk_tm(tw, NCH, P["xn"], 0, QSCALE, 0, None)
        tw = load_res("wq_b", 2048, 1024, 1024, th0)
        th1 = qk_tm(tw, NCH, P["xn"], 0, QSCALE, 1, th0)
        for c in range(NCH):
            b.dma("sy", dt["ag1a_in"][c * 128:(c + 1) * 128, :],
                  hstage[:, c * TPC:(c + 1) * TPC], waits=[th1])
        tq_all = self.qtok("sy")
        tw = load_res("wku_b", 512, 0, 1024, th1)
        th0 = qk_tm(tw, NLC, ckvT, 1, 1.0, 0, tq_all)
        tw = load_res("wku_b", 512, 1024, 1024, th0)
        th1 = qk_tm(tw, NLC, ckvT, 1, 1.0, 1, th0)
        for c in range(NCH):
            b.dma("sy", dt["ag1a_in"][(16 + c) * 128:(17 + c) * 128, :],
                  hstage[:, c * TPC:(c + 1) * TPC], waits=[th1])
        tk_all = self.qtok("sy")
        tvs = None
        v_evs = []
        for half in range(2):
            tw = load_res("wvu_b", 512, half * 1024, 1024,
                          tvs)
            for cc in range(8):
                c = half * 8 + cc
                for th in range(2):
                    gi = cc * 2 + th
                    pi = gi % 6
                    tmm = None
                    for lc in range(NLC):
                        tmm = b.MM(pf[pi][:, :],
                                   wres[:, lc * 1024 + cc * 128:
                                        lc * 1024 + cc * 128 + 128],
                                   ckvT[:, lc * TPC + th * 512:
                                        lc * TPC + th * 512 + 512],
                                   lc == 0, lc == NLC - 1,
                                   waits=[tw, tok_ckv,
                                          v_evs[-6] if len(v_evs) >= 6 and lc == 0 else None]
                                   + list(tk_all))
                    tvs = b.AC(hstage[:, c * TPC + th * 512:
                                      c * TPC + th * 512 + 512],
                               pf[pi][:, :], waits=[tmm] + list(tk_all))
                    v_evs.append(tvs)
        for c in range(NCH):
            b.dma("sy", dt["ag1a_in"][(32 + c) * 128:(33 + c) * 128, :],
                  hstage[:, c * TPC:(c + 1) * TPC], waits=[tvs])
        self.tok_cc1a = b.AG("cc1a", "ag1a_in", "ag1a_out",
                             waits=list(self.qtok("sy")))

        txm = None
        for g in range(NTT):
            tps = None
            for c in range(NCH):
                tp = b.TP(pb[1][:, (c % 8) * 128:(c % 8) * 128 + 128],
                          P["xn"][:, c * TPC + g * 128: c * TPC + g * 128 + 128],
                          P["identb"][:], waits=[self.tok_xn0, txm, tps])
                tps = b.AC(stage_tm[:, (c % 8) * 128:(c % 8) * 128 + 128],
                           pb[1][:, (c % 8) * 128:(c % 8) * 128 + 128],
                           waits=[tp, tvs, txm]) if c % 8 == 7 or True else tps
                if c % 8 == 7:
                    txm = b.dma("sy",
                                dt["xn_tm"][g * 128:(g + 1) * 128,
                                            (c - 7) * 128:(c + 1) * 128],
                                stage_tm[:, 0:1024], waits=[tps])
        self.tok_xntm = self.qtok("sy")
        ar.reset()

    def phase_moe(self):
        b, ar, pf, pb, P, dt = self, self.ar, self.pf, self.pb, self.P, self.dt
        self.gprime = ar.alloc("gprime", [128, NCH * TPC], BF16)
        ar.mark()
        score64 = ar.alloc("score64", [64, TPC], F32)
        ge64 = ar.alloc("ge64", [64, TPC], BF16)
        thc = ar.alloc("thc", [64, 1], F32)
        lo8 = ar.alloc("lo8", [8, 1], F32)
        hi8 = ar.alloc("hi8", [8, 1], F32)
        mid8 = ar.alloc("mid8", [8, 1], F32)
        m8 = ar.alloc("m8", [8, 1], F32)
        d8 = ar.alloc("d8", [8, 1], F32)
        cnt64 = ar.alloc("cnt64", [64, 1], F32)
        trow = ar.alloc("trow", [1, 8], F32)
        thbc = ar.alloc("thbc", [128, 8], F32)
        kept = ar.alloc("kept", [128, NTT * 7], F32)
        keptb = ar.alloc("keptb", [128, NTT * 7], BF16)
        affw = ar.alloc("affw", [128, NTT * 7], F32)
        rank = ar.alloc("rank", [128, NTT * 7], F32)
        st_e = ar.alloc("st_e", [128, SLOTS], F32)
        idxrow = ar.alloc("idxrow", [1, SLOTS], F32)
        wrow = ar.alloc("wrow", [1, SLOTS], F32)
        idxcol = ar.alloc("idxcol", [128, 2], I32)
        wcol = ar.alloc("wcol", [128, 2], F32)
        g7 = ar.alloc("g7m", [128, 7], F32)
        wlb = [ar.alloc(f"mwb{i}", [128, 1024], BF16) for i in range(6)]
        xg_off = ar.off
        xg = ar.alloc("xg", [128, 2048], BF16)
        xgT = ar.alloc("xgT", [128, NCH * SLOTS], BF16)
        h1 = ar.alloc("h1m", [128, 2048], BF16)
        gu = ar.alloc("gum", [128, 1024], BF16)
        guT = ar.alloc("guT", [128, 8 * SLOTS], BF16)
        oe = ar.alloc("oem", [128, 2048], BF16)
        sguT = ar.alloc("sguT", [128, 8 * TPC], BF16)
        silt = ar.alloc("silt", [128, 512], BF16)
        cstg = ar.alloc("cstg", [128, 512], F32)
        rstg = [ar.alloc("rstg0", [128, NTT * 128], BF16, at=xg_off)] * 2

        tso = b.dma("sy", score64[:], dt["ag1b_out"][:, :],
                    waits=[self.tok_cc1b])
        tth = b.DV("memset", lo8[:], 0.0)
        tth = b.DV("memset", hi8[:], 1.0, waits=[tth])
        tth = b.DV("tensor_add", mid8[:], lo8[:], hi8[:], waits=[tth])
        tth = b.DV("tensor_scalar_mul", mid8[:], mid8[:], 0.5, waits=[tth])
        th_state = {"t": tth, "pe": None, "it": 0}

        def theta_iter():
            if th_state["it"] >= NSEARCH:
                return
            th_state["it"] += 1
            t = th_state["t"]
            tp1 = b.MM(pf[4][0:64, 0:1], P["selmap8"][:], mid8[:], True, True,
                       waits=[t, th_state["pe"]])
            t = b.AC(thc[:], pf[4][0:64, 0:1], waits=[tp1])
            t = b.DV("tensor_single_scalar", ge64[:], score64[:], thc[:, 0:1],
                     ALU.is_ge, waits=[t, tso])
            t = b.DV("reduce_sum", cnt64[:], ge64[:], AX.X, waits=[t])
            tp2 = b.MM(pf[4][0:8, 2:3], P["sel64"][:], cnt64[:], True, True,
                       waits=[t])
            th_state["pe"] = tp2
            t = b.AC(d8[:], pf[4][0:8, 2:3], waits=[tp2])
            t = b.DV("tensor_single_scalar", m8[:], d8[:], float(CAP),
                     ALU.is_ge, waits=[t])
            t = b.DV("tensor_sub", d8[:], mid8[:], lo8[:], waits=[t])
            t = b.DV("tensor_mul", d8[:], d8[:], m8[:], waits=[t])
            t = b.DV("tensor_add", lo8[:], lo8[:], d8[:], waits=[t])
            t = b.DV("tensor_sub", d8[:], hi8[:], mid8[:], waits=[t])
            t = b.DV("tensor_mul", d8[:], d8[:], m8[:], waits=[t])
            t = b.DV("tensor_add", hi8[:], mid8[:], d8[:], waits=[t])
            t = b.DV("tensor_add", mid8[:], lo8[:], hi8[:], waits=[t])
            t = b.DV("tensor_scalar_mul", mid8[:], mid8[:], 0.5, waits=[t])
            th_state["t"] = t

        tok_sgu = None
        wfree = [None] * 6
        for c in range(8):
            tmm = None
            for d in range(NCH):
                buf = (c * NCH + d) % 6
                tl1 = b.dma("sy", wlb[buf][:, 0:128],
                            dt["sw1_b"][d * 128:(d + 1) * 128,
                                        c * 128:(c + 1) * 128],
                            waits=[wfree[buf], self.wtok["sw1"]])
                tl2 = b.dma("sy", wlb[buf][:, 128:256],
                            dt["sw1_b"][d * 128:(d + 1) * 128,
                                        (c + 8) * 128:(c + 9) * 128],
                            waits=[wfree[buf]])
                for pi in range(4):
                    cc, th = divmod(pi, 2)
                    tmm = b.MM(pf[pi][:, :],
                               wlb[buf][:, cc * 128:(cc + 1) * 128],
                               P["xn"][:, d * TPC + th * 512:
                                       d * TPC + th * 512 + 512],
                               d == 0, d == NCH - 1,
                               waits=[tl1, tl2, self.tok_xn0,
                                      tok_sgu if d == 0 else None])
                wfree[buf] = tmm
                if (c * NCH + d) % 3 == 2:
                    theta_iter()
            for th in range(2):
                t1 = b.AC(silt[:], pf[2 + th][:, :], waits=[tmm, tok_sgu],
                          func=ACTF.Silu)
                tok_sgu = b.DV("tensor_mul",
                               sguT[:, c * TPC + th * 512: c * TPC + th * 512 + 512],
                               pf[th][:, :], silt[:], waits=[t1])
        while th_state["it"] < NSEARCH:
            theta_iter()

        self.new_phase()
        tp = b.TP(pf[4][0:1, 4:12], lo8[:], P["identf"][0:8, 0:8],
                  waits=[th_state["t"], th_state["pe"]])
        t = b.AC(trow[:], pf[4][0:1, 4:12], waits=[tp])
        tp = b.MM(pf[4][:, 4:12], P["onesf"][:], trow[:], True, True, waits=[t])
        tthbc = b.AC(thbc[:], pf[4][:, 4:12], waits=[tp])

        tk = None
        for g in range(NTT):
            t = b.DV("tensor_tensor", g7[:], self.aff[:, g * 7:(g + 1) * 7],
                     thbc[:, 0:7], ALU.is_ge, waits=[tthbc, tk])
            t = b.DV("tensor_mul", kept[:, g * 7:(g + 1) * 7], g7[:],
                     self.mask2[:, g * 7:(g + 1) * 7], waits=[t])
            t = b.DV("tensor_mul", affw[:, g * 7:(g + 1) * 7],
                     kept[:, g * 7:(g + 1) * 7],
                     self.aff[:, g * 7:(g + 1) * 7], waits=[t])
            tk = b.DV("tensor_copy", keptb[:, g * 7:(g + 1) * 7],
                      kept[:, g * 7:(g + 1) * 7], waits=[t])
        tok_kept = tk
        trk = None
        for g in range(NTT):
            tmm = None
            for gp in range(g + 1):
                tmm = b.MM(pf[4][:, 16:23],
                           (P["trib"] if gp == g else P["onesb"])[:],
                           keptb[:, gp * 7:(gp + 1) * 7],
                           gp == 0, gp == g, waits=[tok_kept, trk])
            trk = b.AC(rank[:, g * 7:(g + 1) * 7], pf[4][:, 16:23],
                       waits=[tmm])
        tok_rank = trk

        tz = b.DV("memset", oe[:], 0.0)
        for g in range(NTT):
            b.dma("sy", dt["routed"][g * 128:(g + 1) * 128, :], oe[:],
                  waits=[tz])
        tzd_all = self.qtok("sy")

        tok_scat = None
        oe_free = tz
        for ex in range(E_R):
            self.new_phase()
            tdi = None
            tdw = None
            for g in range(NTT):
                t = b.DV("tensor_scalar", st_e[:], P["iota224"][:],
                         rank[:, g * 7 + ex: g * 7 + ex + 1], None,
                         ALU.subtract, waits=[tok_rank, tdi, tdw])
                t = b.DV("tensor_single_scalar", st_e[:], st_e[:], 0.0,
                         ALU.is_equal, waits=[t])
                t = b.DV("tensor_scalar", st_e[:], st_e[:],
                         kept[:, g * 7 + ex: g * 7 + ex + 1], None,
                         ALU.mult, waits=[t])
                tdi = b.MM(pf[4][0:1, 32:32 + SLOTS], P["tokid"][:, g:g + 1],
                           st_e[:], g == 0, g == NTT - 1, waits=[t])
                tdw = b.MM(pf[5][0:1, 0:SLOTS],
                           affw[:, g * 7 + ex: g * 7 + ex + 1],
                           st_e[:], g == 0, g == NTT - 1, waits=[t])
            t1 = b.AC(idxrow[:], pf[4][0:1, 32:32 + SLOTS], waits=[tdi])
            t2 = b.AC(wrow[:], pf[5][0:1, 0:SLOTS], waits=[tdw])
            tix = None
            for s2 in range(2):
                wdt = 128 if s2 == 0 else SLOTS - 128
                tp = b.TP(pf[4][0:wdt, 128:129],
                          idxrow[:, s2 * 128: s2 * 128 + wdt],
                          P["identf"][0:1, 0:1], waits=[t1, t2, tix])
                tix = b.DV("tensor_copy", idxcol[0:wdt, s2:s2 + 1],
                           pf[4][0:wdt, 128:129], waits=[tp])
                tp = b.TP(pf[4][0:wdt, 129:130],
                          wrow[:, s2 * 128: s2 * 128 + wdt],
                          P["identf"][0:1, 0:1], waits=[tix])
                tix = b.DV("tensor_copy", wcol[0:wdt, s2:s2 + 1],
                           pf[4][0:wdt, 129:130], waits=[tp])
            tok_xgT = None
            for s2 in range(2):
                wdt = 128 if s2 == 0 else SLOTS - 128
                tg = self.idma(
                    lambda e, s2=s2, wdt=wdt: e.indirect_dma_start(
                        out=xg[0:wdt, :], out_offset=None,
                        in_=dt["xn_tm"][:, :],
                        in_offset=IndirectOffsetOnAxis(idxcol[0:wdt, s2:s2 + 1], 0)),
                    waits=[tix, tok_scat, tok_xgT] + list(self.tok_xntm))
                for c in range(NCH):
                    tp = b.TP(pb[0][:, (c % 8) * 128:(c % 8) * 128 + wdt],
                              xg[0:wdt, c * 128:(c + 1) * 128],
                              P["identb"][0:wdt, 0:wdt], waits=[tg, tok_xgT])
                    tok_xgT = b.AC(
                        xgT[:, c * SLOTS + s2 * 128: c * SLOTS + s2 * 128 + wdt],
                        pb[0][:, (c % 8) * 128:(c % 8) * 128 + wdt], waits=[tp])
            tok_glu = None
            for s2 in range(2):
                wdt = 128 if s2 == 0 else SLOTS - 128
                tmm = None
                for d in range(NCH):
                    buf = d % 6
                    tc = b.dma("sy", wlb[buf][:],
                               dt["rw1_b"][ex * 2048 + d * 128:
                                           ex * 2048 + (d + 1) * 128, 0:1024],
                               waits=[wfree[buf], self.wtok["rw1"]])
                    for cg in range(2):
                        tmm = b.MM(pf[cg][0:wdt, :],
                                   xgT[:, d * SLOTS + s2 * 128:
                                       d * SLOTS + s2 * 128 + wdt],
                                   wlb[buf][:, cg * 512:(cg + 1) * 512],
                                   d == 0, d == NCH - 1,
                                   waits=[tc, tok_xgT,
                                          tok_glu if d == 0 else None])
                    wfree[buf] = tmm
                tev = None
                for cg in range(2):
                    tev = b.AC(h1[0:wdt, cg * 512:(cg + 1) * 512],
                               pf[cg][0:wdt, :], waits=[tmm, tok_glu])
                tmm = None
                for d in range(NCH):
                    buf = d % 6
                    tc = b.dma("sy", wlb[buf][:],
                               dt["rw1_b"][ex * 2048 + d * 128:
                                           ex * 2048 + (d + 1) * 128,
                                           1024:2048],
                               waits=[wfree[buf]])
                    for cg in range(2):
                        tmm = b.MM(pf[2 + cg][0:wdt, :],
                                   xgT[:, d * SLOTS + s2 * 128:
                                       d * SLOTS + s2 * 128 + wdt],
                                   wlb[buf][:, cg * 512:(cg + 1) * 512],
                                   d == 0, d == NCH - 1,
                                   waits=[tc, tev if d == 0 else None])
                    wfree[buf] = tmm
                for cg in range(2):
                    tev = b.AC(h1[0:wdt, 1024 + cg * 512:1024 + (cg + 1) * 512],
                               pf[2 + cg][0:wdt, :], waits=[tmm])
                t = b.AC(gu[0:wdt, :], h1[0:wdt, 1024:2048],
                         waits=[tev], func=ACTF.Silu)
                t = b.DV("tensor_mul", gu[0:wdt, :], gu[0:wdt, :],
                         h1[0:wdt, 0:1024], waits=[t])
                tok_glu = b.DV("tensor_scalar", gu[0:wdt, :], gu[0:wdt, :],
                               wcol[0:wdt, s2:s2 + 1], None, ALU.mult,
                               waits=[t])
                tgt = tok_glu
                for c in range(8):
                    tp = b.TP(pb[0][:, (c % 8) * 128:(c % 8) * 128 + wdt],
                              gu[0:wdt, c * 128:(c + 1) * 128],
                              P["identb"][0:wdt, 0:wdt], waits=[tgt])
                    tgt = b.AC(
                        guT[:, c * SLOTS + s2 * 128: c * SLOTS + s2 * 128 + wdt],
                        pb[0][:, (c % 8) * 128:(c % 8) * 128 + wdt], waits=[tp])
                tok_glu = tgt
            tok_guT = tok_glu
            for s2 in range(2):
                wdt = 128 if s2 == 0 else SLOTS - 128
                for ps in range(2):
                    c0 = ps * 2
                    tmm = None
                    for f in range(8):
                        buf = f % 6
                        tc = b.dma("sy", wlb[buf][:],
                                   dt["rw2_b"][ex * 1024 + f * 128:
                                               ex * 1024 + (f + 1) * 128,
                                               c0 * 512:(c0 + 2) * 512],
                                   waits=[wfree[buf], self.wtok["rw2"]])
                        for cg in range(2):
                            tmm = b.MM(pf[cg][0:wdt, :],
                                       guT[:, f * SLOTS + s2 * 128:
                                           f * SLOTS + s2 * 128 + wdt],
                                       wlb[buf][:, cg * 512:(cg + 1) * 512],
                                       f == 0, f == 7,
                                       waits=[tc, tok_guT,
                                              tok_scat if f == 0 else None])
                        wfree[buf] = tmm
                    tev = None
                    for cg in range(2):
                        cc = c0 + cg
                        tev = b.AC(oe[0:wdt, cc * 512:(cc + 1) * 512],
                                   pf[cg][0:wdt, :], waits=[tmm, tok_scat])
                    tok_scat = self.idma(
                        lambda e, s2=s2, wdt=wdt, c0=c0: e.indirect_dma_start(
                            out=dt["routed"][:, :],
                            out_offset=IndirectOffsetOnAxis(
                                idxcol[0:wdt, s2:s2 + 1], 0),
                            in_=oe[0:wdt, c0 * 512:(c0 + 2) * 512],
                            in_offset=None, compute_op=ALU.add,
                            element_offset=c0 * 512),
                        waits=[tev, tok_scat] + list(tzd_all))
        tok_scatter = None
        scat_all = self.qtok("gp")
        self.new_phase()

        tok_g = None
        trb_prev = None
        for c in range(NCH):
            for g in range(NTT):
                b.dma("sc", rstg[c % 2][:, g * 128:(g + 1) * 128],
                      dt["routed"][g * 128:(g + 1) * 128,
                                   c * 128:(c + 1) * 128],
                      waits=[trb_prev] + list(scat_all))
            trb_all = self.qtok("sc")
            trb_prev = None
            ttp = None
            for g in range(NTT):
                tp = b.TP(pb[1][:, g * 128:(g + 1) * 128],
                          rstg[c % 2][:, g * 128:(g + 1) * 128],
                          P["identb"][:], waits=[ttp, tok_g] + list(trb_all))
                ttp = tp
            trb_prev = ttp
            tmm = None
            for f in range(8):
                buf = (c * 8 + f) % 6
                tc = b.dma("sy", wlb[buf][:, 0:128],
                           dt["sw2_b"][f * 128:(f + 1) * 128,
                                       c * 128:(c + 1) * 128],
                           waits=[wfree[buf], self.wtok["sw2"]])
                for th in range(2):
                    tmm = b.MM(pf[4 + th][:, :],
                               wlb[buf][:, 0:128],
                               sguT[:, f * TPC + th * 512: f * TPC + th * 512 + 512],
                               f == 0, f == 7,
                               waits=[tc, tok_sgu, tok_g if f == 0 else None])
                wfree[buf] = tmm
            for th in range(2):
                trt = b.AC(silt[:], pb[1][:, th * 512:(th + 1) * 512],
                           waits=[ttp, tok_g])
                t = b.DV("tensor_add", cstg[:], pf[4 + th][:, :],
                         silt[:], waits=[tmm, trt])
                tok_g = b.AC(
                    self.gprime[:, c * TPC + th * 512: c * TPC + th * 512 + 512],
                    cstg[:], waits=[t], func=ACTF.Silu)
        self.tok_gprime = tok_g
        self.dbg_dump("gprime", self.gprime, waits=[tok_g])
        ar.reset()

    def phase_attn(self):
        b, ar, pf, pb, P, dt = self, self.ar, self.pf, self.pb, self.P, self.dt
        ar.mark()
        qfT = ar.alloc("qfT", [128, S], BF16)
        kfT = ar.alloc("kfT", [128, S], BF16)
        vfT_off = ar.off
        vfT = ar.alloc("vfT", [128, S], BF16)
        v_tm = ar.alloc("v_tm", [128, S], BF16)
        PT = [ar.alloc(f"PT{i}", [128, 512], BF16,
                       at=vfT_off + i * 1024) for i in range(2)]
        o_stage = ar.alloc("o_stage", [128, 512], BF16, at=vfT_off + 2048)
        lrow = ar.alloc("lrow", [1, 512], F32, at=vfT_off + 3072)
        linv = ar.alloc("linv", [1, 512], F32, at=vfT_off + 5120)
        binv = ar.alloc("binv", [128, 512], F32, at=vfT_off + 7168)

        ag1av = dt["ag1a_out"]
        tok_ag2in = None
        tok_prev_head = None
        for lh in range(2):
            self.new_phase()
            tg = {0: None, 1: None, 2: None}
            for sec, buf in [(0, qfT), (1, kfT), (2, vfT)]:
                for r in range(NCORES):
                    tg[sec] = self.idma(
                        lambda e, sec=sec, buf=buf, r=r, lh=lh:
                        e.indirect_dma_start(
                            out=buf[:, r * TPC:(r + 1) * TPC], out_offset=None,
                            in_=ag1av[:, :],
                            in_offset=IndirectOffsetOnAxis(
                                P["idxqkv"][:, (sec * 2 + lh) * 8 + r:(sec * 2 + lh) * 8 + r + 1], 0)),
                        waits=[self.tok_cc1a, self.tok_x, tok_prev_head])
            gath_all = self.qtok("gp")
            tvt = None
            for kt in range(64):
                tp = b.TP(pb[0][:, (kt % 8) * 128:(kt % 8) * 128 + 128],
                          vfT[:, kt * 128:(kt + 1) * 128], P["identb"][:],
                          waits=[tvt] + list(gath_all))
                tvt = b.AC(v_tm[:, kt * 128:(kt + 1) * 128],
                           pb[0][:, (kt % 8) * 128:(kt % 8) * 128 + 128],
                           waits=[tp])
            tok_norm = None
            exp_t = {}
            pv_t = {}
            for qt in range(16):
                nkc = 4 * (qt + 1)
                rhsq = qfT[:, qt * 512:(qt + 1) * 512]
                for kc in range(nkc):
                    sbank = kc % 2
                    tsc = b.MM(pf[sbank][:, :], kfT[:, kc * 128:(kc + 1) * 128],
                               rhsq, True, True,
                               waits=[exp_t.get((qt, kc - 2)), tok_norm]
                               + list(gath_all))
                    te = b.AC(PT[sbank][:], pf[sbank][:, :],
                              waits=[tsc, pv_t.get((qt, kc - 2))], func=ACTF.Exp)
                    if kc >= 4 * qt:
                        o = kc - 4 * qt
                        te = b.DV("tensor_mul", PT[sbank][:], PT[sbank][:],
                                  P["causal"][:, 384 - o * 128: 896 - o * 128],
                                  waits=[te])
                    exp_t[(qt, kc)] = te
                    b.MM(pf[2][:, :], v_tm[:, kc * 128:(kc + 1) * 128],
                         PT[sbank][:], kc == 0, kc == nkc - 1,
                         waits=[te, tvt, tok_norm])
                    pv_t[(qt, kc)] = b.MM(pf[3][0:1, :], P["onesb"][:, 0:1],
                                          PT[sbank][:], kc == 0, kc == nkc - 1,
                                          waits=[te, tok_norm if kc == 0 else None])
                t = b.AC(lrow[:], pf[3][0:1, :], waits=[pv_t[(qt, nkc - 1)]])
                t = b.DV("reciprocal", linv[:], lrow[:], waits=[t])
                tb = b.MM(pf[4][:, :], P["onesf"][:], linv[:], True, True,
                          waits=[t, tok_norm])
                tb = b.AC(binv[:], pf[4][:, :], waits=[tb])
                tok_norm = b.DV("tensor_mul", o_stage[:], pf[2][:, :],
                                binv[:], waits=[tb, tok_ag2in])
                tok_ag2in = b.dma("sy", dt["ag2_in"][lh * 128:(lh + 1) * 128,
                                                     qt * 512:(qt + 1) * 512],
                                  o_stage[:], waits=[tok_norm])
            tok_prev_head = tok_ag2in
        self.tok_cc2 = b.AG("cc2", "ag2_in", "ag2_out",
                            waits=list(self.qtok("sy")))
        ar.reset()

    def phase_combine0(self):
        b, ar, pf, P, dt = self, self.ar, self.pf, self.P, self.dt
        ar.mark()
        s_t = [ar.alloc(f"s_t{i}", [128, TPC], BF16) for i in range(2)]
        fstg = ar.alloc("fstg", [128, TPC], F32)
        tok_x1 = None
        tprev = [None, None]
        for c in range(NCH):
            buf = c % 2
            tgt = self.idma(
                lambda e, c=c, buf=buf: e.indirect_dma_start(
                    out=s_t[buf][:], out_offset=None,
                    in_=dt["ag2_out"][:, :],
                    in_offset=IndirectOffsetOnAxis(P["idxs2"][:, c:c + 1], 0)),
                waits=[self.tok_cc2, tprev[buf]])
            t = b.DV("tensor_mul", fstg[:], s_t[buf][:],
                     self.gprime[:, c * TPC:(c + 1) * TPC],
                     waits=[tgt, self.tok_gprime, tok_x1])
            tprev[buf] = t
            tok_x1 = b.DV("tensor_add", P["x"][:, c * TPC:(c + 1) * TPC],
                          P["x"][:, c * TPC:(c + 1) * TPC], fstg[:], waits=[t])
        self.tok_x1 = tok_x1
        self.dbg_dump("x1", P["x"], waits=[tok_x1])
        ar.reset()

    def phase_l1(self):
        b, ar, pf, pb, P, dt = self, self.ar, self.pf, self.pb, self.P, self.dt
        mu1 = ar.alloc("mu1", [128, NTT], F32)
        inv1 = ar.alloc("inv1", [128, NTT], F32)
        tok_xn1, _ = self.ln(P["x"], P["xn"], mu1, inv1, 0,
                             self.tok_x1, "ln1")
        self.new_phase()
        self.dbg_dump("xn1", P["xn"], waits=[tok_xn1])
        tok_xd = None
        for c in range(NCH):
            tok_xd = b.dma("sy", dt["x_d"][:, c * TPC:(c + 1) * TPC],
                           P["x"][:, c * TPC:(c + 1) * TPC],
                           waits=[self.tok_x1])
        ar.mark()
        regA = ar.alloc("regA", [128, 32 * TPC], BF16, at=0)
        gp_off = None
        def slot(j):
            if j < 32:
                return regA[:, j * TPC:(j + 1) * TPC]
            return self.gprime[:, (j - 32) * TPC:(j - 31) * TPC]

        wlb = [ar.alloc(f"l1wb{i}", [128, 384], BF16) for i in range(4)]

        wfree = [None] * 4
        tprev = None
        pev = {}
        for cp in range(16):
            tmm = None
            for d in range(NCH):
                buf = (cp * NCH + d) % 4
                tc = b.dma("sy", wlb[buf][:, 0:384],
                           dt["wqkv_b"][d * 128:(d + 1) * 128,
                                        cp * 384:(cp + 1) * 384],
                           waits=[wfree[buf], self.wtok["wqkv"]])
                for cg in range(3):
                    for th in range(2):
                        tmm = b.MM(pf[cg * 2 + th][:, :],
                                   wlb[buf][:, cg * 128:(cg + 1) * 128],
                                   P["xn"][:, d * TPC + th * 512:
                                           d * TPC + th * 512 + 512],
                                   d == 0, d == NCH - 1,
                                   waits=[tc, tok_xn1, tok_xd,
                                          pev.get(cg * 2 + th) if d == 0
                                          else None])
                wfree[buf] = tmm
            for cg in range(3):
                j = cp * 3 + cg
                for th in range(2):
                    tprev = b.AC(slot(j)[:, th * 512:(th + 1) * 512],
                                 pf[cg * 2 + th][:, :], waits=[tmm],
                                 func=ACTF.Silu)
                    pev[cg * 2 + th] = tprev
        tok_qkv = tprev

        self.new_phase()
        ktm = ar.alloc("ktm", [128, 256], BF16)
        vtm = ar.alloc("vtm", [128, 256], BF16)
        kmt = ar.alloc("kmt", [128, 256], BF16)
        cstage = ar.alloc("cstage", [128, 128], F32)
        tstage = ar.alloc("tstage", [128, 128], F32)
        slopes = _slopes(HEADS)
        tok_ag3in = None
        tcd = None
        for h in range(HEADS):
            bd = math.exp(-slopes[h] * BLK)
            tT = None
            for blk in range(NBLK_C):
                ttp = None
                for ts in range(2):
                    tp = b.TP(pb[0][:, ts * 128: ts * 128 + 128],
                              slot(3 * h + 1)[:, blk * 256 + ts * 128:
                                              blk * 256 + (ts + 1) * 128],
                              P["identb"][:], waits=[tok_qkv, ttp])
                    ttp = b.AC(ktm[:, ts * 128:(ts + 1) * 128],
                               pb[0][:, ts * 128: ts * 128 + 128], waits=[tp])
                    tp = b.TP(pb[0][:, 256 + ts * 128: 256 + ts * 128 + 128],
                              slot(3 * h + 2)[:, blk * 256 + ts * 128:
                                              blk * 256 + (ts + 1) * 128],
                              P["identb"][:], waits=[tok_qkv, ttp])
                    ttp = b.AC(vtm[:, ts * 128:(ts + 1) * 128],
                               pb[0][:, 256 + ts * 128: 256 + ts * 128 + 128],
                               waits=[tp])
                tkm = None
                for ts in range(2):
                    tkm = b.DV("tensor_scalar",
                               kmt[:, ts * 128:(ts + 1) * 128],
                               ktm[:, ts * 128:(ts + 1) * 128],
                               P["kdec"][:, h * 2 + ts: h * 2 + ts + 1], None,
                               ALU.mult, waits=[ttp, tkm])
                tmm = None
                for ts in range(2):
                    tmm = b.MM(pf[5][:, 0:128],
                               kmt[:, ts * 128:(ts + 1) * 128],
                               vtm[:, ts * 128:(ts + 1) * 128],
                               ts == 0, ts == 1, waits=[tkm, tT])
                tcs = b.AC(cstage[:], pf[5][:, 0:128], waits=[tmm, tcd])
                tcd = b.dma("sc", dt["cown"][(h * NBLK_C + blk) * 128:
                                             (h * NBLK_C + blk + 1) * 128, :],
                            cstage[:], waits=[tcs])
                if blk == 0:
                    tT = b.DV("tensor_copy", tstage[:], cstage[:],
                              waits=[tcs, tok_ag3in])
                else:
                    tT = b.DV("scalar_tensor_tensor", tstage[:], tstage[:],
                              bd, cstage[:], ALU.mult, ALU.add, waits=[tcs])
            tok_ag3in = b.dma("sc", dt["ag3_in"][h * 128:(h + 1) * 128, :],
                              tstage[:], waits=[tT, tok_ag3in])
        tok_cc3 = b.AG("cc3", "ag3_in", "ag3_out",
                       waits=list(self.qtok("sc")))

        kv = ar.alloc("kv", [128, 128], F32)
        kvb = ar.alloc("kvb", [128, 128], BF16)
        Tld = [ar.alloc(f"Tld{i}", [128, 128], F32) for i in range(2)]
        Cld = [ar.alloc(f"Cld{i}", [128, 128], F32) for i in range(2)]
        qdr = ar.alloc("qdr", [1, 256], F32)
        qm = ar.alloc("qm", [128, 256], BF16)
        dd = [ar.alloc(f"dd{i}", [128, 512], F32) for i in range(2)]
        PT = [ar.alloc(f"l1PT{i}", [128, 256], BF16) for i in range(2)]
        tqm_last = None
        for h in range(HEADS):
            self.new_phase()
            bd = math.exp(-slopes[h] * BLK)
            tkv = None
            tldp = [None, None]
            for r in range(NCORES - 1):
                tl = b.dma("sy", Tld[r % 2][:],
                           dt["ag3_out"][(r * HEADS + h) * 128:
                                         (r * HEADS + h + 1) * 128, :],
                           waits=[tok_cc3, tldp[r % 2]])
                if r == 0:
                    tkv = b.DV("tensor_scalar", kv[:], Tld[0][:],
                               P["coef"][:, 0 * HEADS + h: 0 * HEADS + h + 1],
                               None, ALU.mult, waits=[tl])
                else:
                    tkv = b.DV("scalar_tensor_tensor", kv[:], Tld[r % 2][:],
                               P["coef"][:, r * HEADS + h: r * HEADS + h + 1],
                               kv[:], ALU.mult, ALU.add, waits=[tl, tkv])
                tldp[r % 2] = tkv
            tql = b.dma("sy", qdr[:], dt["qdec_d"][h:h + 1, :],
                        waits=[tqm_last])
            tqb = b.MM(pf[4][:, 0:256], P["onesf"][:], qdr[:], True, True,
                       waits=[tql, tqm_last])
            tclp = [None, None]
            tddp = [None, None]
            tevac = None
            for blk in range(NBLK_C):
                ttp = None
                for ts in range(2):
                    tp = b.TP(pb[0][:, 512 + ts * 128: 512 + ts * 128 + 128],
                              slot(3 * h + 2)[:, blk * 256 + ts * 128:
                                              blk * 256 + (ts + 1) * 128],
                              P["identb"][:], waits=[tok_qkv, ttp])
                    ttp = b.AC(vtm[:, ts * 128:(ts + 1) * 128],
                               pb[0][:, 512 + ts * 128: 512 + ts * 128 + 128],
                               waits=[tp])
                tkvb = b.DV("tensor_copy", kvb[:], kv[:], waits=[tkv, tevac])
                tqm = b.DV("tensor_mul", qm[:],
                           slot(3 * h)[:, blk * 256:(blk + 1) * 256],
                           pf[4][:, 0:256], waits=[tqb, tok_qkv, tevac])
                tqm_last = tqm
                tnd = b.MM(pf[3][:, 0:256], kvb[:], qm[:], True, False,
                           waits=[tkvb, tqm])
                tpt = None
                for ks in range(2):
                    tdd = b.dma("sy", dd[ks][:, 0:256],
                                dt["ddec_d"][(h * 2 + ks) * 128:
                                             (h * 2 + ks + 1) * 128, :],
                                waits=[tddp[ks]])
                    tqk = b.MM(pf[ks][:, 0:256],
                               slot(3 * h + 1)[:, blk * 256 + ks * 128:
                                               blk * 256 + (ks + 1) * 128],
                               slot(3 * h)[:, blk * 256:(blk + 1) * 256],
                               True, True, waits=[tok_qkv, tpt, tevac])
                    tpt = b.DV("tensor_mul", PT[ks][:], pf[ks][:, 0:256],
                               dd[ks][:, 0:256], waits=[tqk, tdd])
                    tddp[ks] = tpt
                    b.MM(pf[3][:, 0:256], vtm[:, ks * 128:(ks + 1) * 128],
                         PT[ks][:], False, ks == 1, waits=[tpt, ttp])
                    if ks == 1:
                        tdone = (self.sem["pe"], self.sem["pe"].n)
                tevac = b.AC(slot(3 * h)[:, blk * 256:(blk + 1) * 256],
                             pf[3][:, 0:256], waits=[tdone])
                tcl = b.dma("sy", Cld[blk % 2][:],
                            dt["cown"][(h * NBLK_C + blk) * 128:
                                       (h * NBLK_C + blk + 1) * 128, :],
                            waits=[tclp[blk % 2], tok_ag3in])
                tkv = b.DV("scalar_tensor_tensor", kv[:], kv[:], bd,
                           Cld[blk % 2][:], ALU.mult, ALU.add,
                           waits=[tcl, tkvb, tnd])
                tclp[blk % 2] = tkv
            tok_scan = tevac

        self.new_phase()
        sqs = ar.alloc("sqs", [128, TPC], BF16)
        nrows = ar.alloc("nrows", [1, 2 * TPC], F32)
        rs_bc = ar.alloc("rs_bc", [128, TPC], F32)
        tvar = None
        tsq = None
        for hc in range(NCH):
            tsq = b.AC(sqs[:], slot(3 * (hc // 1) + 0)[:, 0:0]
                       if False else sqs[:], waits=[]) if False else tsq
        tvarp = [None, None]
        for hc in range(NCH):
            tsq = b.AC(sqs[:], slot(3 * hc)[:, 0:TPC],
                       waits=[tok_scan, tsq, tvarp[0], tvarp[1]],
                       func=ACTF.Square)
            for th in range(2):
                tvp = b.MM(pf[th][0:1, :], P["onesb"][:, 0:1],
                           sqs[:, th * 512:(th + 1) * 512],
                           hc == 0, hc == NCH - 1, waits=[tsq])
                tvarp[th] = tvp
        tbc = None
        for th in range(2):
            o = th * 512
            t = b.DV("tensor_scalar_mul", nrows[:, o:o + 512],
                     pf[th][0:1, :], 1.0 / D, waits=[tvarp[th]])
            t = b.AC(nrows[:, o:o + 512], nrows[:, o:o + 512],
                     waits=[t], func=ACTF.Sqrt,
                     bias=P["fcon"][0:1, 1:2])
            t = b.DV("reciprocal", nrows[:, TPC + o:TPC + o + 512],
                     nrows[:, o:o + 512], waits=[t])
            tb = b.MM(pf[2 + th][:, :], P["onesf"][:],
                      nrows[:, TPC + o:TPC + o + 512], True, True, waits=[t])
            tbc = b.AC(rs_bc[:, o:o + 512], pf[2 + th][:, :], waits=[tb])

        self.new_phase()
        sigst = ar.alloc("sigst", [128, 512], BF16)
        gtst = ar.alloc("gtst", [128, 512], F32)
        wfree = [None] * 4
        tprev = None
        pev = {}
        for cp in range(6):
            ch0 = cp * 3
            ncg = 3 if cp < 5 else 1
            tmm = None
            for d in range(NCH):
                buf = (cp * NCH + d) % 4
                tc = b.dma("sy", wlb[buf][:, 0:ncg * 128],
                           dt["wg_b"][d * 128:(d + 1) * 128,
                                      ch0 * 128:(ch0 + ncg) * 128],
                           waits=[wfree[buf], self.wtok["wg"]])
                for cg in range(ncg):
                    for th in range(2):
                        tmm = b.MM(pf[cg * 2 + th][:, :],
                                   wlb[buf][:, cg * 128:(cg + 1) * 128],
                                   P["xn"][:, d * TPC + th * 512:
                                           d * TPC + th * 512 + 512],
                                   d == 0, d == NCH - 1,
                                   waits=[tc, pev.get(cg * 2 + th)
                                          if d == 0 else None])
                wfree[buf] = tmm
            for cg in range(ncg):
                hc = ch0 + cg
                for th in range(2):
                    t = b.AC(sigst[:], pf[cg * 2 + th][:, :],
                             waits=[tmm, tprev], func=ACTF.Sigmoid)
                    pev[cg * 2 + th] = t
                    t = b.DV("scalar_tensor_tensor", gtst[:],
                             slot(3 * hc)[:, th * 512:(th + 1) * 512],
                             P["nwv"][:, hc:hc + 1],
                             rs_bc[:, th * 512:(th + 1) * 512],
                             ALU.mult, ALU.mult, waits=[t, tbc, tok_scan])
                    tprev = b.DV("tensor_mul",
                                 slot(3 * hc)[:, th * 512:(th + 1) * 512],
                                 gtst[:], sigst[:], waits=[tprev, t])
        tok_gated = tprev

        self.new_phase()
        wfree = [None] * 4
        tprev = None
        for cp in range(6):
            ch0 = cp * 3
            ncg = 3 if cp < 5 else 1
            tmm = None
            for d in range(NCH):
                buf = (cp * NCH + d) % 4
                tc = b.dma("sy", wlb[buf][:, 0:ncg * 128],
                           dt["wout_b"][d * 128:(d + 1) * 128,
                                        ch0 * 128:(ch0 + ncg) * 128],
                           waits=[wfree[buf], self.wtok["wout"]])
                for cg in range(ncg):
                    for th in range(2):
                        tmm = b.MM(pf[cg * 2 + th][:, :],
                                   wlb[buf][:, cg * 128:(cg + 1) * 128],
                                   slot(3 * d)[:, th * 512:(th + 1) * 512],
                                   d == 0, d == NCH - 1,
                                   waits=[tc, tok_gated,
                                          pev.get(10 + cg * 2 + th)
                                          if d == 0 else None])
                wfree[buf] = tmm
            for cg in range(ncg):
                hc = ch0 + cg
                for th in range(2):
                    t = b.AC(sigst[:], pf[cg * 2 + th][:, :],
                             waits=[tmm, tprev])
                    pev[10 + cg * 2 + th] = t
                    tprev = b.dma("sc", dt["s1_d"][hc * 128:(hc + 1) * 128,
                                                   th * 512:(th + 1) * 512],
                                  sigst[:], waits=[t])
        tok_s1 = tprev
        self.tok_s1 = tok_s1
        self.tok_l1ffn_deps = (tok_s1, tok_xn1)
        ar.reset()

    def phase_out(self):
        b, ar, pf, pb, P, dt = self, self.ar, self.pf, self.pb, self.P, self.dt
        tok_s1, tok_xn1 = self.tok_l1ffn_deps
        ar.mark()
        gust = ar.alloc("gust", [128, L1HC * 512], BF16)
        gstg_off = ar.off
        gstg = ar.alloc("gstg", [128, 6 * 512], BF16)
        wl = [ar.alloc(f"fwl{i}", [128, 768], I32) for i in range(3)]
        wlb = [ar.alloc(f"fwb{i}", [128, 768], BF16) for i in range(3)]
        s1t = [ar.alloc("s1t0", [128, 512], BF16)] * 2
        xt = [ar.alloc("xt0", [128, 512], F32, at=gstg_off)] * 2
        g1s = ar.alloc("g1s", [128, 512], BF16)
        fst = ar.alloc("fst", [128, 512], F32, at=gstg_off + 2048)

        tok_out = None
        pev = {}
        for th in range(2):
            self.new_phase()
            pev = {}
            wfree = [None] * 3
            tprev = None
            for cp in range(8):
                hc0 = cp * 6
                ncg = 6 if cp < 7 else 2
                for which, wsrc, scl, fn in [
                        (0, "wgT_i", P["sgv"], ACTF.Silu),
                        (1, "wuT_i", P["suv"], ACTF.Copy)]:
                    tmm = None
                    for d in range(NCH):
                        buf = d % 3
                        tl = b.dma("sy", wl[buf][:, 0:ncg * 128],
                                   dt[wsrc][d * 128:(d + 1) * 128,
                                            hc0 * 128:(hc0 + ncg) * 128],
                                   waits=[wfree[buf]])
                        tc = b.DV("tensor_scalar_add", wlb[buf][:, 0:ncg * 128],
                                  wl[buf][:, 0:ncg * 128], -128.0, waits=[tl])
                        for cg in range(ncg):
                            tmm = b.MM(pf[cg % 6][:, :],
                                       wlb[buf][:, cg * 128:(cg + 1) * 128],
                                       P["xn"][:, d * TPC + th * 512:
                                               d * TPC + th * 512 + 512],
                                       d == 0, d == NCH - 1,
                                       waits=[tc, tok_xn1,
                                              pev.get(cg % 6)
                                              if d == 0 else None])
                        wfree[buf] = tmm
                    for cg in range(ncg):
                        hc = hc0 + cg
                        if which == 0:
                            tprev = b.AC(gstg[:, cg * 512:(cg + 1) * 512],
                                         pf[cg % 6][:, :], waits=[tmm],
                                         func=fn, scale=scl[:, hc:hc + 1])
                            pev[cg % 6] = tprev
                        else:
                            t = b.AC(g1s[:], pf[cg % 6][:, :],
                                     waits=[tmm, tprev],
                                     func=fn, scale=scl[:, hc:hc + 1])
                            pev[cg % 6] = t
                            tprev = b.DV("tensor_mul",
                                         gust[:, hc * 512:(hc + 1) * 512],
                                         gstg[:, cg * 512:(cg + 1) * 512],
                                         g1s[:], waits=[t])
            tok_gu = tprev
            self.new_phase()
            wfree = [None] * 3
            tprev = None
            for cp in range(6):
                ch0 = cp * 3
                ncg = 3 if cp < 5 else 1
                tmm = None
                for f in range(L1HC):
                    buf = f % 3
                    tl = b.dma("sy", wl[buf][:, 0:ncg * 128],
                               dt["wdT_i"][f * 128:(f + 1) * 128,
                                           ch0 * 128:(ch0 + ncg) * 128],
                               waits=[wfree[buf]])
                    tc = b.DV("tensor_scalar_add", wlb[buf][:, 0:ncg * 128],
                              wl[buf][:, 0:ncg * 128], -128.0, waits=[tl])
                    for cg in range(ncg):
                        tmm = b.MM(pf[cg * 2][:, :],
                                   wlb[buf][:, cg * 128:(cg + 1) * 128],
                                   gust[:, f * 512:(f + 1) * 512],
                                   f == 0, f == L1HC - 1,
                                   waits=[tc, tok_gu,
                                          pev.get(20 + cg * 2)
                                          if f == 0 else None])
                    wfree[buf] = tmm
                for cg in range(ncg):
                    c = ch0 + cg
                    tsl = b.dma("sc", s1t[cg % 2][:],
                                dt["s1_d"][c * 128:(c + 1) * 128,
                                           th * 512:(th + 1) * 512],
                                waits=[tok_s1, tprev])
                    txl = b.dma("sc", xt[cg % 2][:],
                                dt["x_d"][:, c * TPC + th * 512:
                                          c * TPC + th * 512 + 512],
                                waits=[tprev])
                    t = b.AC(g1s[:], pf[cg * 2][:, :], waits=[tmm, tprev],
                             func=ACTF.Silu, scale=P["sdv"][:, c:c + 1])
                    pev[20 + cg * 2] = t
                    t = b.DV("tensor_mul", fst[:], g1s[:], s1t[cg % 2][:],
                             waits=[t, tsl, tok_out])
                    t = b.DV("tensor_add", fst[:], fst[:], xt[cg % 2][:],
                             waits=[t, txl])
                    tprev = t
                    tok_out = self.dma(
                        "gp", self.out_fm[c * 128:(c + 1) * 128,
                                          th * 512:(th + 1) * 512],
                        fst[:], waits=[t])
        fins = [(s, s.n) for s in self.pool["gp"]]

        def fin(e, _f=tuple(fins)):
            for ws, wn in _f:
                if wn > 0:
                    e.wait_ge(ws.h, wn)
        self.q["gp"].append(fin)
        ar.reset()



_BUILD_CACHE = {}


def _host_tables(core):
    t0 = core * TPC
    pos = np.arange(t0, t0 + TPC, dtype=np.float64)
    freqs = 1.0 / THETA ** (np.arange(0, 96, 2, dtype=np.float64) / 96.0)
    emb = pos[:, None] * freqs[None, :]
    cos = np.cos(emb).astype(np.float32)
    sin = np.sin(emb).astype(np.float32)
    ropec = np.zeros((128, NTT * 48), np.float32)
    ropes = np.zeros((128, NTT * 48), np.float32)
    for g in range(NTT):
        ropec[:, g * 48:(g + 1) * 48] = cos[g * 128:(g + 1) * 128]
        ropes[:, g * 48:(g + 1) * 48] = sin[g * 128:(g + 1) * 128]
    slopes = np.array(_slopes(HEADS), np.float64)
    arr = np.arange(BLK, dtype=np.float64) + 1.0
    qdec = np.exp(-slopes[:, None] * arr[None, :]).astype(np.float32)
    kdec = np.zeros((128, HEADS * 2), np.float32)
    for h in range(HEADS):
        kd = np.exp(-slopes[h] * (BLK - arr)).astype(np.float32)
        kdec[:, h * 2] = kd[0:128]
        kdec[:, h * 2 + 1] = kd[128:256]
    ddec = np.zeros((HEADS * 2 * 128, BLK), np.float32)
    idx = arr[:, None] - arr[None, :]
    for h in range(HEADS):
        m_ = np.where(idx >= 0, np.exp(-slopes[h] * idx), 0.0)
        mT = m_.T.astype(np.float32)
        ddec[(h * 2) * 128:(h * 2 + 1) * 128] = mT[0:128]
        ddec[(h * 2 + 1) * 128:(h * 2 + 2) * 128] = mT[128:256]
    coef = np.zeros((128, 7 * HEADS), np.float32)
    for r in range(NCORES - 1):
        for h in range(HEADS):
            if r < core:
                bd = math.exp(-slopes[h] * BLK)
                coef[:, r * HEADS + h] = bd ** (NBLK_C * (core - 1 - r))
    causal = np.zeros((128, 896), np.float32)
    for p in range(128):
        causal[p, p + 384:] = 1.0
    trib = np.triu(np.ones((128, 128), np.float32), 1)
    iota224 = np.broadcast_to(np.arange(SLOTS, dtype=np.float32), (128, SLOTS)).copy()
    tokid = (np.arange(NTT, dtype=np.float32)[None, :] * 128
             + np.arange(128, dtype=np.float32)[:, None]).astype(np.float32)
    sel64 = np.zeros((64, 8), np.float32)
    sel64[np.arange(64), np.arange(64) % 8] = 1.0
    selmap8 = np.zeros((8, 64), np.float32)
    selmap8[np.arange(64) % 8, np.arange(64)] = 1.0
    idxqkv = np.zeros((128, 48), np.int32)
    for sec in range(3):
        for lh in range(2):
            hme = 2 * core + lh
            for r in range(NCORES):
                idxqkv[:, (sec * 2 + lh) * 8 + r] = \
                    (r * 48 + sec * 16 + hme) * 128 + np.arange(128)
    idxs2 = np.zeros((128, NCH), np.int32)
    for hc in range(NCH):
        idxs2[:, hc] = (hc * 128 + np.arange(128)) * S + TPC * core
    fcon = np.zeros((128, 4), np.float32)
    fcon[:, 0] = 1e-5
    fcon[:, 1] = 1e-6
    fcon[:, 2] = QSCALE
    bf = ml_dtypes.bfloat16
    return dict(
        fcon_d=fcon,
        ropec=ropec, ropes=ropes, qdec_d=qdec, kdec_d=kdec, ddec_d=ddec,
        coef_d=coef, causal_d=causal.astype(bf), identb_d=np.eye(128, dtype=bf),
        identf_d=np.eye(128, dtype=np.float32), trib_d=trib.astype(bf),
        onesb_d=np.ones((128, 128), bf), onesf_d=np.ones((1, 128), np.float32),
        onescol_d=np.ones((128, 1), np.float32), iota224_d=iota224,
        tokid_d=tokid, sel64_d=sel64, selmap8_d=selmap8,
        idxqkv_d=idxqkv, idxs2_d=idxs2)


def kernel(**inputs):
    global LAST_DEBUG
    key = _DEBUG
    if key not in _BUILD_CACHE:
        bld = Builder(debug_point=key)
        _BUILD_CACHE[key] = (bld.build(), bld)
    nc, bld = _BUILD_CACHE[key]

    f32 = np.float32
    x = np.asarray(inputs["L0_attn_x"], f32).reshape(S, D)
    gate_w = np.asarray(inputs["L0_ffn_gate_w"], f32)
    gatew = np.zeros((128, NCH * 7), f32)
    for c in range(NCH):
        gatew[:, c * 7:(c + 1) * 7] = gate_w[c * 128:(c + 1) * 128]
    ebias = np.asarray(inputs["L0_ffn_expert_bias"], f32).reshape(7, 1)
    sgv = np.ascontiguousarray(
        np.asarray(inputs["L1_ffn_s_gate"], f32).reshape(L1HC, 128).T)
    suv = np.ascontiguousarray(
        np.asarray(inputs["L1_ffn_s_up"], f32).reshape(L1HC, 128).T)
    sdv = np.ascontiguousarray(
        np.asarray(inputs["L1_ffn_s_down"], f32).reshape(NCH, 128).T)
    nwv = np.ascontiguousarray(
        np.asarray(inputs["L1_attn_norm_w"], f32).reshape(NCH, 128).T)
    shared = dict(
        wq_w=np.asarray(inputs["L0_attn_Wq"], f32),
        wkv_w=np.asarray(inputs["L0_attn_Wkv_down"], f32),
        wku_w=np.asarray(inputs["L0_attn_Wk_up"], f32),
        wvu_w=np.asarray(inputs["L0_attn_Wv_up"], f32),
        sw1_w=np.asarray(inputs["L0_ffn_shared_w1"], f32).reshape(D, 2048),
        sw2_w=np.asarray(inputs["L0_ffn_shared_w2"], f32).reshape(1024, D),
        rw1_w=np.asarray(inputs["L0_ffn_routed_w1"], f32).reshape(E_R * D, 2048),
        rw2_w=np.asarray(inputs["L0_ffn_routed_w2"], f32).reshape(E_R * 1024, D),
        wqkv_w=np.asarray(inputs["L1_attn_Wqkv"], f32),
        wg_w=np.asarray(inputs["L1_attn_Wg"], f32),
        wout_w=np.asarray(inputs["L1_attn_Wout"], f32),
        wgT_i=np.ascontiguousarray(np.asarray(inputs["L1_ffn_wq_gate"], np.int32).T),
        wuT_i=np.ascontiguousarray(np.asarray(inputs["L1_ffn_wq_up"], np.int32).T),
        wdT_i=np.ascontiguousarray(np.asarray(inputs["L1_ffn_wq_down"], np.int32).T),
        gatew=gatew, ebias=ebias, sgv=sgv, suv=suv, sdv=sdv, nwv=nwv)

    in_maps = []
    for r in range(NCORES):
        m = dict(shared)
        m["x_t"] = np.ascontiguousarray(x[r * TPC:(r + 1) * TPC].T)
        m.update(_host_tables(r))
        in_maps.append(m)
    res = run_bass_kernel_spmd(nc, in_maps, core_ids=list(range(NCORES)))
    out = np.zeros((S, D), f32)
    dbg = {}
    for r in range(NCORES):
        out[r * TPC:(r + 1) * TPC] = res.results[r]["out_fm"].T
        if _DEBUG is not None:
            dbg[r] = res.results[r]["dbg"].T
    LAST_DEBUG = dbg
    return out.reshape(1, S, D)
